# version 01
# baseline (speedup 1.0000x reference)
import numpy as np
import ml_dtypes
from contextlib import ExitStack

import concourse.bass as bass
import concourse.mybir as mybir
from concourse.bass_utils import run_bass_kernel_spmd
from concourse.bass import IndirectOffsetOnAxis

F32 = mybir.dt.float32
BF16 = mybir.dt.bfloat16
I8 = mybir.dt.int8
I32 = mybir.dt.int32
AF = mybir.ActivationFunctionType
ALU = mybir.AluOpType
AX = mybir.AxisListType

NCORE = 8
TOK = 1024
D = 2048
NCH = 16
LN_EPS = 1e-5

_uid = [0]
RACE_DETECT = [False]
L1DBG = [None]


def uname(s):
    _uid[0] += 1
    return f"{s}_{_uid[0]}"


class Sem:
    def __init__(self, stack, nc, name):
        self.h = stack.enter_context(nc.semaphore(name))
        self.n = 0

    def reserve(self, k):
        base = self.n
        self.n += k
        return base


class Ctx:
    def __init__(self, nc, blk, stack):
        self.nc = nc
        self.blk = blk
        self.stack = stack
        self.sW = Sem(stack, nc, "sW")
        self.sG = Sem(stack, nc, "sG")
        self.sO = Sem(stack, nc, "sO")
        self.sP = Sem(stack, nc, "sP")
        self.sV = Sem(stack, nc, "sV")
        self.sS = Sem(stack, nc, "sS")
        self.sQ = Sem(stack, nc, "sQ")
        self.sC = Sem(stack, nc, "sC")
        self.sB = Sem(stack, nc, "sB")
        self.ios = {}
        self.bufs = {}

    def barrier(self):
        base = self.sB.reserve(5)
        target = base + 5
        waits = [(s.h, s.n) for s in (self.sW, self.sG, self.sO, self.sC)
                 if s.n > 0]

        def f(e):
            e.drain().then_inc(self.sB.h, 1)
            e.wait_ge(self.sB.h, target)
            for h, n in waits:
                e.wait_ge(h, n)

        for sec in (self.blk.sync, self.blk.tensor, self.blk.vector,
                    self.blk.scalar, self.blk.gpsimd):
            sec(f)



def _bf16(x):
    return np.ascontiguousarray(x).astype(ml_dtypes.bfloat16)


def make_const_inputs(core):
    pos = core % 4
    ii = np.arange(128)
    c = {}
    c["ident_bf16"] = _bf16(np.eye(128, dtype=np.float32))
    c["ident_f32"] = np.eye(128, dtype=np.float32)
    c["ones_col_bf16"] = _bf16(np.ones((128, 1), np.float32))
    c["ones_row_f32"] = np.ones((1, 128), np.float32)
    c["ones_row_bf16"] = _bf16(np.ones((1, 128), np.float32))
    oh = np.zeros((8, 8 * 128), np.float32)
    for e in range(8):
        oh[e, e * 128:(e + 1) * 128] = 1.0
    c["onehot8"] = oh
    c["eps_ln"] = np.full((128, 1), 1e-5, np.float32)
    c["eps_a"] = np.full((128, 1), 1e-6, np.float32)
    c["mask_su"] = (ii[None, :] > ii[:, None]).astype(np.float32)
    c["mask_u"] = (ii[None, :] >= ii[:, None]).astype(np.float32)
    c["mask_su_neg"] = -c["mask_su"]
    c["mask_sl_neg"] = -(ii[None, :] < ii[:, None]).astype(np.float32)
    c["one_col"] = np.ones((128, 1), np.float32)
    b = core // 4
    rloc = core % 4
    BLK = 6208 * 2048
    hks = [4 * rloc + i for i in range(4)]
    hvs = [8 * rloc + i for i in range(8)]

    def mixrow(m, ch, p):
        return (b * 4 + m) * 12416 + ch * 128 + p

    g1 = np.zeros((128, 96), np.int32)
    col = 0
    pieces = []
    for hk in hks:
        pieces.append(4 * hk + 0)
    for hk in hks:
        pieces.append(4 * hk + 1)
    for hv in hvs:
        pieces.append(4 * (hv // 2) + 2 + (hv % 2))
    for hv in hvs:
        pieces.append(64 + 2 * (hv // 2) + (hv % 2))
    for ch in pieces:
        for m in range(4):
            g1[:, col] = mixrow(m, ch, ii)
            col += 1
    c["g1idx"] = g1
    g2 = np.zeros((4, 16), np.int32)
    for ui, hv in enumerate(hvs):
        for m in range(4):
            g2[m, ui] = (b * 4 + m) * 64 + hv
            g2[m, 8 + ui] = (b * 4 + m) * 64 + 32 + hv
    c["g2idx"] = g2
    g3 = np.zeros((128, 16), np.int32)
    col = 0
    for hk in hks:
        g3[:, col] = hk * 512 + ii
        col += 1
    for hk in hks:
        g3[:, col] = hk * 512 + 128 + ii
        col += 1
    for hv in hvs:
        g3[:, col] = (hv // 2) * 512 + 256 + (hv % 2) * 128 + ii
        col += 1
    c["g3idx"] = g3
    g4 = np.zeros((128, 32), np.int32)
    for hv in range(32):
        rc = b * 4 + hv // 8
        u = hv % 8
        g4[:, hv] = rc * 4096 + u * 512 + ii * 4 + rloc
    c["g4idx"] = g4
    big = np.int32(1 << 28)
    hid = np.full((128, 1), big, np.int32)
    if pos > 0:
        hid[:, 0] = (core - 1) * 128 + ii
    c["halo_idx"] = hid
    for s in range(3):
        j = pos - 3 + s
        cid = np.full((128, 1), big, np.int32)
        if 0 <= j < pos:
            cid[:, 0] = ((core // 4) * 4 + j) * 128 + ii
        c[f"comp_idx{s}"] = cid
    return c


CONST_SPECS = {
    "ident_bf16": ([128, 128], BF16),
    "ident_f32": ([128, 128], F32),
    "ones_col_bf16": ([128, 1], BF16),
    "ones_row_f32": ([1, 128], F32),
    "ones_row_bf16": ([1, 128], BF16),
    "onehot8": ([8, 8 * 128], F32),
    "eps_ln": ([128, 1], F32),
    "eps_a": ([128, 1], F32),
    "mask_su": ([128, 128], F32),
    "mask_u": ([128, 128], F32),
    "mask_su_neg": ([128, 128], F32),
    "mask_sl_neg": ([128, 128], F32),
    "one_col": ([128, 1], F32),
    "g1idx": ([128, 96], I32),
    "g2idx": ([4, 16], I32),
    "g3idx": ([128, 16], I32),
    "g4idx": ([128, 32], I32),
    "halo_idx": ([128, 1], I32),
    "comp_idx0": ([128, 1], I32),
    "comp_idx1": ([128, 1], I32),
    "comp_idx2": ([128, 1], I32),
}

WEIGHT_SPECS = {
    "L0_attn_Wq": ([2048, 2048], F32),
    "L0_attn_Wk": ([2048, 512], F32),
    "L0_attn_Wv": ([2048, 512], F32),
    "L0_attn_Wo": ([2048, 2048], F32),
    "L0_ffn_gate_w": ([8, 2048], F32),
    "L0_ffn_w1": ([8, 2048, 1408], F32),
    "L0_ffn_w3": ([8, 2048, 1408], F32),
    "L0_ffn_w2": ([8, 1408, 2048], F32),
    "L0_ffn_sw1": ([2048, 2816], F32),
    "L0_ffn_sw3": ([2048, 2816], F32),
    "L0_ffn_sw2": ([2816, 2048], F32),
    "L1_attn_W_qkvz": ([2048, 12288], F32),
    "L1_attn_W_ba": ([2048, 64], F32),
    "L1_attn_conv_w": ([8192, 4], F32),
    "L1_attn_dt_bias": ([32, 1], F32),
    "L1_attn_A_log": ([32, 1], F32),
    "L1_attn_norm_w": ([1, 128], F32),
    "L1_attn_W_out": ([4096, 2048], F32),
    "L1_ffn_w_gate": ([2048, 8192], I8),
    "L1_ffn_w_up": ([2048, 8192], I8),
    "L1_ffn_w_down": ([8192, 2048], I8),
    "L1_ffn_s_gate": ([128, 64], F32),
    "L1_ffn_s_up": ([128, 64], F32),
    "L1_ffn_s_down": ([128, 16], F32),
}

KC_NAMES = ["ident_bf16", "ident_f32", "ones_col_bf16", "ones_row_f32",
            "ones_row_bf16", "onehot8", "eps_ln", "eps_a", "mask_su",
            "mask_u", "mask_su_neg", "mask_sl_neg", "one_col",
            "g1idx", "g2idx", "g3idx", "g4idx", "halo_idx",
            "comp_idx0", "comp_idx1", "comp_idx2",
            "L1_attn_dt_bias", "L1_attn_A_log", "L1_attn_norm_w",
            "L1_ffn_s_gate", "L1_ffn_s_up", "L1_ffn_s_down"]



def emit_prologue(c):
    nc, blk = c.nc, c.blk
    io, kc = c.ios, c.bufs["KC"]
    res = c.bufs["RES"]
    gwt = c.bufs["gate_wT"]

    wb = c.sW.reserve(16 * (1 + len(KC_NAMES)))
    gb = c.sG.reserve(16)

    @blk.sync
    def _(s):
        s.dma_start(out=res[:].rearrange("p (c t) -> p c t", t=TOK),
                    in_=io["x0T"].rearrange("(c p) t -> p c t", p=128)
                    ).then_inc(c.sW.h, 16)
        for nm in KC_NAMES:
            s.dma_start(out=kc[nm][:], in_=io[nm]).then_inc(c.sW.h, 16)

    pb = c.sP.reserve(NCH)
    vb = c.sV.reserve(NCH)
    with (
        nc.sbuf_tensor(uname("gate_w_bf"), [8, 2048], BF16) as gw,
        nc.psum_tensor(uname("pgt"), [128, 16], BF16) as pgt,
    ):
        @blk.gpsimd
        def _(g):
            g.dma_start(out=gw[:], in_=io["L0_ffn_gate_w"]).then_inc(c.sG.h, 16)

        @blk.tensor
        def _(t):
            t.wait_ge(c.sG.h, gb + 16)
            t.wait_ge(c.sW.h, wb + 16 * (1 + len(KC_NAMES)))
            for ch in range(NCH):
                if ch >= 2:
                    t.wait_ge(c.sV.h, vb + ch - 1)
                sl = (ch % 2) * 8
                t.transpose(pgt[:, sl:sl + 8], gw[:, ch * 128:(ch + 1) * 128],
                            kc["ident_bf16"][0:8, 0:8]).then_inc(c.sP.h, 1)

        @blk.vector
        def _(v):
            for ch in range(NCH):
                v.wait_ge(c.sP.h, pb + ch + 1)
                sl = (ch % 2) * 8
                v.tensor_copy(gwt[:, ch * 8:(ch + 1) * 8],
                              pgt[:, sl:sl + 8]).then_inc(c.sV.h, 1)
    c.barrier()


def emit_ln(c, tok0, T, ln, ln_off=0, ln_stride=None):
    assert T <= 512
    if ln_stride is None:
        ln_stride = T
    nc, blk = c.nc, c.blk
    kc, res = c.bufs["KC"], c.bufs["RES"]

    def L(ch):
        return ln[:, ln_off + ch * ln_stride: ln_off + ch * ln_stride + T]

    with (
        nc.sbuf_tensor(uname("ln_sq"), [128, 2 * T], BF16) as sq,
        nc.sbuf_tensor(uname("ln_mb"), [128, T], F32) as mb,
        nc.sbuf_tensor(uname("ln_rb"), [128, T], F32) as rb,
        nc.sbuf_tensor(uname("ln_tp"), [128, T], F32) as tmp,
        nc.psum_tensor(uname("ln_red"), [1, 2 * T], F32) as ps_red,
        nc.psum_tensor(uname("ln_bc"), [128, 2 * T], F32) as ps_bc,
    ):
        qb = c.sQ.reserve(NCH)
        sb = c.sS.reserve(NCH)
        pb = c.sP.reserve(NCH)

        @blk.gpsimd
        def _(g):
            for ch in range(NCH):
                g.tensor_copy(L(ch),
                              res[:, ch * TOK + tok0:ch * TOK + tok0 + T]
                              ).then_inc(c.sQ.h, 1)

        @blk.scalar
        def _(s):
            for ch in range(NCH):
                s.wait_ge(c.sQ.h, qb + ch + 1)
                if ch >= 2:
                    s.wait_ge(c.sP.h, pb + ch - 1)
                s.activation(sq[:, (ch % 2) * T:(ch % 2) * T + T], L(ch),
                             AF.Square).then_inc(c.sS.h, 1)

        @blk.tensor
        def _(t):
            t.wait_ge(c.sQ.h, qb + NCH)
            for ch in range(NCH):
                t.matmul(ps_red[0:1, 0:T], kc["ones_col_bf16"][:], L(ch),
                         start=(ch == 0), stop=(ch == NCH - 1))
            for ch in range(NCH):
                t.wait_ge(c.sS.h, sb + ch + 1)
                t.matmul(ps_red[0:1, T:2 * T], kc["ones_col_bf16"][:],
                         sq[:, (ch % 2) * T:(ch % 2) * T + T],
                         start=(ch == 0), stop=(ch == NCH - 1)
                         ).then_inc(c.sP.h, 1)

        vb = c.sV.reserve(3)
        sb2 = c.sS.reserve(1)

        @blk.vector
        def _(v):
            v.wait_ge(c.sP.h, pb + NCH)
            v.tensor_scalar(mb[0:1, 0:T], ps_red[0:1, 0:T], -1.0 / D, None,
                            ALU.mult)
            v.tensor_scalar(rb[0:1, 0:T], ps_red[0:1, 0:T], 1.0 / D,
                            None, ALU.mult).then_inc(c.sV.h, 1)
            v.wait_ge(c.sV.h, vb + 1)
            v.tensor_mul(rb[0:1, 0:T], rb[0:1, 0:T],
                         rb[0:1, 0:T]).then_inc(c.sV.h, 1)
            v.wait_ge(c.sV.h, vb + 2)
            v.scalar_tensor_tensor(rb[0:1, 0:T], ps_red[0:1, T:2 * T],
                                   1.0 / D, rb[0:1, 0:T],
                                   ALU.mult, ALU.subtract).then_inc(c.sV.h, 1)

        @blk.scalar
        def _(s):
            s.wait_ge(c.sV.h, vb + 3)
            s.activation(rb[0:1, 0:T], rb[0:1, 0:T],
                         AF.Ln, bias=kc["eps_ln"][0:1, :])
            s.activation(rb[0:1, 0:T], rb[0:1, 0:T],
                         AF.Exp, scale=-0.5).then_inc(c.sS.h, 1)

        pb2 = c.sP.reserve(1)

        @blk.tensor
        def _(t):
            t.wait_ge(c.sS.h, sb2 + 1)
            t.matmul(ps_bc[:, 0:T], kc["ones_row_f32"][:], mb[0:1, 0:T],
                     start=True, stop=True)
            t.matmul(ps_bc[:, T:2 * T], kc["ones_row_f32"][:],
                     rb[0:1, 0:T],
                     start=True, stop=True).then_inc(c.sP.h, 1)

        vb2 = c.sV.reserve(2 + 2 * NCH)

        @blk.vector
        def _(v):
            v.wait_ge(c.sP.h, pb2 + 1)
            v.tensor_copy(mb[:], ps_bc[:, 0:T]).then_inc(c.sV.h, 1)
            v.tensor_copy(rb[:], ps_bc[:, T:2 * T]).then_inc(c.sV.h, 1)
            for ch in range(NCH):
                v.tensor_add(tmp[:], res[:, ch * TOK + tok0:ch * TOK + tok0 + T],
                             mb[:]).then_inc(c.sV.h, 1)
                v.tensor_mul(L(ch), tmp[:], rb[:]).then_inc(c.sV.h, 1)
        c.barrier()


def _w_ap_moe(io, e, which):
    if e < 8:
        a = io[f"L0_ffn_{which}"][e]
    else:
        j = e - 8
        if which == "w1":
            a = io["L0_ffn_sw1"][:, j * 1408:(j + 1) * 1408]
        elif which == "w3":
            a = io["L0_ffn_sw3"][:, j * 1408:(j + 1) * 1408]
        else:
            a = io["L0_ffn_sw2"][j * 1408:(j + 1) * 1408, :]
    return a.rearrange("(c p) i -> p c i", p=128)


def emit_l0ffn(c, ln, wslot, hs, experts=tuple(range(10)), halves=(0, 1), dump_gate=False):
    nc, blk = c.nc, c.blk
    io, kc, res = c.ios, c.bufs["KC"], c.bufs["RES"]
    gwt = c.bufs["gate_wT"]
    T = 512

    with (
        nc.sbuf_tensor(uname("gt_lg"), [128, 32], F32) as LG,
        nc.sbuf_tensor(uname("gt_m"), [128, 16], F32) as M,
        nc.sbuf_tensor(uname("gt_w"), [128, 32], F32) as W,
        nc.sbuf_tensor(uname("gt_wt"), [8, 512], F32) as Wt,
        nc.sbuf_tensor(uname("gt_tp"), [128, 32], F32) as GTMP,
    ):
        for half in halves:
            tok0 = half * T
            emit_ln(c, tok0, T, ln)

            with (
                nc.psum_tensor(uname("pf_lg"), [128, 32], F32) as ps_lg,
                nc.psum_tensor(uname("pf_wt"), [8, 256], F32) as ps_wt,
            ):
                pb = c.sP.reserve(1)

                @blk.tensor
                def _(t):
                    for tt in range(4):
                        for ch in range(NCH):
                            inst = t.matmul(
                                ps_lg[:, tt * 8:(tt + 1) * 8],
                                ln[:, ch * T + tt * 128:ch * T + tt * 128 + 128],
                                gwt[:, ch * 8:(ch + 1) * 8],
                                start=(ch == 0), stop=(ch == NCH - 1))
                    inst.then_inc(c.sP.h, 1)

                DBG_BAR = False
                vb = c.sV.reserve(1)
                sb = c.sS.reserve(1)

                @blk.vector
                def _(v):
                    v.wait_ge(c.sP.h, pb + 1)
                    v.tensor_copy(LG[:], ps_lg[:])
                    for tt in range(4):
                        v.reduce_max(M[:, tt:tt + 1],
                                     LG[:, tt * 8:(tt + 1) * 8], axis=AX.X)
                    for tt in range(4):
                        v.tensor_scalar(M[:, 4 + tt:5 + tt], M[:, tt:tt + 1],
                                        -1.0, None, ALU.mult)
                    for tt in range(4):
                        v.tensor_scalar(GTMP[:, tt * 8:(tt + 1) * 8],
                                        LG[:, tt * 8:(tt + 1) * 8],
                                        M[:, tt:tt + 1], None, ALU.is_ge)
                    for tt in range(4):
                        v.scalar_tensor_tensor(GTMP[:, tt * 8:(tt + 1) * 8],
                                               GTMP[:, tt * 8:(tt + 1) * 8],
                                               -30000.0,
                                               LG[:, tt * 8:(tt + 1) * 8],
                                               ALU.mult, ALU.add)
                    for tt in range(4):
                        inst = v.reduce_max(M[:, 8 + tt:9 + tt],
                                            GTMP[:, tt * 8:(tt + 1) * 8],
                                            axis=AX.X)
                    inst.then_inc(c.sV.h, 1)
                if DBG_BAR:
                    c.barrier()

                @blk.scalar
                def _(s):
                    s.wait_ge(c.sV.h, vb + 1)
                    for tt in range(4):
                        inst = s.activation(W[:, tt * 8:(tt + 1) * 8],
                                            LG[:, tt * 8:(tt + 1) * 8], AF.Exp,
                                            bias=M[:, 4 + tt:5 + tt])
                    inst.then_inc(c.sS.h, 1)
                if DBG_BAR:
                    c.barrier()

                vb2 = c.sV.reserve(1)

                @blk.vector
                def _(v):
                    v.wait_ge(c.sS.h, sb + 1)
                    for tt in range(4):
                        v.tensor_scalar(GTMP[:, tt * 8:(tt + 1) * 8],
                                        LG[:, tt * 8:(tt + 1) * 8],
                                        M[:, 8 + tt:9 + tt], None, ALU.is_ge)
                    for tt in range(4):
                        v.tensor_mul(W[:, tt * 8:(tt + 1) * 8],
                                     W[:, tt * 8:(tt + 1) * 8],
                                     GTMP[:, tt * 8:(tt + 1) * 8])
                    for tt in range(4):
                        v.reduce_sum(M[:, 12 + tt:13 + tt],
                                     W[:, tt * 8:(tt + 1) * 8], axis=AX.X)
                    for tt in range(4):
                        v.reciprocal(M[:, 12 + tt:13 + tt],
                                     M[:, 12 + tt:13 + tt])
                    for tt in range(4):
                        inst = v.tensor_scalar(W[:, tt * 8:(tt + 1) * 8],
                                               W[:, tt * 8:(tt + 1) * 8],
                                               M[:, 12 + tt:13 + tt], 0.5,
                                               ALU.mult, ALU.mult)
                    inst.then_inc(c.sV.h, 1)

                if DBG_BAR:
                    c.barrier()
                pb2 = c.sP.reserve(4)
                vb3 = c.sV.reserve(4)

                @blk.tensor
                def _(t):
                    t.wait_ge(c.sV.h, vb2 + 1)
                    for tt in range(4):
                        if tt >= 2:
                            t.wait_ge(c.sV.h, vb3 + tt - 1)
                        t.transpose(
                            ps_wt[:, (tt % 2) * 128:(tt % 2) * 128 + 128],
                            W[:, tt * 8:(tt + 1) * 8],
                            kc["ident_f32"][:]).then_inc(c.sP.h, 1)

                @blk.vector
                def _(v):
                    for tt in range(4):
                        v.wait_ge(c.sP.h, pb2 + tt + 1)
                        v.tensor_copy(
                            Wt[:, tt * 128:(tt + 1) * 128],
                            ps_wt[:, (tt % 2) * 128:(tt % 2) * 128 + 128]
                            ).then_inc(c.sV.h, 1)
                c.barrier()
                if dump_gate:
                    res_ = c.bufs["RES"]
                    vbD = c.sV.reserve(3)

                    @blk.vector
                    def _(v):
                        v.tensor_copy(res_[:, 0:32], W[:]).then_inc(c.sV.h, 1)
                        v.tensor_copy(res_[0:8, 40:552], Wt[:]).then_inc(c.sV.h, 1)
                        v.tensor_copy(res_[:, 600:632], LG[:]).then_inc(c.sV.h, 1)
                    c.barrier()

            with (
                nc.psum_tensor(uname("pf_wb"), [128, 512], F32) as ps_wb,
                nc.psum_tensor(uname("pf_h0"), [128, 512], F32) as ph0,
                nc.psum_tensor(uname("pf_h1"), [128, 512], F32) as ph1,
                nc.psum_tensor(uname("pf_y0"), [128, 512], F32) as py0,
                nc.psum_tensor(uname("pf_y1"), [128, 512], F32) as py1,
            ):
                gbase = c.sG.reserve(16 * 3 * len(experts))
                pe_incs = []
                for e in experts:
                    pe_incs += [11, 11, 16]
                pe_base = c.sP.reserve(sum(pe_incs))
                sc_base = c.sS.reserve(len(experts) * 11)
                v_counts = []
                for e in experts:
                    v_counts += [0, 22 if e < 8 else 11, 16]
                v_base = c.sV.reserve(sum(v_counts))

                def pe_target(p):
                    return pe_base + sum(pe_incs[:p + 1])

                def v_target(p):
                    return v_base + sum(v_counts[:p + 1])

                @blk.gpsimd
                def _(g):
                    for ei, e in enumerate(experts):
                        for wi, which in enumerate(("w1", "w3", "w2")):
                            p = ei * 3 + wi
                            if p >= 2:
                                g.wait_ge(c.sP.h, pe_target(p - 2))
                            g.dma_start(out=wslot[p % 2][:],
                                        in_=_w_ap_moe(io, e, which)
                                        ).then_inc(c.sG.h, 16)

                @blk.tensor
                def _(t):
                    for ei, e in enumerate(experts):
                        p = ei * 3
                        t.wait_ge(c.sG.h, gbase + 16 * (p + 1))
                        if e < 8:
                            t.matmul(ps_wb[:],
                                     kc["onehot8"][:, e * 128:(e + 1) * 128],
                                     Wt[:], start=True, stop=True)
                        slot = wslot[p % 2]
                        for it in range(11):
                            ph = (ph0, ph1)[it % 2]
                            if it >= 2:
                                t.wait_ge(c.sS.h, sc_base + 11 * ei + it - 1)
                            for ch2 in range(NCH):
                                inst = t.matmul(
                                    ph[:],
                                    slot[:, ch2 * 1408 + it * 128:
                                         ch2 * 1408 + it * 128 + 128],
                                    ln[:, ch2 * T:(ch2 + 1) * T],
                                    start=(ch2 == 0), stop=(ch2 == NCH - 1))
                            inst.then_inc(c.sP.h, 1)
                        p = ei * 3 + 1
                        t.wait_ge(c.sG.h, gbase + 16 * (p + 1))
                        t.wait_ge(c.sS.h, sc_base + 11 * (ei + 1))
                        slot = wslot[p % 2]
                        per = 2 if e < 8 else 1
                        for it in range(11):
                            ph = (ph0, ph1)[it % 2]
                            if it >= 2:
                                t.wait_ge(c.sV.h,
                                          v_target(p - 1) + per * (it - 1))
                            for ch2 in range(NCH):
                                inst = t.matmul(
                                    ph[:],
                                    slot[:, ch2 * 1408 + it * 128:
                                         ch2 * 1408 + it * 128 + 128],
                                    ln[:, ch2 * T:(ch2 + 1) * T],
                                    start=(ch2 == 0), stop=(ch2 == NCH - 1))
                            inst.then_inc(c.sP.h, 1)
                        p = ei * 3 + 2
                        t.wait_ge(c.sG.h, gbase + 16 * (p + 1))
                        t.wait_ge(c.sV.h, v_target(p - 1))
                        slot = wslot[p % 2]
                        for ot in range(NCH):
                            py = (py0, py1)[ot % 2]
                            if ot >= 2:
                                t.wait_ge(c.sV.h, v_target(p - 1) + ot - 1)
                            for it in range(11):
                                inst = t.matmul(
                                    py[:],
                                    slot[:, it * 2048 + ot * 128:
                                         it * 2048 + ot * 128 + 128],
                                    hs[:, it * T:(it + 1) * T],
                                    start=(it == 0), stop=(it == 10))
                            inst.then_inc(c.sP.h, 1)

                @blk.scalar
                def _(s):
                    for ei, e in enumerate(experts):
                        base_p = pe_base + sum(pe_incs[:ei * 3])
                        for it in range(11):
                            s.wait_ge(c.sP.h, base_p + it + 1)
                            ph = (ph0, ph1)[it % 2]
                            s.activation(hs[:, it * T:(it + 1) * T], ph[:],
                                         AF.Silu).then_inc(c.sS.h, 1)

                @blk.vector
                def _(v):
                    for ei, e in enumerate(experts):
                        p = ei * 3 + 1
                        base_p = pe_base + sum(pe_incs[:p])
                        for it in range(11):
                            v.wait_ge(c.sP.h, base_p + it + 1)
                            ph = (ph0, ph1)[it % 2]
                            v.tensor_mul(hs[:, it * T:(it + 1) * T],
                                         hs[:, it * T:(it + 1) * T],
                                         ph[:]).then_inc(c.sV.h, 1)
                            if e < 8:
                                v.tensor_mul(hs[:, it * T:(it + 1) * T],
                                             hs[:, it * T:(it + 1) * T],
                                             ps_wb[:]).then_inc(c.sV.h, 1)
                        p = ei * 3 + 2
                        base_p = pe_base + sum(pe_incs[:p])
                        osc = 1.0 if e < 8 else 0.5
                        for ot in range(NCH):
                            v.wait_ge(c.sP.h, base_p + ot + 1)
                            py = (py0, py1)[ot % 2]
                            v.scalar_tensor_tensor(
                                res[:, ot * TOK + tok0:ot * TOK + tok0 + T],
                                py[:], osc,
                                res[:, ot * TOK + tok0:ot * TOK + tok0 + T],
                                ALU.mult, ALU.add).then_inc(c.sV.h, 1)
                c.barrier()


def emit_l0attn(c):
    nc, blk = c.nc, c.blk
    io, kc, res = c.ios, c.bufs["KC"], c.bufs["RES"]
    qscr = io["qscr"]
    kv_in = io["kv_in"]
    kv_out = io["kv_out"]
    GROUPS = [[0, 1, 2, 3], [4, 5, 6, 7]]

    with (
        nc.sbuf_tensor(uname("at_ln"), [128, NCH * TOK], BF16) as ln,
        nc.sbuf_tensor(uname("at_kf"), [128, 4 * TOK], BF16) as kf,
        nc.sbuf_tensor(uname("at_vt"), [128, 8 * 512], BF16) as vt,
        nc.sbuf_tensor(uname("at_wkv"), [128, 2 * NCH * 512], BF16) as wkv,
        nc.sbuf_tensor(uname("at_wq0"), [128, NCH * 128], BF16) as wq0,
        nc.sbuf_tensor(uname("at_wq1"), [128, NCH * 128], BF16) as wq1,
        nc.sbuf_tensor(uname("at_qt"), [128, 2 * TOK], BF16) as qt,
        nc.sbuf_tensor(uname("at_sq"), [128, TOK], BF16) as sqb,
        nc.sbuf_tensor(uname("at_row"), [1, TOK], F32) as row,
    ):
        emit_ln(c, 0, 512, ln, ln_off=0, ln_stride=TOK)
        emit_ln(c, 512, 512, ln, ln_off=512, ln_stride=TOK)
        import contextlib as _ctl
        _stkA = _ctl.ExitStack()
        pq0 = _stkA.enter_context(nc.psum_tensor(uname("at_pq0"), [128, TOK], F32))
        pq1 = _stkA.enter_context(nc.psum_tensor(uname("at_pq1"), [128, TOK], F32))
        psm = _stkA.enter_context(nc.psum_tensor(uname("at_psm"), [1, TOK], F32))
        pbc = _stkA.enter_context(nc.psum_tensor(uname("at_pbc"), [128, TOK], F32))

        wq_slots = (wq0, wq1)
        gb = c.sG.reserve(16 * (2 + 16 + 2))
        cb = c.sC.reserve(1)

        pe_b = c.sP.reserve(4 * 3 + 8 + 16 * 3)
        v_b = c.sV.reserve(4 * 3 + 8 + 16 * 3)
        s_b = c.sS.reserve(2 * (4 + 16))
        w_b = c.sW.reserve(16 * 16)

        def peK(i, j):
            return pe_b + 3 * i + j

        def peV(i):
            return pe_b + 12 + i + 1

        def peQ(i, j):
            return pe_b + 20 + 3 * i + j

        def vK(i, j):
            return v_b + 3 * i + j

        def vV(i):
            return v_b + 12 + i + 1

        def vQ(i, j):
            return v_b + 20 + 3 * i + j

        def sK(i, j):
            return s_b + 2 * i + j

        def sQ_(i, j):
            return s_b + 8 + 2 * i + j

        @blk.gpsimd
        def _(g):
            g.dma_start(
                out=wkv[:, 0:NCH * 512].rearrange("p (a b) -> p a b", b=512),
                in_=io["L0_attn_Wk"].rearrange("(c p) o -> p c o", p=128)
            ).then_inc(c.sG.h, 16)
            g.dma_start(
                out=wkv[:, NCH * 512:].rearrange("p (a b) -> p a b", b=512),
                in_=io["L0_attn_Wv"].rearrange("(c p) o -> p c o", p=128)
            ).then_inc(c.sG.h, 16)
            g.wait_ge(c.sV.h, vV(7))
            g.dma_start(
                out=kv_in[0:512, :].rearrange("(i p) t -> p i t", p=128),
                in_=kf[:].rearrange("p (i t) -> p i t", t=TOK)
            ).then_inc(c.sG.h, 16)
            g.dma_start(
                out=kv_in[512:1024, :].rearrange("(a r) (h f) -> (r h) a f",
                                                 a=8, h=2),
                in_=vt[:].rearrange("p (a f) -> p a f", f=512)
            ).then_inc(c.sG.h, 16)
            g.wait_ge(c.sG.h, gb + 16 * 4)
            g.collective_compute(
                "AllGather", ALU.bypass, replica_groups=GROUPS,
                ins=[kv_in], outs=[kv_out]).then_inc(c.sC.h, 1)
            for h in range(16):
                if h >= 2:
                    g.wait_ge(c.sP.h, peQ(h - 2, 1))
                g.dma_start(
                    out=wq_slots[h % 2][:].rearrange("p (a b) -> p a b", b=128),
                    in_=io["L0_attn_Wq"][:, h * 128:(h + 1) * 128]
                    .rearrange("(c p) o -> p c o", p=128)).then_inc(c.sG.h, 16)

        @blk.tensor
        def _(t):
            t.wait_ge(c.sG.h, gb + 16 * 2)
            for i in range(4):
                if i >= 2:
                    t.wait_ge(c.sV.h, vK(i - 2, 1))
                for ch in range(NCH):
                    for hh in range(2):
                        inst = t.matmul(
                            (pq0, pq1)[i % 2][:, hh * 512:(hh + 1) * 512],
                            wkv[:, ch * 512 + i * 128:ch * 512 + i * 128 + 128],
                            ln[:, ch * TOK + hh * 512:ch * TOK + (hh + 1) * 512],
                            start=(ch == 0), stop=(ch == NCH - 1))
                inst.then_inc(c.sP.h, 1)
                t.wait_ge(c.sS.h, sK(i, 1))
                if i >= 1:
                    t.wait_ge(c.sV.h, vK(i - 1, 2))
                for hh in range(2):
                    inst = t.matmul(psm[0:1, hh * 512:(hh + 1) * 512],
                                    kc["ones_col_bf16"][:],
                                    sqb[:, hh * 512:(hh + 1) * 512],
                                    start=True, stop=True)
                inst.then_inc(c.sP.h, 1)
                t.wait_ge(c.sS.h, sK(i, 2))
                if i >= 1:
                    t.wait_ge(c.sV.h, vK(i - 1, 3))
                for hh in range(2):
                    inst = t.matmul(pbc[:, hh * 512:(hh + 1) * 512],
                                    kc["ones_row_f32"][:],
                                    row[0:1, hh * 512:(hh + 1) * 512],
                                    start=True, stop=True)
                inst.then_inc(c.sP.h, 1)
            for a in range(8):
                pv = (pq0, pq1)[a % 2]
                if a >= 2:
                    t.wait_ge(c.sV.h, vV(a - 2))
                else:
                    t.wait_ge(c.sV.h, vK(2 + a, 1))
                for ch in range(NCH):
                    inst = t.matmul(
                        pv[:, 0:512],
                        ln[:, ch * TOK + a * 128:ch * TOK + a * 128 + 128],
                        wkv[:, NCH * 512 + ch * 512:NCH * 512 + (ch + 1) * 512],
                        start=(ch == 0), stop=(ch == NCH - 1))
                inst.then_inc(c.sP.h, 1)
            for h in range(16):
                pq = (pq0, pq1)[h % 2]
                t.wait_ge(c.sG.h, gb + 16 * (4 + h + 1))
                if h >= 2:
                    t.wait_ge(c.sV.h, vQ(h - 2, 1))
                else:
                    t.wait_ge(c.sV.h, vV(6 + h))
                for ch in range(NCH):
                    for hh in range(2):
                        inst = t.matmul(
                            pq[:, hh * 512:(hh + 1) * 512],
                            wq_slots[h % 2][:, ch * 128:(ch + 1) * 128],
                            ln[:, ch * TOK + hh * 512:ch * TOK + (hh + 1) * 512],
                            start=(ch == 0), stop=(ch == NCH - 1))
                inst.then_inc(c.sP.h, 1)
                t.wait_ge(c.sS.h, sQ_(h, 1))
                if h >= 1:
                    t.wait_ge(c.sV.h, vQ(h - 1, 2))
                for hh in range(2):
                    inst = t.matmul(psm[0:1, hh * 512:(hh + 1) * 512],
                                    kc["ones_col_bf16"][:],
                                    sqb[:, hh * 512:(hh + 1) * 512],
                                    start=True, stop=True)
                inst.then_inc(c.sP.h, 1)
                t.wait_ge(c.sS.h, sQ_(h, 2))
                if h >= 1:
                    t.wait_ge(c.sV.h, vQ(h - 1, 3))
                for hh in range(2):
                    inst = t.matmul(pbc[:, hh * 512:(hh + 1) * 512],
                                    kc["ones_row_f32"][:],
                                    row[0:1, hh * 512:(hh + 1) * 512],
                                    start=True, stop=True)
                inst.then_inc(c.sP.h, 1)

        @blk.vector
        def _(v):
            for i in range(4):
                v.wait_ge(c.sP.h, peK(i, 1))
                v.tensor_copy(kf[:, i * TOK:(i + 1) * TOK],
                              (pq0, pq1)[i % 2][:]).then_inc(c.sV.h, 1)
                v.wait_ge(c.sP.h, peK(i, 2))
                if i >= 1:
                    v.wait_ge(c.sS.h, sK(i - 1, 2))
                v.tensor_copy(row[:], psm[:]).then_inc(c.sV.h, 1)
                v.wait_ge(c.sP.h, peK(i, 3))
                v.tensor_mul(kf[:, i * TOK:(i + 1) * TOK],
                             kf[:, i * TOK:(i + 1) * TOK],
                             pbc[:]).then_inc(c.sV.h, 1)
            for a in range(8):
                v.wait_ge(c.sP.h, peV(a))
                v.tensor_copy(vt[:, a * 512:(a + 1) * 512],
                              (pq0, pq1)[a % 2][:, 0:512]).then_inc(c.sV.h, 1)
            for h in range(16):
                pq = (pq0, pq1)[h % 2]
                v.wait_ge(c.sP.h, peQ(h, 1))
                if h >= 2:
                    v.wait_ge(c.sW.h, w_b + 16 * (h - 1))
                v.tensor_copy(qt[:, (h % 2) * TOK:(h % 2 + 1) * TOK],
                              pq[:]).then_inc(c.sV.h, 1)
                v.wait_ge(c.sP.h, peQ(h, 2))
                if h >= 1:
                    v.wait_ge(c.sS.h, sQ_(h - 1, 2))
                v.tensor_copy(row[:], psm[:]).then_inc(c.sV.h, 1)
                v.wait_ge(c.sP.h, peQ(h, 3))
                v.tensor_mul(qt[:, (h % 2) * TOK:(h % 2 + 1) * TOK],
                             qt[:, (h % 2) * TOK:(h % 2 + 1) * TOK],
                             pbc[:]).then_inc(c.sV.h, 1)

        @blk.scalar
        def _(s):
            for i in range(4):
                s.wait_ge(c.sV.h, vK(i, 1))
                if i >= 1:
                    s.wait_ge(c.sP.h, peK(i - 1, 2))
                s.activation(sqb[:], kf[:, i * TOK:(i + 1) * TOK],
                             AF.Square).then_inc(c.sS.h, 1)
                s.wait_ge(c.sV.h, vK(i, 2))
                s.activation(row[:], row[:], AF.Ln, scale=1.0 / 128.0,
                             bias=kc["eps_a"][0:1, :])
                s.activation(row[:], row[:], AF.Exp,
                             scale=-0.5).then_inc(c.sS.h, 1)
            for h in range(16):
                s.wait_ge(c.sV.h, vQ(h, 1))
                if h >= 1:
                    s.wait_ge(c.sP.h, peQ(h - 1, 2))
                elif True:
                    s.wait_ge(c.sP.h, peK(3, 2))
                s.activation(sqb[:], qt[:, (h % 2) * TOK:(h % 2 + 1) * TOK],
                             AF.Square).then_inc(c.sS.h, 1)
                s.wait_ge(c.sV.h, vQ(h, 2))
                s.activation(row[:], row[:], AF.Ln, scale=1.0 / 128.0,
                             bias=kc["eps_a"][0:1, :])
                s.activation(row[:], row[:], AF.Exp,
                             scale=-0.5).then_inc(c.sS.h, 1)

        @blk.sync
        def _(s):
            for h in range(16):
                s.wait_ge(c.sV.h, vQ(h, 3))
                s.dma_start(out=qscr[h],
                            in_=qt[:, (h % 2) * TOK:(h % 2 + 1) * TOK]
                            ).then_inc(c.sW.h, 16)
        c.barrier()
        _stkA.close()

    with (
        nc.sbuf_tensor(uname("at_KF"), [128, 4 * 4096], BF16) as KF,
        nc.sbuf_tensor(uname("at_VF"), [128, 32 * 512], BF16) as VF,
        nc.sbuf_tensor(uname("at_qg"), [128, 8 * TOK], BF16) as qg,
        nc.sbuf_tensor(uname("at_OT"), [128, 8 * TOK], BF16) as OT,
        nc.sbuf_tensor(uname("at_PT"), [128, 2 * TOK], BF16) as PT,
        nc.sbuf_tensor(uname("at_bcs"), [128, TOK], F32) as bcs,
        nc.sbuf_tensor(uname("at_row2"), [1, TOK], F32) as row2,
        nc.sbuf_tensor(uname("at_wo0"), [128, 8 * 128], BF16) as wo0,
        nc.sbuf_tensor(uname("at_wo1"), [128, 8 * 128], BF16) as wo1,
        nc.psum_tensor(uname("at_st0"), [128, TOK], F32) as st0,
        nc.psum_tensor(uname("at_st1"), [128, TOK], F32) as st1,
        nc.psum_tensor(uname("at_po"), [128, TOK], F32) as po,
        nc.psum_tensor(uname("at_ps2"), [1, TOK], F32) as ps2,
    ):
        wo_slots = (wo0, wo1)
        kv_b = c.sW.reserve(16 * 8)

        @blk.sync
        def _(s):
            s.wait_ge(c.sC.h, c.sC.n)
            for m in range(4):
                s.dma_start(
                    out=KF[:].rearrange("p (i t) -> p i t", t=4096)
                    [:, :, m * TOK:(m + 1) * TOK],
                    in_=kv_out[m * 1024:m * 1024 + 512, :]
                    .rearrange("(i p) t -> p i t", p=128)).then_inc(c.sW.h, 16)
                s.dma_start(
                    out=VF[:].rearrange("p (a f) -> p a f", f=512)
                    [:, m * 8:(m + 1) * 8, :],
                    in_=kv_out[m * 1024 + 512:(m + 1) * 1024, :]
                    .rearrange("(a r) (h f) -> (r h) a f", a=8, h=2)
                    ).then_inc(c.sW.h, 16)

        for grp in range(2):
            qb2 = c.sW.reserve(16)
            pe_now = c.sP.n

            @blk.sync
            def _(s):
                s.wait_ge(c.sP.h, pe_now)
                s.dma_start(out=qg[:].rearrange("p (h t) -> p h t", t=TOK),
                            in_=qscr[grp * 8:(grp + 1) * 8]
                            .rearrange("h p t -> p h t")
                            ).then_inc(c.sW.h, 16)

            pe_b2 = c.sP.reserve(8 * 65)
            s_b2 = c.sS.reserve(8 * 32)
            v_b2 = c.sV.reserve(8 * 4)

            def peH(hl, j):
                return pe_b2 + 65 * hl + j

            def sH(hl, j):
                return s_b2 + 32 * hl + j

            def vH(hl, j):
                return v_b2 + 4 * hl + j

            @blk.tensor
            def _(t):
                t.wait_ge(c.sW.h, kv_b + 16 * 8)
                t.wait_ge(c.sW.h, qb2 + 16)
                t.wait_ge(c.sV.h, v_b2)
                for hl in range(8):
                    H = grp * 8 + hl
                    kvh = H // 4
                    KFh = KF[:].rearrange("p (i t) -> p i t", t=4096)
                    if hl >= 1:
                        t.wait_ge(c.sV.h, vH(hl - 1, 4))
                    for kt in range(32):
                        st = (st0, st1)[kt % 2]
                        if kt >= 2:
                            t.wait_ge(c.sS.h, sH(hl, kt - 1))
                        elif hl >= 1:
                            t.wait_ge(c.sS.h, sH(hl - 1, 31 + kt))
                        for hh in range(2):
                            inst = t.matmul(
                                st[:, hh * 512:(hh + 1) * 512],
                                KFh[:, kvh, kt * 128:(kt + 1) * 128],
                                qg[:, hl * TOK + hh * 512:hl * TOK + (hh + 1) * 512],
                                start=True, stop=True)
                        inst.then_inc(c.sP.h, 1)
                        t.wait_ge(c.sS.h, sH(hl, kt + 1))
                        VFt = VF[:].rearrange("p (a f) -> p a f", f=512)
                        for hh in range(2):
                            t.matmul(
                                po[:, hh * 512:(hh + 1) * 512],
                                VFt[:, kt, kvh * 128:(kvh + 1) * 128],
                                PT[:, (kt % 2) * TOK + hh * 512:
                                   (kt % 2) * TOK + (hh + 1) * 512],
                                start=(kt == 0), stop=(kt == 31))
                        for hh in range(2):
                            inst = t.matmul(
                                ps2[0:1, hh * 512:(hh + 1) * 512],
                                kc["ones_col_bf16"][:],
                                PT[:, (kt % 2) * TOK + hh * 512:
                                   (kt % 2) * TOK + (hh + 1) * 512],
                                start=(kt == 0), stop=(kt == 31))
                        inst.then_inc(c.sP.h, 1)
                    t.wait_ge(c.sV.h, vH(hl, 2))
                    for hh in range(2):
                        inst = t.matmul(st0[:, hh * 512:(hh + 1) * 512],
                                        kc["ones_row_f32"][:],
                                        row2[0:1, hh * 512:(hh + 1) * 512],
                                        start=True, stop=True)
                    inst.then_inc(c.sP.h, 1)

            @blk.scalar
            def _(s):
                for hl in range(8):
                    for kt in range(32):
                        s.wait_ge(c.sP.h, peH(hl, 2 * kt + 1))
                        st = (st0, st1)[kt % 2]
                        s.activation(PT[:, (kt % 2) * TOK:(kt % 2 + 1) * TOK],
                                     st[:], AF.Exp,
                                     scale=float(1.0 / np.sqrt(128.0))
                                     ).then_inc(c.sS.h, 1)

            @blk.vector
            def _(v):
                for hl in range(8):
                    v.wait_ge(c.sP.h, peH(hl, 64))
                    v.tensor_copy(row2[:], ps2[:]).then_inc(c.sV.h, 1)
                    v.reciprocal(row2[:], row2[:]).then_inc(c.sV.h, 1)
                    v.wait_ge(c.sP.h, peH(hl, 65))
                    v.tensor_copy(bcs[:], st0[:]).then_inc(c.sV.h, 1)
                    v.tensor_mul(OT[:, hl * TOK:(hl + 1) * TOK], po[:],
                                 bcs[:]).then_inc(c.sV.h, 1)

            ob = c.sG.reserve(16 * 16)
            pe_b3 = c.sP.reserve(16)
            v_b3 = c.sV.reserve(16)

            @blk.gpsimd
            def _(g):
                for ot in range(16):
                    if ot >= 2:
                        g.wait_ge(c.sP.h, pe_b3 + ot - 1)
                    g.dma_start(
                        out=wo_slots[ot % 2][:].rearrange("p (a b) -> p a b",
                                                          b=128),
                        in_=io["L0_attn_Wo"][grp * 1024:(grp + 1) * 1024,
                                             ot * 128:(ot + 1) * 128]
                        .rearrange("(c p) o -> p c o", p=128)
                        ).then_inc(c.sG.h, 16)

            @blk.tensor
            def _(t):
                t.wait_ge(c.sV.h, v_b2 + 32)
                for ot in range(16):
                    st = (st0, st1)[ot % 2]
                    t.wait_ge(c.sG.h, ob + 16 * (ot + 1))
                    if ot >= 2:
                        t.wait_ge(c.sV.h, v_b3 + ot - 1)
                    for ch in range(8):
                        for hh in range(2):
                            inst = t.matmul(
                                st[:, hh * 512:(hh + 1) * 512],
                                wo_slots[ot % 2][:, ch * 128:(ch + 1) * 128],
                                OT[:, ch * TOK + hh * 512:ch * TOK + (hh + 1) * 512],
                                start=(ch == 0), stop=(ch == 7))
                    inst.then_inc(c.sP.h, 1)

            @blk.vector
            def _(v):
                for ot in range(16):
                    v.wait_ge(c.sP.h, pe_b3 + ot + 1)
                    st = (st0, st1)[ot % 2]
                    v.tensor_add(res[:, ot * TOK:(ot + 1) * TOK],
                                 res[:, ot * TOK:(ot + 1) * TOK],
                                 st[:]).then_inc(c.sV.h, 1)
            c.barrier()
    c.barrier()


def emit_l1ffn(c, ln, wg_slot, wu_slot, wd_slot, G, FUSE, sdh):
    nc, blk = c.nc, c.blk
    io, kc, res = c.ios, c.bufs["KC"], c.bufs["RES"]
    NS = 16

    emit_ln(c, 0, 512, ln, ln_off=0, ln_stride=1024)
    emit_ln(c, 512, 512, ln, ln_off=512, ln_stride=1024)

    ps_ctx = (
        nc.psum_tensor(uname("f1_pg0"), [128, 1024], F32),
        nc.psum_tensor(uname("f1_pg1"), [128, 1024], F32),
        nc.psum_tensor(uname("f1_pu0"), [128, 1024], F32),
        nc.psum_tensor(uname("f1_pu1"), [128, 1024], F32),
    )
    c.ps_pg0 = c.stack2_enter(ps_ctx[0]) if False else None
    import contextlib
    _stk = contextlib.ExitStack()
    c.ps_pg0 = _stk.enter_context(ps_ctx[0])
    c.ps_pg1 = _stk.enter_context(ps_ctx[1])
    c.ps_pu0 = _stk.enter_context(ps_ctx[2])
    c.ps_pu1 = _stk.enter_context(ps_ctx[3])

    vb0 = c.sV.reserve(1)

    @blk.vector
    def _(v):
        v.tensor_scalar(sdh[:], kc["L1_ffn_s_down"][:], 0.5, None,
                        ALU.mult).then_inc(c.sV.h, 1)

    gbase = c.sG.reserve(16 * 3 * NS)
    pe_base = c.sP.reserve(24 * NS)
    sc_base = c.sS.reserve(4 * NS)
    v_base = c.sV.reserve(20 * NS)

    def pe_slice(j):
        return pe_base + 24 * (j + 1)

    def pe_gateup(j, k):
        return pe_base + 24 * j + k

    def pe_down(j, ot):
        return pe_base + 24 * j + 8 + ot + 1

    def v_slice(j):
        return v_base + 20 * (j + 1)

    def v_fuse(j, k):
        return v_base + 20 * j + k

    def v_down(j, ot):
        return v_base + 20 * j + 4 + ot + 1

    @blk.gpsimd
    def _(g):
        for j in range(NS):
            if j >= 2:
                g.wait_ge(c.sP.h, pe_slice(j - 2))
            g.dma_start(
                out=wg_slot[j % 2][:].rearrange("p (a b) -> p a b", b=512),
                in_=io["L1_ffn_w_gate"][:, j * 512:(j + 1) * 512]
                .rearrange("(c p) o -> p c o", p=128)).then_inc(c.sG.h, 16)
            g.dma_start(
                out=wu_slot[j % 2][:].rearrange("p (a b) -> p a b", b=512),
                in_=io["L1_ffn_w_up"][:, j * 512:(j + 1) * 512]
                .rearrange("(c p) o -> p c o", p=128)).then_inc(c.sG.h, 16)
            if j >= 1:
                g.wait_ge(c.sP.h, pe_slice(j - 1))
            g.dma_start(
                out=wd_slot[:].rearrange("p (a b) -> p a b", b=2048),
                in_=io["L1_ffn_w_down"][j * 512:(j + 1) * 512, :]
                .rearrange("(c p) o -> p c o", p=128)).then_inc(c.sG.h, 16)

    @blk.tensor
    def _(t):
        t.wait_ge(c.sV.h, vb0 + 1)
        for j in range(NS):
            wg, wu = wg_slot[j % 2], wu_slot[j % 2]
            t.wait_ge(c.sG.h, gbase + 16 * (3 * j + 2))
            if j >= 1:
                t.wait_ge(c.sV.h, v_slice(j - 1))
            for pt in range(4):
                pg = (c.ps_pg0, c.ps_pg1)[pt % 2]
                pu = (c.ps_pu0, c.ps_pu1)[pt % 2]
                if pt >= 2:
                    t.wait_ge(c.sS.h, sc_base + 4 * j + pt - 1)
                    t.wait_ge(c.sV.h, v_fuse(j, pt - 1))
                for ch in range(NCH):
                    for hh in range(2):
                        inst = t.matmul(
                            pg[:, hh * 512:(hh + 1) * 512],
                            wg[:, ch * 512 + pt * 128:ch * 512 + pt * 128 + 128],
                            ln[:, ch * TOK + hh * 512:ch * TOK + (hh + 1) * 512],
                            start=(ch == 0), stop=(ch == NCH - 1))
                inst.then_inc(c.sP.h, 1)
                for ch in range(NCH):
                    for hh in range(2):
                        inst = t.matmul(
                            pu[:, hh * 512:(hh + 1) * 512],
                            wu[:, ch * 512 + pt * 128:ch * 512 + pt * 128 + 128],
                            ln[:, ch * TOK + hh * 512:ch * TOK + (hh + 1) * 512],
                            start=(ch == 0), stop=(ch == NCH - 1))
                inst.then_inc(c.sP.h, 1)
            t.wait_ge(c.sG.h, gbase + 16 * (3 * j + 3))
            t.wait_ge(c.sV.h, v_fuse(j, 4))
            for ot in range(NCH):
                pd = (c.ps_pg0, c.ps_pg1)[ot % 2]
                if ot >= 2:
                    t.wait_ge(c.sV.h, v_down(j, ot - 2))
                for ic in range(4):
                    for hh in range(2):
                        inst = t.matmul(
                            pd[:, hh * 512:(hh + 1) * 512],
                            wd_slot[:, ic * 2048 + ot * 128:
                                    ic * 2048 + ot * 128 + 128],
                            FUSE[:, ic * TOK + hh * 512:
                                 ic * TOK + (hh + 1) * 512],
                            start=(ic == 0), stop=(ic == 3))
                inst.then_inc(c.sP.h, 1)

    @blk.scalar
    def _(s):
        for j in range(NS):
            for pt in range(4):
                s.wait_ge(c.sP.h, pe_gateup(j, 2 * pt + 1))
                k = 4 * j + pt
                if k >= 2:
                    jj, pp = divmod(k - 2, 4)
                    s.wait_ge(c.sV.h, v_fuse(jj, pp + 1))
                pg = (c.ps_pg0, c.ps_pg1)[pt % 2]
                s.activation(G[:, (pt % 2) * TOK:(pt % 2 + 1) * TOK], pg[:],
                             AF.Silu,
                             scale=kc["L1_ffn_s_gate"][:, k:k + 1]
                             ).then_inc(c.sS.h, 1)

    @blk.vector
    def _(v):
        for j in range(NS):
            for pt in range(4):
                v.wait_ge(c.sP.h, pe_gateup(j, 2 * pt + 2))
                v.wait_ge(c.sS.h, sc_base + 4 * j + pt + 1)
                pu = (c.ps_pu0, c.ps_pu1)[pt % 2]
                v.scalar_tensor_tensor(
                    FUSE[:, pt * TOK:(pt + 1) * TOK], pu[:],
                    kc["L1_ffn_s_up"][:, 4 * j + pt:4 * j + pt + 1],
                    G[:, (pt % 2) * TOK:(pt % 2 + 1) * TOK],
                    ALU.mult, ALU.mult).then_inc(c.sV.h, 1)
            for ot in range(NCH):
                v.wait_ge(c.sP.h, pe_down(j, ot))
                pd = (c.ps_pg0, c.ps_pg1)[ot % 2]
                v.scalar_tensor_tensor(
                    res[:, ot * TOK:(ot + 1) * TOK], pd[:],
                    sdh[:, ot:ot + 1],
                    res[:, ot * TOK:(ot + 1) * TOK],
                    ALU.mult, ALU.add).then_inc(c.sV.h, 1)
    c.barrier()
    _stk.close()




class Chain:

    def __init__(self, c):
        self.c = c
        self.tok = {"P": 0, "V": 0, "S": 0, "G": 0, "W": 0}

    def _waits(self, e, skip):
        c = self.c
        sems = {"P": c.sP, "V": c.sV, "S": c.sS, "G": c.sG, "W": c.sW}
        for k, v in self.tok.items():
            if k != skip and v:
                e.wait_ge(sems[k].h, v)

    def p(self, fn):
        c = self.c

        def sec(t):
            self._waits(t, "P")
            inst = fn(t)
            inst.then_inc(c.sP.h, 1)
        c.blk.tensor(sec)
        c.sP.n += 1
        self.tok["P"] = c.sP.n

    def v(self, fn):
        c = self.c

        def sec(vv):
            self._waits(vv, "V")
            inst = fn(vv)
            inst.then_inc(c.sV.h, 1)
        c.blk.vector(sec)
        c.sV.n += 1
        self.tok["V"] = c.sV.n

    def s(self, fn):
        c = self.c

        def sec(ss):
            self._waits(ss, "S")
            inst = fn(ss)
            inst.then_inc(c.sS.h, 1)
        c.blk.scalar(sec)
        c.sS.n += 1
        self.tok["S"] = c.sS.n

    def g(self, fn, ndma=1):
        c = self.c

        def sec(gg):
            self._waits(gg, "G")
            inst = fn(gg)
            inst.then_inc(c.sG.h, 16)
        c.blk.gpsimd(sec)
        c.sG.n += 16 * ndma
        self.tok["G"] = c.sG.n

    def w(self, fn, ndma=1):
        c = self.c

        def sec(ss):
            self._waits(ss, "W")
            inst = fn(ss)
            inst.then_inc(c.sW.h, 16)
        c.blk.sync(sec)
        c.sW.n += 16 * ndma
        self.tok["W"] = c.sW.n


def emit_l1attn(c):
    nc, blk = c.nc, c.blk
    io, kc, res = c.ios, c.bufs["KC"], c.bufs["RES"]
    SEQ = 4096
    QS = float(128.0 ** -0.5)

    with (
        nc.sbuf_tensor(uname("la_ln"), [128, NCH * TOK], BF16) as ln,
        nc.sbuf_tensor(uname("la_w0"), [128, NCH * 128], BF16) as w0,
        nc.sbuf_tensor(uname("la_w1"), [128, NCH * 128], BF16) as w1,
        nc.sbuf_tensor(uname("la_mx0"), [128, TOK], BF16) as mx0,
        nc.sbuf_tensor(uname("la_mx1"), [128, TOK], BF16) as mx1,
        nc.sbuf_tensor(uname("la_ba"), [128, TOK], F32) as bav,
    ):
        emit_ln(c, 0, 512, ln, ln_off=0, ln_stride=TOK)
        emit_ln(c, 512, 512, ln, ln_off=512, ln_stride=TOK)
        wsl, msl = (w0, w1), (mx0, mx1)

        with (
            nc.psum_tensor(uname("la_pm0"), [128, TOK], F32) as pm0,
            nc.psum_tensor(uname("la_pm1"), [128, TOK], F32) as pm1,
            nc.psum_tensor(uname("la_pba"), [64, TOK], F32) as pba,
        ):
            psl = (pm0, pm1)
            gb = c.sG.reserve(16 * 97)
            pe_b = c.sP.reserve(97)
            v_b = c.sV.reserve(97)
            sw_b = c.sW.reserve(16 * 97)
            wq = io["L1_attn_W_qkvz"]

            def chcols(ch):
                if ch < 64:
                    hk, s_ = ch // 4, ch % 4
                    return 768 * hk + (s_ * 128 if s_ < 2
                                       else 256 + (s_ - 2) * 128)
                zz = ch - 64
                return 768 * (zz // 2) + 512 + (zz % 2) * 128

            @blk.gpsimd
            def _(g):
                for ch in range(96):
                    if ch >= 2:
                        g.wait_ge(c.sP.h, pe_b + ch - 1)
                    g.dma_start(
                        out=wsl[ch % 2][:].rearrange("p (a b) -> p a b", b=128),
                        in_=wq[:, chcols(ch):chcols(ch) + 128]
                        .rearrange("(cc p) o -> p cc o", p=128)
                        ).then_inc(c.sG.h, 16)
                g.wait_ge(c.sP.h, pe_b + 96)
                g.dma_start(
                    out=wsl[0][:, 0:16 * 64].rearrange("p (a b) -> p a b",
                                                       b=64),
                    in_=io["L1_attn_W_ba"]
                    .rearrange("(cc p) o -> p cc o", p=128)
                    ).then_inc(c.sG.h, 16)

            @blk.tensor
            def _(t):
                for ch in range(96):
                    pm = psl[ch % 2]
                    t.wait_ge(c.sG.h, gb + 16 * (ch + 1))
                    if ch >= 2:
                        t.wait_ge(c.sV.h, v_b + ch - 1)
                    for cc in range(NCH):
                        for hh in range(2):
                            inst = t.matmul(
                                pm[:, hh * 512:(hh + 1) * 512],
                                wsl[ch % 2][:, cc * 128:(cc + 1) * 128],
                                ln[:, cc * TOK + hh * 512:
                                   cc * TOK + (hh + 1) * 512],
                                start=(cc == 0), stop=(cc == NCH - 1))
                    inst.then_inc(c.sP.h, 1)
                t.wait_ge(c.sG.h, gb + 16 * 97)
                t.wait_ge(c.sV.h, v_b + 95)
                for cc in range(NCH):
                    for hh in range(2):
                        inst = t.matmul(
                            pba[:, hh * 512:(hh + 1) * 512],
                            wsl[0][:, cc * 64:(cc + 1) * 64],
                            ln[:, cc * TOK + hh * 512:
                               cc * TOK + (hh + 1) * 512],
                            start=(cc == 0), stop=(cc == NCH - 1))
                inst.then_inc(c.sP.h, 1)

            @blk.vector
            def _(v):
                for ch in range(96):
                    v.wait_ge(c.sP.h, pe_b + ch + 1)
                    if ch >= 2:
                        v.wait_ge(c.sW.h, sw_b + 16 * (ch - 1))
                    v.tensor_copy(msl[ch % 2][:],
                                  psl[ch % 2][:]).then_inc(c.sV.h, 1)
                v.wait_ge(c.sP.h, pe_b + 97)
                v.tensor_copy(bav[0:64, :], pba[:]).then_inc(c.sV.h, 1)

            @blk.sync
            def _(s):
                for ch in range(96):
                    s.wait_ge(c.sV.h, v_b + ch + 1)
                    s.dma_start(
                        out=io["cc1_in"][ch * 64:(ch + 1) * 64, :]
                        .rearrange("r (a k) -> (r a) k", a=2),
                        in_=msl[ch % 2][:]).then_inc(c.sW.h, 16)
                s.wait_ge(c.sV.h, v_b + 97)
                s.dma_start(out=io["ba_scr"], in_=bav[0:64, :]
                            ).then_inc(c.sW.h, 16)
        c.barrier()

        with (
            nc.sbuf_tensor(uname("la_bg"), [64, TOK], F32) as bg,
            nc.sbuf_tensor(uname("la_na"), [32, 1], F32) as negA,
        ):
            ch0 = Chain(c)
            def split_ba(s):
                for h in range(16):
                    inst = s.dma_start(out=bg[2 * h:2 * h + 2, :],
                                       in_=io["ba_scr"][4 * h:4 * h + 2, :])
                    if h < 15:
                        inst.then_inc(c.sW.h, 16)
                return inst
            ch0.w(split_ba, ndma=16)

            def split_ba2(s):
                for h in range(16):
                    inst = s.dma_start(
                        out=bg[32 + 2 * h:32 + 2 * h + 2, :],
                        in_=io["ba_scr"][4 * h + 2:4 * h + 4, :])
                    if h < 15:
                        inst.then_inc(c.sW.h, 16)
                return inst
            ch0.w(split_ba2, ndma=16)
            ch0.s(lambda s: s.activation(negA[:], kc["L1_attn_A_log"][:],
                                         AF.Exp))
            ch0.v(lambda v: v.tensor_scalar(negA[:], negA[:], -1.0, None,
                                            ALU.mult))
            ch0.s(lambda s: s.activation(bg[0:32, :], bg[0:32, :], AF.Exp,
                                         scale=-1.0))
            ch0.v(lambda v: v.tensor_scalar(bg[0:32, :], bg[0:32, :], 1.0,
                                            None, ALU.add))
            ch0.v(lambda v: v.reciprocal(bg[0:32, :], bg[0:32, :]))
            ch0.v(lambda v: v.tensor_scalar(bg[32:64, :], bg[32:64, :],
                                            kc["L1_attn_dt_bias"][:], None,
                                            ALU.add))
            ch0.s(lambda s: s.activation(bav[32:64, :], bg[32:64, :],
                                         AF.Abs))
            ch0.s(lambda s: s.activation(bav[32:64, :], bav[32:64, :],
                                         AF.Exp, scale=-1.0))
            ch0.s(lambda s: s.activation(bav[32:64, :], bav[32:64, :], AF.Ln,
                                         bias=kc["one_col"][0:32, :]))
            ch0.v(lambda v: v.tensor_scalar_max(bg[32:64, :], bg[32:64, :],
                                                0.0))
            ch0.v(lambda v: v.tensor_add(bg[32:64, :], bg[32:64, :],
                                         bav[32:64, :]))
            ch0.v(lambda v: v.tensor_scalar(bg[32:64, :], bg[32:64, :],
                                            negA[0:32, :], None, ALU.mult))
            ch0.w(lambda s: s.dma_start(out=io["ccbg_in"], in_=bg[0:64, :]))
        c.barrier()

    ccb = c.sC.reserve(2)

    @blk.gpsimd
    def _(g):
        g.collective_compute(
            "AllGather", ALU.bypass, replica_groups=[list(range(8))],
            ins=[io["cc1_in"]], outs=[io["cc1_out"]]).then_inc(c.sC.h, 1)
        g.collective_compute(
            "AllGather", ALU.bypass, replica_groups=[list(range(8))],
            ins=[io["ccbg_in"]], outs=[io["ccbg_out"]]).then_inc(c.sC.h, 1)
    c.barrier()

    cc1f = io["cc1_out"].rearrange("r (a k) -> (r a) k", a=2)
    ccbgf = io["ccbg_out"]
    cwf = io["L1_attn_conv_w"]
    bascrf = io["ba_scr"].rearrange("r (k one) -> (r k) one", one=1)

    import contextlib as _ctl2
    _stk2 = _ctl2.ExitStack()
    QN = _stk2.enter_context(nc.sbuf_tensor(uname("lb_QN"), [128, SEQ], BF16))
    KN = _stk2.enter_context(nc.sbuf_tensor(uname("lb_KN"), [128, SEQ], BF16))
    V0 = _stk2.enter_context(nc.sbuf_tensor(uname("lb_V0"), [128, SEQ], BF16))
    V1 = _stk2.enter_context(nc.sbuf_tensor(uname("lb_V1"), [128, SEQ], BF16))
    Z0 = _stk2.enter_context(nc.sbuf_tensor(uname("lb_Z0"), [128, SEQ], BF16))
    Z1 = _stk2.enter_context(nc.sbuf_tensor(uname("lb_Z1"), [128, SEQ], BF16))
    OG0 = _stk2.enter_context(nc.sbuf_tensor(uname("lb_OG0"), [128, SEQ], BF16))
    OG1 = _stk2.enter_context(nc.sbuf_tensor(uname("lb_OG1"), [128, SEQ], BF16))
    PAD = _stk2.enter_context(nc.sbuf_tensor(uname("lb_PAD"), [128, SEQ + 3], BF16))
    C32 = _stk2.enter_context(nc.sbuf_tensor(uname("lb_C32"), [128, 512], F32))
    CB = _stk2.enter_context(nc.sbuf_tensor(uname("lb_CB"), [128, 512], F32))
    SQB = _stk2.enter_context(nc.sbuf_tensor(uname("lb_SQ"), [128, 1024], BF16))
    GROW = _stk2.enter_context(nc.sbuf_tensor(uname("lb_GROW"), [1, SEQ], F32))
    BROW = _stk2.enter_context(nc.sbuf_tensor(uname("lb_BROW"), [1, SEQ], F32))
    TROW = _stk2.enter_context(nc.sbuf_tensor(uname("lb_TROW"), [1, 1024], F32))
    CW = _stk2.enter_context(nc.sbuf_tensor(uname("lb_CW"), [128, 4], F32))
    T32 = _stk2.enter_context(nc.sbuf_tensor(uname("lb_T32"), [32, 128], F32))
    GT = _stk2.enter_context(nc.sbuf_tensor(uname("lb_GT"), [128, 2 * 6 * 32], F32))
    NWB = _stk2.enter_context(nc.sbuf_tensor(uname("lb_NWB"), [128, 128], F32))
    SC = _stk2.enter_context(nc.sbuf_tensor(uname("lb_SC"), [128, 2 * 2048], BF16))
    SBF = _stk2.enter_context(nc.sbuf_tensor(uname("lb_SBF"), [128, 2 * 128], BF16))
    S32 = _stk2.enter_context(nc.sbuf_tensor(uname("lb_S32"), [128, 2 * 128], F32))
    O32 = _stk2.enter_context(nc.sbuf_tensor(uname("lb_O32"), [128, 2 * 128], F32))
    ZSC = _stk2.enter_context(nc.sbuf_tensor(uname("lb_ZSC"), [128, 2 * 128], F32))
    R1 = _stk2.enter_context(nc.sbuf_tensor(uname("lb_R1"), [128, 4], F32))
    ROW4 = _stk2.enter_context(nc.sbuf_tensor(uname("lb_R4"), [4, 1024], F32))
    pT0 = _stk2.enter_context(nc.psum_tensor(uname("lb_pT0"), [128, 640], BF16))
    pT1 = _stk2.enter_context(nc.psum_tensor(uname("lb_pT1"), [128, 640], BF16))
    pA0 = _stk2.enter_context(nc.psum_tensor(uname("lb_pA0"), [128, 512], F32))
    pA1 = _stk2.enter_context(nc.psum_tensor(uname("lb_pA1"), [128, 512], F32))
    pB0 = _stk2.enter_context(nc.psum_tensor(uname("lb_pB0"), [128, 512], F32))
    pB1 = _stk2.enter_context(nc.psum_tensor(uname("lb_pB1"), [128, 512], F32))
    if True:

        pTs, pAs, pBs = (pT0, pT1), (pA0, pA1), (pB0, pB1)
        Vs, Zs, OGs = (V0, V1), (Z0, Z1), (OG0, OG1)
        GTW = 6 * 32

        def gtc(s_, which, n):
            o = s_ * GTW + which * 32 + n
            return GT[:, o:o + 1]

        def gta(s_, which):
            o = s_ * GTW + which * 32
            return GT[:, o:o + 32]

        def sc(s_, slot, w=128):
            o = s_ * 2048 + slot * 128
            return SC[:, o:o + w]

        gidx1, gidx2, gidx3 = kc["g1idx"], kc["g2idx"], kc["g3idx"]

        chN = Chain(c)
        chN.p(lambda t: t.matmul(pA0[:, 0:128], kc["ones_row_f32"][:],
                                 kc["L1_attn_norm_w"][:], start=True,
                                 stop=True))
        chN.v(lambda v: v.tensor_copy(NWB[:], pA0[:, 0:128]))
        chN.v(lambda v: v.memset(PAD[:, 0:3], 0.0))
        c.barrier()

        def load_piece(ch, colbase, cwcol, dst):
            if cwcol is not None:
                ch.g(lambda g: g.indirect_dma_start(
                    out=CW[:], out_offset=None, in_=cwf,
                    in_offset=IndirectOffsetOnAxis(
                        gidx3[:, cwcol:cwcol + 1], 0)))
            for m in range(4):
                ch.g(lambda g, m=m: g.indirect_dma_start(
                    out=dst[:, m * 1024:(m + 1) * 1024], out_offset=None,
                    in_=cc1f, in_offset=IndirectOffsetOnAxis(
                        gidx1[:, colbase + m:colbase + m + 1], 0)))

        def conv_silu(ch, dst):
            W_ = 512
            for i in range(SEQ // W_):
                o = 3 + i * W_

                def cv(v, o=o):
                    v.tensor_scalar(C32[:], PAD[:, o:o + W_], CW[:, 3:4],
                                    None, ALU.mult)
                    v.scalar_tensor_tensor(C32[:], PAD[:, o - 1:o - 1 + W_],
                                           CW[:, 2:3], C32[:], ALU.mult,
                                           ALU.add)
                    v.scalar_tensor_tensor(C32[:], PAD[:, o - 2:o - 2 + W_],
                                           CW[:, 1:2], C32[:], ALU.mult,
                                           ALU.add)
                    return v.scalar_tensor_tensor(
                        C32[:], PAD[:, o - 3:o - 3 + W_], CW[:, 0:1], C32[:],
                        ALU.mult, ALU.add)
                ch.v(cv)
                ch.s(lambda s: s.activation(CB[:], C32[:], AF.Exp,
                                            scale=-1.0))
                ch.v(lambda v: v.tensor_scalar(CB[:], CB[:], 1.0, None,
                                               ALU.add))
                ch.v(lambda v: v.reciprocal(CB[:], CB[:]))
                ch.v(lambda v, i=i: v.tensor_mul(
                    dst[:, i * W_:(i + 1) * W_], C32[:], CB[:]))

        def l2norm_piece(ch, dst, qscale):
            for q4 in range(4):
                ch.s(lambda s, q4=q4: s.activation(
                    SQB[:], dst[:, q4 * 1024:(q4 + 1) * 1024], AF.Square))

                def red(t, q4=q4):
                    t.matmul(pA0[0:1, 0:512], kc["ones_col_bf16"][:],
                             SQB[:, 0:512], start=True, stop=True)
                    return t.matmul(pB0[0:1, 0:512], kc["ones_col_bf16"][:],
                                    SQB[:, 512:1024], start=True, stop=True)
                ch.p(red)
                ch.v(lambda v: v.tensor_copy(TROW[0:1, 0:512],
                                             pA0[0:1, 0:512]))
                ch.v(lambda v: v.tensor_copy(TROW[0:1, 512:1024],
                                             pB0[0:1, 0:512]))
                ch.s(lambda s: s.activation(TROW[0:1, 0:1024],
                                            TROW[0:1, 0:1024], AF.Ln,
                                            bias=kc["eps_a"][0:1, :]))
                ch.s(lambda s: s.activation(TROW[0:1, 0:1024],
                                            TROW[0:1, 0:1024], AF.Exp,
                                            scale=-0.5))
                if qscale:
                    ch.v(lambda v: v.tensor_scalar(TROW[0:1, 0:1024],
                                                   TROW[0:1, 0:1024], QS,
                                                   None, ALU.mult))
                for j in range(2):
                    ch.p(lambda t, j=j: t.matmul(
                        pB0[:, 0:512], kc["ones_row_f32"][:],
                        TROW[0:1, j * 512:(j + 1) * 512], start=True,
                        stop=True))
                    ch.v(lambda v, q4=q4, j=j: v.tensor_mul(
                        dst[:, q4 * 1024 + j * 512:q4 * 1024 + (j + 1) * 512],
                        dst[:, q4 * 1024 + j * 512:q4 * 1024 + (j + 1) * 512],
                        pB0[:, 0:512]))

        def unit_rows(ch, s_, u):
            ch.g(lambda g: g.indirect_dma_start(
                out=ROW4[:], out_offset=None, in_=ccbgf,
                in_offset=IndirectOffsetOnAxis(gidx2[:, 8 + u:9 + u], 0)))
            ch.w(lambda s: s.dma_start(out=io["ba_scr"][0:4, :],
                                       in_=ROW4[:]))
            ch.w(lambda s: s.dma_start(
                out=GROW.rearrange("o (r t) -> o r t", r=4),
                in_=io["ba_scr"][0:4, :]
                .rearrange("(one r) t -> one r t", one=1)))
            ch.g(lambda g: g.indirect_dma_start(
                out=ROW4[:], out_offset=None, in_=ccbgf,
                in_offset=IndirectOffsetOnAxis(gidx2[:, u:u + 1], 0)))
            ch.w(lambda s: s.dma_start(out=io["ba_scr"][4:8, :],
                                       in_=ROW4[:]))
            ch.w(lambda s: s.dma_start(
                out=BROW.rearrange("o (r t) -> o r t", r=4),
                in_=io["ba_scr"][4:8, :]
                .rearrange("(one r) t -> one r t", one=1)))

            def scan(v):
                for n in range(32):
                    inst = v.tensor_tensor_scan(
                        GROW[0:1, n * 128:(n + 1) * 128],
                        GROW[0:1, n * 128:(n + 1) * 128],
                        GROW[0:1, n * 128:(n + 1) * 128],
                        0.0, ALU.add, ALU.bypass)
                return inst
            ch.v(scan)
            ch.w(lambda s: s.dma_start(
                out=io["ba_scr"][0:4, :].rearrange("(one r) t -> one r t",
                                                   one=1),
                in_=GROW.rearrange("o (r t) -> o r t", r=4)))
            ch.w(lambda s: s.dma_start(
                out=io["ba_scr"][4:8, :].rearrange("(one r) t -> one r t",
                                                   one=1),
                in_=BROW.rearrange("o (r t) -> o r t", r=4)))
            ch.w(lambda s: s.dma_start(
                out=T32[:], in_=io["ba_scr"][0:4, :]
                .rearrange("(a1) (a2 b) -> (a1 a2) b", a2=8)))
            ch.p(lambda t: t.transpose(pAs[s_][:, 0:32], T32[:],
                                       kc["ident_f32"][0:32, 0:32]))
            ch.v(lambda v, s_=s_: v.tensor_copy(gta(s_, 0), pAs[s_][:, 0:32]))
            ch.w(lambda s: s.dma_start(
                out=T32[:], in_=io["ba_scr"][4:8, :]
                .rearrange("(a1) (a2 b) -> (a1 a2) b", a2=8)))
            ch.p(lambda t, s_=s_: t.transpose(pAs[s_][:, 0:32], T32[:],
                                              kc["ident_f32"][0:32, 0:32]))
            ch.v(lambda v, s_=s_: v.tensor_copy(gta(s_, 1), pAs[s_][:, 0:32]))
            ch.s(lambda s, s_=s_: s.activation(gta(s_, 2), gta(s_, 0),
                                               AF.Exp))
            ch.v(lambda v, s_=s_: v.tensor_mul(gta(s_, 4), gta(s_, 1),
                                               gta(s_, 2)))
            gv = GROW.rearrange("o (n t) -> o n t", t=128)
            ch.s(lambda s: s.activation(TROW[0:1, 0:32],
                                        gv[0:1, :, 127:128]
                                        .rearrange("o n t -> o (n t)"),
                                        AF.Exp))
            ch.v(lambda v: v.tensor_copy(TROW[0:1, 32:64],
                                         gv[0:1, :, 127:128]
                                         .rearrange("o n t -> o (n t)")))

            def bc_gl(t, s_=s_):
                t.matmul(pAs[s_][:, 0:32], kc["ones_row_f32"][:],
                         TROW[0:1, 0:32], start=True, stop=True)
                return t.matmul(pAs[s_][:, 32:64], kc["ones_row_f32"][:],
                                TROW[0:1, 32:64], start=True, stop=True)
            ch.p(bc_gl)
            ch.v(lambda v, s_=s_: v.tensor_copy(gta(s_, 5), pAs[s_][:, 0:32]))
            ch.v(lambda v, s_=s_: v.tensor_copy(gta(s_, 3),
                                                pAs[s_][:, 32:64]))
            ch.v(lambda v, s_=s_: v.memset(
                S32[:, s_ * 128:(s_ + 1) * 128], 0.0))
            ch.v(lambda v, s_=s_: v.memset(
                SBF[:, s_ * 128:(s_ + 1) * 128], 0.0))

        def unit_chunk(ch, s_, n):
            pA, pB, pT = pAs[s_], pBs[s_], pTs[s_]
            Vv, Zz, OGg = Vs[s_], Zs[s_], OGs[s_]
            nsl = slice(n * 128, (n + 1) * 128)
            Sb = SBF[:, s_ * 128:(s_ + 1) * 128]
            Sf = S32[:, s_ * 128:(s_ + 1) * 128]
            Of = O32[:, s_ * 128:(s_ + 1) * 128]
            Zf = ZSC[:, s_ * 128:(s_ + 1) * 128]
            r1 = R1[:, s_:s_ + 1]

            kd = R1[:, 2 + s_:3 + s_]
            Dm = Zf
            D2m = Of
            Dx = ZSC[:, (1 - s_) * 128:(2 - s_) * 128]

            ch.s(lambda s: s.activation(TROW[0:1, 0:128], GROW[0:1, nsl],
                                        AF.Exp))
            ch.v(lambda v: v.tensor_mul(TROW[0:1, 256:384], BROW[0:1, nsl],
                                        TROW[0:1, 0:128]))

            def bc4(t):
                t.matmul(pA[:, 0:128], kc["ones_row_f32"][:],
                         TROW[0:1, 0:128], start=True, stop=True)
                t.matmul(pA[:, 128:256], kc["ones_row_f32"][:],
                         BROW[0:1, nsl], start=True, stop=True)
                t.matmul(pA[:, 256:384], kc["ones_row_f32"][:],
                         TROW[0:1, 256:384], start=True, stop=True)
                return t.matmul(pA[:, 384:512], kc["ones_row_f32"][:],
                                GROW[0:1, nsl], start=True, stop=True)
            ch.p(bc4)
            ch.v(lambda v: v.tensor_mul(sc(s_, 9), QN[:, nsl], pA[:, 0:128]))
            ch.v(lambda v: v.tensor_mul(sc(s_, 10), KN[:, nsl],
                                        pA[:, 128:256]))
            ch.v(lambda v: v.tensor_mul(sc(s_, 11), KN[:, nsl],
                                        pA[:, 256:384]))
            ch.v(lambda v: v.tensor_scalar(Dm, pA[:, 384:512],
                                           gtc(s_, 0, n), None,
                                           ALU.subtract))
            ch.v(lambda v: v.tensor_scalar_max(D2m, Dm, 0.0))
            ch.v(lambda v: v.tensor_scalar(Dm, Dm, 0.0, None, ALU.min))
            ch.s(lambda s: s.activation(Dm, Dm, AF.Exp))
            ch.s(lambda s: s.activation(D2m, D2m, AF.Exp, scale=-1.0))
            ch.p(lambda t: t.transpose(pT[:, 0:128], KN[:, nsl],
                                       kc["ident_bf16"][:]))
            ch.v(lambda v: v.tensor_copy(sc(s_, 14), pT[:, 0:128]))
            ch.p(lambda t: t.transpose(pT[:, 128:256], Vv[:, nsl],
                                       kc["ident_bf16"][:]))
            ch.v(lambda v: v.tensor_scalar(sc(s_, 0), pT[:, 128:256],
                                           gtc(s_, 1, n), None, ALU.mult))
            ch.v(lambda v: v.tensor_scalar(sc(s_, 1), sc(s_, 14),
                                           gtc(s_, 4, n), None, ALU.mult))
            ch.v(lambda v: v.tensor_sub(kd, gtc(s_, 3, n), gtc(s_, 0, n)))
            ch.s(lambda s: s.activation(kd, kd, AF.Exp))
            ch.v(lambda v: v.tensor_scalar(sc(s_, 5), sc(s_, 14), kd,
                                           None, ALU.mult))
            ch.p(lambda t: t.matmul(pA[:, 384:512], KN[:, nsl], sc(s_, 10),
                                    start=True, stop=True))
            ch.v(lambda v: v.tensor_mul(Dx, pA[:, 384:512], Dm))
            ch.v(lambda v: v.tensor_mul(sc(s_, 2), Dx,
                                        kc["mask_su_neg"][:]))
            ch.p(lambda t: t.matmul(pB[:, 0:128], sc(s_, 10), KN[:, nsl],
                                    start=True, stop=True))
            ch.v(lambda v: v.tensor_mul(Dx, pB[:, 0:128], D2m))
            ch.v(lambda v: v.tensor_mul(sc(s_, 3), Dx,
                                        kc["mask_sl_neg"][:]))
            ch.p(lambda t: t.matmul(pB[:, 128:256], KN[:, nsl], QN[:, nsl],
                                    start=True, stop=True))
            ch.v(lambda v: v.tensor_mul(Dx, pB[:, 128:256], Dm))
            ch.v(lambda v: v.tensor_mul(sc(s_, 4), Dx,
                                        kc["mask_u"][:]))
            ch.v(lambda v: v.tensor_copy(sc(s_, 6), sc(s_, 3)))
            ch.v(lambda v: v.tensor_copy(sc(s_, 7), sc(s_, 2)))
            ch.p(lambda t: t.matmul(pA[:, 0:256], sc(s_, 2), sc(s_, 0, 256),
                                    start=True, stop=True))
            ch.v(lambda v: v.tensor_add(sc(s_, 0, 256), sc(s_, 0, 256),
                                        pA[:, 0:256]))
            for lv in range(6):
                def sq(t, lv=lv):
                    inst = t.matmul(pB[:, 256:384], sc(s_, 7), sc(s_, 6),
                                    start=True, stop=True)
                    if lv < 5:
                        inst = t.matmul(pB[:, 384:512], sc(s_, 6), sc(s_, 7),
                                        start=True, stop=True)
                    return inst
                ch.p(sq)
                ch.v(lambda v: v.tensor_copy(sc(s_, 6), pB[:, 256:384]))
                if lv < 5:
                    ch.v(lambda v: v.tensor_copy(sc(s_, 7), pB[:, 384:512]))
                    ch.p(lambda t: t.matmul(pA[:, 0:256], sc(s_, 7),
                                            sc(s_, 0, 256), start=True,
                                            stop=True))
                else:
                    ch.p(lambda t: t.matmul(pA[:, 0:256], sc(s_, 6),
                                            sc(s_, 0, 256), start=True,
                                            stop=True))
                ch.v(lambda v: v.tensor_add(sc(s_, 0, 256), sc(s_, 0, 256),
                                            pA[:, 0:256]))
            ch.p(lambda t: t.transpose(pT[:, 256:384], sc(s_, 1),
                                       kc["ident_bf16"][:]))
            ch.v(lambda v: v.tensor_copy(sc(s_, 8), pT[:, 256:384]))
            ch.p(lambda t: t.matmul(pA[:, 384:512], sc(s_, 8), Sb,
                                    start=True, stop=True))
            ch.v(lambda v: v.tensor_sub(sc(s_, 15), sc(s_, 0),
                                        pA[:, 384:512]))
            def omm(t):
                t.matmul(pB[:, 256:384], sc(s_, 9), Sb, start=True,
                         stop=False)
                return t.matmul(pB[:, 256:384], sc(s_, 4), sc(s_, 15),
                                start=False, stop=True)
            ch.p(omm)
            ch.p(lambda t: t.matmul(pB[:, 384:512], sc(s_, 5), sc(s_, 15),
                                    start=True, stop=True))
            ch.v(lambda v: v.tensor_scalar(Sf, Sf, gtc(s_, 5, n), None,
                                           ALU.mult))
            ch.v(lambda v: v.tensor_add(Sf, Sf, pB[:, 384:512]))
            ch.v(lambda v: v.tensor_copy(Sb, Sf))
            ch.p(lambda t: t.transpose(pT[:, 384:512], Zz[:, nsl],
                                       kc["ident_bf16"][:]))
            ch.s(lambda s: s.activation(Zf, pT[:, 384:512], AF.Exp,
                                        scale=-1.0))
            ch.v(lambda v: v.tensor_scalar(Zf, Zf, 1.0, None, ALU.add))
            ch.v(lambda v: v.reciprocal(Zf, Zf))
            ch.v(lambda v: v.tensor_mul(sc(s_, 12), pT[:, 384:512], Zf))
            ch.v(lambda v: v.tensor_mul(Of, pB[:, 256:384], sc(s_, 12)))
            ch.s(lambda s: s.activation(Zf, Of, AF.Square))
            ch.v(lambda v: v.reduce_sum(r1, Zf, axis=AX.X))
            ch.s(lambda s: s.activation(r1, r1, AF.Ln, scale=1.0 / 128.0,
                                        bias=kc["eps_a"][:]))
            ch.s(lambda s: s.activation(r1, r1, AF.Exp, scale=-0.5))
            ch.v(lambda v: v.tensor_scalar(Of, Of, r1, None, ALU.mult))
            ch.v(lambda v: v.tensor_mul(sc(s_, 13), Of, NWB[:]))
            ch.p(lambda t: t.transpose(pT[:, 512:640], sc(s_, 13),
                                       kc["ident_bf16"][:]))
            ch.v(lambda v: v.tensor_copy(OGg[:, nsl], pT[:, 512:640]))

        def dbg_dump(tile, width=SEQ):
            c.barrier()
            chD = Chain(c)
            chD.v(lambda v: v.tensor_copy(res[:, 0:width], tile[:, 0:width]))
            c.barrier()
            _stk2.close()

        if L1DBG[0] in ("cc1", "cc1o"):
            chD = Chain(c)
            srcap = (io["cc1_in"] if L1DBG[0] == "cc1" else io["cc1_out"])
            chD.w(lambda s: s.dma_start(
                out=PAD[:, 0:1024],
                in_=srcap[0:64, :].rearrange("r (a k) -> (r a) k", a=2)))
            chD.v(lambda v: v.tensor_copy(res[:, 0:1024], PAD[:, 0:1024]))
            c.barrier()
            _stk2.close()
            return
        for hkl in range(4):
            chL = Chain(c)
            load_piece(chL, 4 * hkl, hkl, PAD[:, 3:3 + SEQ])
            if L1DBG[0] == "raw":
                dbg_dump(PAD[:, 3:3 + SEQ])
                return
            conv_silu(chL, QN)
            if L1DBG[0] == "conv":
                dbg_dump(QN)
                return
            l2norm_piece(chL, QN, qscale=True)
            if L1DBG[0] == "qn":
                dbg_dump(QN)
                return
            load_piece(chL, 16 + 4 * hkl, 4 + hkl, PAD[:, 3:3 + SEQ])
            conv_silu(chL, KN)
            l2norm_piece(chL, KN, qscale=False)
            for s_ in range(2):
                u = 2 * hkl + s_
                load_piece(chL, 32 + 4 * u, 8 + u, PAD[:, 3:3 + SEQ])
                conv_silu(chL, Vs[s_])
                load_piece(chL, 64 + 4 * u, None, Zs[s_][:, 0:SEQ])
            c.barrier()
            for s_ in range(2):
                chU = Chain(c)
                unit_rows(chU, s_, 2 * hkl + s_)
                c.barrier()
                if L1DBG[0] == "rows" and s_ == 0:
                    dbg_dump(GT, width=GTW)
                    return
                for n in range(32):
                    unit_chunk(chU, s_, n)
                    if n == 0 and s_ == 0 and L1DBG[0] == "ck0":
                        c.barrier()
                        chD = Chain(c)
                        chD.v(lambda v: v.tensor_copy(res[:, 0:2048],
                                                      SC[:, 0:2048]))
                        chD.v(lambda v: v.tensor_copy(res[:, 2048:2176],
                                                      S32[:, 0:128]))
                        chD.v(lambda v: v.tensor_copy(res[:, 2176:2304],
                                                      OG0[:, 0:128]))
                        c.barrier()
                        _stk2.close()
                        return
                c.barrier()
            wspill = c.sW.reserve(2 * 16)

            @blk.sync
            def _(s, hkl=hkl):
                for s_ in range(2):
                    u = 2 * hkl + s_
                    s.dma_start(
                        out=io["cc2_in"][u * 256:(u + 1) * 256, :]
                        .rearrange("r (a k) -> (r a) k", a=2),
                        in_=OGs[s_][:]).then_inc(c.sW.h, 16)
            c.barrier()

    _stk2.close()
    cc2b = c.sC.reserve(1)

    @blk.gpsimd
    def _(g):
        g.collective_compute(
            "AllGather", ALU.bypass, replica_groups=[list(range(8))],
            ins=[io["cc2_in"]], outs=[io["cc2_out"]]).then_inc(c.sC.h, 1)
    c.barrier()

    cc2f = io["cc2_out"].rearrange("r (a k) -> (r a) k", a=2)
    with (
        nc.sbuf_tensor(uname("lc_og"), [128, 8 * TOK], BF16) as OGC,
        nc.sbuf_tensor(uname("lc_wo"), [128, 8 * 2048], BF16) as WOF,
        nc.psum_tensor(uname("lc_py0"), [128, TOK], F32) as py0,
        nc.psum_tensor(uname("lc_py1"), [128, TOK], F32) as py1,
        nc.psum_tensor(uname("lc_py2"), [128, TOK], F32) as py2,
    ):
        gidx4 = kc["g4idx"]
        pys = (py0, py1, py2)
        for grp in range(4):
            gb4 = c.sG.reserve(16 * 16)
            pe4 = c.sP.reserve(16)
            v4 = c.sV.reserve(16)

            @blk.gpsimd
            def _(g, grp=grp):
                for i in range(8):
                    hv = grp * 8 + i
                    g.indirect_dma_start(
                        out=OGC[:, i * TOK:(i + 1) * TOK], out_offset=None,
                        in_=cc2f, in_offset=IndirectOffsetOnAxis(
                            gidx4[:, hv:hv + 1], 0)).then_inc(c.sG.h, 16)
                for i in range(8):
                    hv = grp * 8 + i
                    g.dma_start(
                        out=WOF[:, i * 2048:(i + 1) * 2048]
                        .rearrange("p (a b) -> p a b", b=128),
                        in_=io["L1_attn_W_out"][hv * 128:(hv + 1) * 128, :]
                        .rearrange("(cc p) o -> p cc o", p=128)
                        ).then_inc(c.sG.h, 16)

            @blk.tensor
            def _(t, grp=grp, gb4=gb4, v4=v4):
                t.wait_ge(c.sG.h, gb4 + 16 * 16)
                for ot in range(NCH):
                    py = pys[ot % 3]
                    if ot >= 3:
                        t.wait_ge(c.sV.h, v4 + ot - 2)
                    for i in range(8):
                        for hh in range(2):
                            inst = t.matmul(
                                py[:, hh * 512:(hh + 1) * 512],
                                WOF[:, i * 2048 + ot * 128:
                                    i * 2048 + (ot + 1) * 128],
                                OGC[:, i * TOK + hh * 512:
                                    i * TOK + (hh + 1) * 512],
                                start=(i == 0), stop=(i == 7))
                    inst.then_inc(c.sP.h, 1)

            @blk.vector
            def _(v, grp=grp, pe4=pe4, v4=v4):
                for ot in range(NCH):
                    v.wait_ge(c.sP.h, pe4 + ot + 1)
                    py = pys[ot % 3]
                    v.tensor_add(res[:, ot * TOK:(ot + 1) * TOK],
                                 res[:, ot * TOK:(ot + 1) * TOK],
                                 py[:]).then_inc(c.sV.h, 1)
            c.barrier()


def emit_epilogue(c):
    nc, blk = c.nc, c.blk
    res, io = c.bufs["RES"], c.ios
    ob = c.sO.reserve(16)

    @blk.sync
    def _(s):
        s.dma_start(out=io["outT"].rearrange("(c p) t -> p c t", p=128),
                    in_=res[:].rearrange("p (c t) -> p c t", t=TOK)
                    ).then_inc(c.sO.h, 16)
        s.wait_ge(c.sO.h, ob + 16)



PHASES_ALL = ["l0ffn_a", "l0attn", "l0ffn_b", "l1ffn_a", "l1attn", "l1ffn_b"]
PHASES_DONE = ["l0ffn_a", "l0attn", "l0ffn_b", "l1ffn_a", "l1attn",
               "l1ffn_b"]


def build_program(phases=None):
    if phases is None:
        phases = PHASES_DONE
    _uid[0] = 0
    nc = bass.Bass("TRN2", target_bir_lowering=False, debug=False,
                   detect_race_conditions=RACE_DETECT[0])
    io = {}
    io["x0T"] = nc.dram_tensor("x0T", [D, TOK], F32, kind="ExternalInput").ap()
    for nm, (shp, dt_) in CONST_SPECS.items():
        io[nm] = nc.dram_tensor(nm, shp, dt_, kind="ExternalInput").ap()
    for nm, (shp, dt_) in WEIGHT_SPECS.items():
        io[nm] = nc.dram_tensor(nm, shp, dt_, kind="ExternalInput").ap()
    io["outT"] = nc.dram_tensor("outT", [D, TOK], F32,
                                kind="ExternalOutput").ap()
    io["qscr"] = nc.dram_tensor("qscr", [16, 128, TOK], BF16,
                                kind="Internal").ap()
    io["kv_in"] = nc.dram_tensor("kv_in", [1024, 1024], BF16,
                                 kind="Internal").ap()
    io["kv_out"] = nc.dram_tensor("kv_out", [4096, 1024], BF16,
                                  kind="Internal").ap()
    io["cc1_in"] = nc.dram_tensor("cc1_in", [6208, 2048], BF16,
                                  kind="Internal").ap()
    io["cc1_out"] = nc.dram_tensor("cc1_out", [8 * 6208, 2048], BF16,
                                   kind="Internal").ap()
    io["ccbg_in"] = nc.dram_tensor("ccbg_in", [64, 1024], F32,
                                   kind="Internal").ap()
    io["ccbg_out"] = nc.dram_tensor("ccbg_out", [512, 1024], F32,
                                    kind="Internal").ap()
    io["cc2_in"] = nc.dram_tensor("cc2_in", [2048, 2048], BF16,
                                  kind="Internal").ap()
    io["cc2_out"] = nc.dram_tensor("cc2_out", [16384, 2048], BF16,
                                   kind="Internal").ap()
    io["ba_scr"] = nc.dram_tensor("ba_scr", [64, 1024], F32,
                                  kind="Internal").ap()

    with ExitStack() as stack:
        res = stack.enter_context(nc.sbuf_tensor("RES", [128, NCH * TOK], F32))
        kc = {}
        for nm in KC_NAMES:
            shp, dt_ = (CONST_SPECS.get(nm) or WEIGHT_SPECS[nm])
            kc[nm] = stack.enter_context(nc.sbuf_tensor("kc_" + nm, shp, dt_))
        gwt = stack.enter_context(
            nc.sbuf_tensor("gate_wT", [128, NCH * 8], BF16))

        blk = stack.enter_context(nc.Block())
        c = Ctx(nc, blk, stack)
        c.ios = io
        c.bufs = {"RES": res, "KC": kc, "gate_wT": gwt}

        emit_prologue(c)

        for ph in phases:
            if ph in ("l0ffn_a", "l0ffn_b"):
                with (
                    nc.sbuf_tensor(uname("ffn_ln"), [128, NCH * 512], BF16) as ln,
                    nc.sbuf_tensor(uname("ffn_w0"), [128, 22528], BF16) as w0,
                    nc.sbuf_tensor(uname("ffn_w1"), [128, 22528], BF16) as w1,
                    nc.sbuf_tensor(uname("ffn_hs"), [128, 11 * 512], BF16) as hs,
                ):
                    emit_l0ffn(c, ln, (w0, w1), hs)
            elif ph == "l0attn":
                emit_l0attn(c)
            elif ph == "l1attn":
                emit_l1attn(c)
            elif ph in ("l1ffn_a", "l1ffn_b"):
                with (
                    nc.sbuf_tensor(uname("f1_ln"), [128, NCH * TOK], BF16) as ln,
                    nc.sbuf_tensor(uname("f1_wg0"), [128, 16 * 512], BF16) as wg0,
                    nc.sbuf_tensor(uname("f1_wg1"), [128, 16 * 512], BF16) as wg1,
                    nc.sbuf_tensor(uname("f1_wu0"), [128, 16 * 512], BF16) as wu0,
                    nc.sbuf_tensor(uname("f1_wu1"), [128, 16 * 512], BF16) as wu1,
                    nc.sbuf_tensor(uname("f1_wd"), [128, 4 * 2048], BF16) as wd,
                    nc.sbuf_tensor(uname("f1_G"), [128, 2 * TOK], BF16) as G,
                    nc.sbuf_tensor(uname("f1_F"), [128, 4 * TOK], BF16) as FU,
                    nc.sbuf_tensor(uname("f1_sdh"), [128, 16], F32) as sdh,
                ):
                    emit_l1ffn(c, ln, (wg0, wg1), (wu0, wu1), wd, G, FU, sdh)
            else:
                raise NotImplementedError(ph)

        emit_epilogue(c)
    return nc



def prep_weight_inputs(inputs):
    w = {}
    w["L0_attn_Wq"] = np.ascontiguousarray(
        inputs["L0_attn_Wq"].reshape(2048, 2048))
    w["L0_attn_Wk"] = np.ascontiguousarray(
        inputs["L0_attn_Wk"].reshape(2048, 512))
    w["L0_attn_Wv"] = np.ascontiguousarray(
        inputs["L0_attn_Wv"].reshape(2048, 512))
    w["L0_attn_Wo"] = np.ascontiguousarray(inputs["L0_attn_Wo"])
    for nm in ("gate_w", "w1", "w3", "w2", "sw1", "sw3", "sw2"):
        w[f"L0_ffn_{nm}"] = np.ascontiguousarray(inputs[f"L0_ffn_{nm}"])
    w["L1_attn_W_qkvz"] = np.ascontiguousarray(inputs["L1_attn_W_qkvz"])
    w["L1_attn_W_ba"] = np.ascontiguousarray(inputs["L1_attn_W_ba"])
    w["L1_attn_conv_w"] = np.ascontiguousarray(inputs["L1_attn_conv_w"])
    w["L1_attn_dt_bias"] = np.ascontiguousarray(
        inputs["L1_attn_dt_bias"].reshape(32, 1))
    w["L1_attn_A_log"] = np.ascontiguousarray(
        inputs["L1_attn_A_log"].reshape(32, 1))
    w["L1_attn_norm_w"] = np.ascontiguousarray(
        inputs["L1_attn_norm_w"].reshape(1, 128))
    w["L1_attn_W_out"] = np.ascontiguousarray(inputs["L1_attn_W_out"])
    for nm in ("w_gate", "w_up", "w_down"):
        w[f"L1_ffn_{nm}"] = np.ascontiguousarray(inputs[f"L1_ffn_{nm}"])
    w["L1_ffn_s_gate"] = np.ascontiguousarray(
        inputs["L1_ffn_s_gate"].reshape(64, 128).T)
    w["L1_ffn_s_up"] = np.ascontiguousarray(
        inputs["L1_ffn_s_up"].reshape(64, 128).T)
    w["L1_ffn_s_down"] = np.ascontiguousarray(
        inputs["L1_ffn_s_down"].reshape(16, 128).T)
    return w


def make_in_maps(inputs, phases=None):
    x = inputs["L0_attn_x_q"]
    w = prep_weight_inputs(inputs)
    in_maps = []
    for core in range(NCORE):
        b, p = core // 4, core % 4
        xs = np.ascontiguousarray(x[b, p * TOK:(p + 1) * TOK, :].T)
        m = {"x0T": xs}
        m.update(make_const_inputs(core))
        m.update(w)
        in_maps.append(m)
    return in_maps


def kernel(**inputs):
    nc = build_program()
    in_maps = make_in_maps(inputs)
    res = run_bass_kernel_spmd(nc, in_maps, core_ids=list(range(NCORE)))
    out = np.empty((2, 4096, 2048), np.float32)
    for core in range(NCORE):
        b, p = core // 4, core % 4
        out[b, p * TOK:(p + 1) * TOK, :] = res.results[core]["outT"].T
    return out


# version 02
# speedup vs baseline: 1.2003x; 1.2003x over previous
import numpy as np
import ml_dtypes
from contextlib import ExitStack

import concourse.bass as bass
import concourse.mybir as mybir
from concourse.bass_utils import run_bass_kernel_spmd
from concourse.bass import IndirectOffsetOnAxis

F32 = mybir.dt.float32
BF16 = mybir.dt.bfloat16
I8 = mybir.dt.int8
I32 = mybir.dt.int32
AF = mybir.ActivationFunctionType
ALU = mybir.AluOpType
AX = mybir.AxisListType

NCORE = 8
TOK = 1024
D = 2048
NCH = 16
LN_EPS = 1e-5

_uid = [0]
RACE_DETECT = [False]
L1DBG = [None]


def uname(s):
    _uid[0] += 1
    return f"{s}_{_uid[0]}"


class Sem:
    def __init__(self, stack, nc, name):
        self.h = stack.enter_context(nc.semaphore(name))
        self.n = 0

    def reserve(self, k):
        base = self.n
        self.n += k
        return base


class Ctx:
    def __init__(self, nc, blk, stack):
        self.nc = nc
        self.blk = blk
        self.stack = stack
        self.sW = Sem(stack, nc, "sW")
        self.sG = Sem(stack, nc, "sG")
        self.sO = Sem(stack, nc, "sO")
        self.sP = Sem(stack, nc, "sP")
        self.sV = Sem(stack, nc, "sV")
        self.sS = Sem(stack, nc, "sS")
        self.sQ = Sem(stack, nc, "sQ")
        self.sC = Sem(stack, nc, "sC")
        self.sB = Sem(stack, nc, "sB")
        self.ios = {}
        self.bufs = {}

    def barrier(self):
        base = self.sB.reserve(5)
        target = base + 5
        waits = [(s.h, s.n) for s in (self.sW, self.sG, self.sO, self.sC)
                 if s.n > 0]

        def f(e):
            e.drain().then_inc(self.sB.h, 1)
            e.wait_ge(self.sB.h, target)
            for h, n in waits:
                e.wait_ge(h, n)

        for sec in (self.blk.sync, self.blk.tensor, self.blk.vector,
                    self.blk.scalar, self.blk.gpsimd):
            sec(f)



def _bf16(x):
    return np.ascontiguousarray(x).astype(ml_dtypes.bfloat16)


def make_const_inputs(core):
    pos = core % 4
    ii = np.arange(128)
    c = {}
    c["ident_bf16"] = _bf16(np.eye(128, dtype=np.float32))
    c["ident_f32"] = np.eye(128, dtype=np.float32)
    c["ones_col_bf16"] = _bf16(np.ones((128, 1), np.float32))
    c["ones_row_f32"] = np.ones((1, 128), np.float32)
    c["ones_row_bf16"] = _bf16(np.ones((1, 128), np.float32))
    oh = np.zeros((8, 8 * 128), np.float32)
    for e in range(8):
        oh[e, e * 128:(e + 1) * 128] = 1.0
    c["onehot8"] = oh
    c["eps_ln"] = np.full((128, 1), 1e-5, np.float32)
    c["eps_a"] = np.full((128, 1), 1e-6, np.float32)
    c["mask_su"] = (ii[None, :] > ii[:, None]).astype(np.float32)
    c["mask_u"] = (ii[None, :] >= ii[:, None]).astype(np.float32)
    c["mask_su_neg"] = -c["mask_su"]
    c["mask_sl_neg"] = -(ii[None, :] < ii[:, None]).astype(np.float32)
    c["one_col"] = np.ones((128, 1), np.float32)
    b = core // 4
    rloc = core % 4
    BLK = 6208 * 2048
    hks = [4 * rloc + i for i in range(4)]
    hvs = [8 * rloc + i for i in range(8)]

    def mixrow(m, ch, p):
        return (b * 4 + m) * 12416 + ch * 128 + p

    g1 = np.zeros((128, 96), np.int32)
    col = 0
    pieces = []
    for hk in hks:
        pieces.append(4 * hk + 0)
    for hk in hks:
        pieces.append(4 * hk + 1)
    for hv in hvs:
        pieces.append(4 * (hv // 2) + 2 + (hv % 2))
    for hv in hvs:
        pieces.append(64 + 2 * (hv // 2) + (hv % 2))
    for ch in pieces:
        for m in range(4):
            g1[:, col] = mixrow(m, ch, ii)
            col += 1
    c["g1idx"] = g1
    g2 = np.zeros((4, 16), np.int32)
    for ui, hv in enumerate(hvs):
        for m in range(4):
            g2[m, ui] = (b * 4 + m) * 64 + hv
            g2[m, 8 + ui] = (b * 4 + m) * 64 + 32 + hv
    c["g2idx"] = g2
    g3 = np.zeros((128, 16), np.int32)
    col = 0
    for hk in hks:
        g3[:, col] = hk * 512 + ii
        col += 1
    for hk in hks:
        g3[:, col] = hk * 512 + 128 + ii
        col += 1
    for hv in hvs:
        g3[:, col] = (hv // 2) * 512 + 256 + (hv % 2) * 128 + ii
        col += 1
    c["g3idx"] = g3
    g4 = np.zeros((128, 32), np.int32)
    for hv in range(32):
        rc = b * 4 + hv // 8
        u = hv % 8
        g4[:, hv] = rc * 4096 + u * 512 + ii * 4 + rloc
    c["g4idx"] = g4
    big = np.int32(1 << 28)
    hid = np.full((128, 1), big, np.int32)
    if pos > 0:
        hid[:, 0] = (core - 1) * 128 + ii
    c["halo_idx"] = hid
    for s in range(3):
        j = pos - 3 + s
        cid = np.full((128, 1), big, np.int32)
        if 0 <= j < pos:
            cid[:, 0] = ((core // 4) * 4 + j) * 128 + ii
        c[f"comp_idx{s}"] = cid
    return c


CONST_SPECS = {
    "ident_bf16": ([128, 128], BF16),
    "ident_f32": ([128, 128], F32),
    "ones_col_bf16": ([128, 1], BF16),
    "ones_row_f32": ([1, 128], F32),
    "ones_row_bf16": ([1, 128], BF16),
    "onehot8": ([8, 8 * 128], F32),
    "eps_ln": ([128, 1], F32),
    "eps_a": ([128, 1], F32),
    "mask_su": ([128, 128], F32),
    "mask_u": ([128, 128], F32),
    "mask_su_neg": ([128, 128], F32),
    "mask_sl_neg": ([128, 128], F32),
    "one_col": ([128, 1], F32),
    "g1idx": ([128, 96], I32),
    "g2idx": ([4, 16], I32),
    "g3idx": ([128, 16], I32),
    "g4idx": ([128, 32], I32),
    "halo_idx": ([128, 1], I32),
    "comp_idx0": ([128, 1], I32),
    "comp_idx1": ([128, 1], I32),
    "comp_idx2": ([128, 1], I32),
}

WEIGHT_SPECS = {
    "L0_attn_Wq": ([2048, 2048], F32),
    "L0_attn_Wk": ([2048, 512], F32),
    "L0_attn_Wv": ([2048, 512], F32),
    "L0_attn_Wo": ([2048, 2048], F32),
    "L0_ffn_gate_w": ([8, 2048], F32),
    "L0_ffn_w1": ([8, 2048, 1408], F32),
    "L0_ffn_w3": ([8, 2048, 1408], F32),
    "L0_ffn_w2": ([8, 1408, 2048], F32),
    "L0_ffn_sw1": ([2048, 2816], F32),
    "L0_ffn_sw3": ([2048, 2816], F32),
    "L0_ffn_sw2": ([2816, 2048], F32),
    "L1_attn_W_qkvz": ([2048, 12288], F32),
    "L1_attn_W_ba": ([2048, 64], F32),
    "L1_attn_conv_w": ([8192, 4], F32),
    "L1_attn_dt_bias": ([32, 1], F32),
    "L1_attn_A_log": ([32, 1], F32),
    "L1_attn_norm_w": ([1, 128], F32),
    "L1_attn_W_out": ([4096, 2048], F32),
    "L1_ffn_w_gate": ([2048, 8192], I8),
    "L1_ffn_w_up": ([2048, 8192], I8),
    "L1_ffn_w_down": ([8192, 2048], I8),
    "L1_ffn_s_gate": ([128, 64], F32),
    "L1_ffn_s_up": ([128, 64], F32),
    "L1_ffn_s_down": ([128, 16], F32),
}

KC_NAMES = ["ident_bf16", "ident_f32", "ones_col_bf16", "ones_row_f32",
            "ones_row_bf16", "onehot8", "eps_ln", "eps_a", "mask_su",
            "mask_u", "mask_su_neg", "mask_sl_neg", "one_col",
            "g1idx", "g2idx", "g3idx", "g4idx", "halo_idx",
            "comp_idx0", "comp_idx1", "comp_idx2",
            "L1_attn_dt_bias", "L1_attn_A_log", "L1_attn_norm_w",
            "L1_ffn_s_gate", "L1_ffn_s_up", "L1_ffn_s_down"]



def emit_prologue(c):
    nc, blk = c.nc, c.blk
    io, kc = c.ios, c.bufs["KC"]
    res = c.bufs["RES"]
    gwt = c.bufs["gate_wT"]

    wb = c.sW.reserve(16 * (1 + len(KC_NAMES)))
    gb = c.sG.reserve(16)

    @blk.sync
    def _(s):
        s.dma_start(out=res[:].rearrange("p (c t) -> p c t", t=TOK),
                    in_=io["x0T"].rearrange("(c p) t -> p c t", p=128)
                    ).then_inc(c.sW.h, 16)
        for nm in KC_NAMES:
            s.dma_start(out=kc[nm][:], in_=io[nm]).then_inc(c.sW.h, 16)

    pb = c.sP.reserve(NCH)
    vb = c.sV.reserve(NCH)
    with (
        nc.sbuf_tensor(uname("gate_w_bf"), [8, 2048], BF16) as gw,
        nc.psum_tensor(uname("pgt"), [128, 16], BF16) as pgt,
    ):
        @blk.gpsimd
        def _(g):
            g.dma_start(out=gw[:], in_=io["L0_ffn_gate_w"]).then_inc(c.sG.h, 16)

        @blk.tensor
        def _(t):
            t.wait_ge(c.sG.h, gb + 16)
            t.wait_ge(c.sW.h, wb + 16 * (1 + len(KC_NAMES)))
            for ch in range(NCH):
                if ch >= 2:
                    t.wait_ge(c.sV.h, vb + ch - 1)
                sl = (ch % 2) * 8
                t.transpose(pgt[:, sl:sl + 8], gw[:, ch * 128:(ch + 1) * 128],
                            kc["ident_bf16"][0:8, 0:8]).then_inc(c.sP.h, 1)

        @blk.vector
        def _(v):
            for ch in range(NCH):
                v.wait_ge(c.sP.h, pb + ch + 1)
                sl = (ch % 2) * 8
                v.tensor_copy(gwt[:, ch * 8:(ch + 1) * 8],
                              pgt[:, sl:sl + 8]).then_inc(c.sV.h, 1)
    c.barrier()


def emit_ln(c, tok0, T, ln, ln_off=0, ln_stride=None):
    assert T <= 512
    if ln_stride is None:
        ln_stride = T
    nc, blk = c.nc, c.blk
    kc, res = c.bufs["KC"], c.bufs["RES"]

    def L(ch):
        return ln[:, ln_off + ch * ln_stride: ln_off + ch * ln_stride + T]

    with (
        nc.sbuf_tensor(uname("ln_sq"), [128, 2 * T], BF16) as sq,
        nc.sbuf_tensor(uname("ln_mb"), [128, T], F32) as mb,
        nc.sbuf_tensor(uname("ln_rb"), [128, T], F32) as rb,
        nc.sbuf_tensor(uname("ln_tp"), [128, T], F32) as tmp,
        nc.psum_tensor(uname("ln_red"), [1, 2 * T], F32) as ps_red,
        nc.psum_tensor(uname("ln_bc"), [128, 2 * T], F32) as ps_bc,
    ):
        qb = c.sQ.reserve(NCH)
        sb = c.sS.reserve(NCH)
        pb = c.sP.reserve(NCH)

        @blk.gpsimd
        def _(g):
            for ch in range(NCH):
                g.tensor_copy(L(ch),
                              res[:, ch * TOK + tok0:ch * TOK + tok0 + T]
                              ).then_inc(c.sQ.h, 1)

        @blk.scalar
        def _(s):
            for ch in range(NCH):
                s.wait_ge(c.sQ.h, qb + ch + 1)
                if ch >= 2:
                    s.wait_ge(c.sP.h, pb + ch - 1)
                s.activation(sq[:, (ch % 2) * T:(ch % 2) * T + T], L(ch),
                             AF.Square).then_inc(c.sS.h, 1)

        @blk.tensor
        def _(t):
            t.wait_ge(c.sQ.h, qb + NCH)
            for ch in range(NCH):
                t.matmul(ps_red[0:1, 0:T], kc["ones_col_bf16"][:], L(ch),
                         start=(ch == 0), stop=(ch == NCH - 1))
            for ch in range(NCH):
                t.wait_ge(c.sS.h, sb + ch + 1)
                t.matmul(ps_red[0:1, T:2 * T], kc["ones_col_bf16"][:],
                         sq[:, (ch % 2) * T:(ch % 2) * T + T],
                         start=(ch == 0), stop=(ch == NCH - 1)
                         ).then_inc(c.sP.h, 1)

        vb = c.sV.reserve(3)
        sb2 = c.sS.reserve(1)

        @blk.vector
        def _(v):
            v.wait_ge(c.sP.h, pb + NCH)
            v.tensor_scalar(mb[0:1, 0:T], ps_red[0:1, 0:T], -1.0 / D, None,
                            ALU.mult)
            v.tensor_scalar(rb[0:1, 0:T], ps_red[0:1, 0:T], 1.0 / D,
                            None, ALU.mult).then_inc(c.sV.h, 1)
            v.wait_ge(c.sV.h, vb + 1)
            v.tensor_mul(rb[0:1, 0:T], rb[0:1, 0:T],
                         rb[0:1, 0:T]).then_inc(c.sV.h, 1)
            v.wait_ge(c.sV.h, vb + 2)
            v.scalar_tensor_tensor(rb[0:1, 0:T], ps_red[0:1, T:2 * T],
                                   1.0 / D, rb[0:1, 0:T],
                                   ALU.mult, ALU.subtract).then_inc(c.sV.h, 1)

        @blk.scalar
        def _(s):
            s.wait_ge(c.sV.h, vb + 3)
            s.activation(rb[0:1, 0:T], rb[0:1, 0:T],
                         AF.Ln, bias=kc["eps_ln"][0:1, :])
            s.activation(rb[0:1, 0:T], rb[0:1, 0:T],
                         AF.Exp, scale=-0.5).then_inc(c.sS.h, 1)

        pb2 = c.sP.reserve(1)

        @blk.tensor
        def _(t):
            t.wait_ge(c.sS.h, sb2 + 1)
            t.matmul(ps_bc[:, 0:T], kc["ones_row_f32"][:], mb[0:1, 0:T],
                     start=True, stop=True)
            t.matmul(ps_bc[:, T:2 * T], kc["ones_row_f32"][:],
                     rb[0:1, 0:T],
                     start=True, stop=True).then_inc(c.sP.h, 1)

        vb2 = c.sV.reserve(2 + 2 * NCH)

        @blk.vector
        def _(v):
            v.wait_ge(c.sP.h, pb2 + 1)
            v.tensor_copy(mb[:], ps_bc[:, 0:T]).then_inc(c.sV.h, 1)
            v.tensor_copy(rb[:], ps_bc[:, T:2 * T]).then_inc(c.sV.h, 1)
            for ch in range(NCH):
                v.tensor_add(tmp[:], res[:, ch * TOK + tok0:ch * TOK + tok0 + T],
                             mb[:]).then_inc(c.sV.h, 1)
                v.tensor_mul(L(ch), tmp[:], rb[:]).then_inc(c.sV.h, 1)
        c.barrier()


def _w_ap_moe(io, e, which):
    if e < 8:
        a = io[f"L0_ffn_{which}"][e]
    else:
        j = e - 8
        if which == "w1":
            a = io["L0_ffn_sw1"][:, j * 1408:(j + 1) * 1408]
        elif which == "w3":
            a = io["L0_ffn_sw3"][:, j * 1408:(j + 1) * 1408]
        else:
            a = io["L0_ffn_sw2"][j * 1408:(j + 1) * 1408, :]
    return a.rearrange("(c p) i -> p c i", p=128)


def emit_l0ffn(c, ln, wslot, hs, experts=tuple(range(10)), halves=(0, 1), dump_gate=False):
    nc, blk = c.nc, c.blk
    io, kc, res = c.ios, c.bufs["KC"], c.bufs["RES"]
    gwt = c.bufs["gate_wT"]
    T = 512

    with (
        nc.sbuf_tensor(uname("gt_lg"), [128, 32], F32) as LG,
        nc.sbuf_tensor(uname("gt_m"), [128, 16], F32) as M,
        nc.sbuf_tensor(uname("gt_w"), [128, 32], F32) as W,
        nc.sbuf_tensor(uname("gt_wt"), [8, 512], F32) as Wt,
        nc.sbuf_tensor(uname("gt_tp"), [128, 32], F32) as GTMP,
    ):
        for half in halves:
            tok0 = half * T
            emit_ln(c, tok0, T, ln)

            with (
                nc.psum_tensor(uname("pf_lg"), [128, 32], F32) as ps_lg,
                nc.psum_tensor(uname("pf_wt"), [8, 256], F32) as ps_wt,
            ):
                pb = c.sP.reserve(1)

                @blk.tensor
                def _(t):
                    for tt in range(4):
                        for ch in range(NCH):
                            inst = t.matmul(
                                ps_lg[:, tt * 8:(tt + 1) * 8],
                                ln[:, ch * T + tt * 128:ch * T + tt * 128 + 128],
                                gwt[:, ch * 8:(ch + 1) * 8],
                                start=(ch == 0), stop=(ch == NCH - 1))
                    inst.then_inc(c.sP.h, 1)

                DBG_BAR = False
                vb = c.sV.reserve(1)
                sb = c.sS.reserve(1)

                @blk.vector
                def _(v):
                    v.wait_ge(c.sP.h, pb + 1)
                    v.tensor_copy(LG[:], ps_lg[:])
                    for tt in range(4):
                        v.reduce_max(M[:, tt:tt + 1],
                                     LG[:, tt * 8:(tt + 1) * 8], axis=AX.X)
                    for tt in range(4):
                        v.tensor_scalar(M[:, 4 + tt:5 + tt], M[:, tt:tt + 1],
                                        -1.0, None, ALU.mult)
                    for tt in range(4):
                        v.tensor_scalar(GTMP[:, tt * 8:(tt + 1) * 8],
                                        LG[:, tt * 8:(tt + 1) * 8],
                                        M[:, tt:tt + 1], None, ALU.is_ge)
                    for tt in range(4):
                        v.scalar_tensor_tensor(GTMP[:, tt * 8:(tt + 1) * 8],
                                               GTMP[:, tt * 8:(tt + 1) * 8],
                                               -30000.0,
                                               LG[:, tt * 8:(tt + 1) * 8],
                                               ALU.mult, ALU.add)
                    for tt in range(4):
                        inst = v.reduce_max(M[:, 8 + tt:9 + tt],
                                            GTMP[:, tt * 8:(tt + 1) * 8],
                                            axis=AX.X)
                    inst.then_inc(c.sV.h, 1)
                if DBG_BAR:
                    c.barrier()

                @blk.scalar
                def _(s):
                    s.wait_ge(c.sV.h, vb + 1)
                    for tt in range(4):
                        inst = s.activation(W[:, tt * 8:(tt + 1) * 8],
                                            LG[:, tt * 8:(tt + 1) * 8], AF.Exp,
                                            bias=M[:, 4 + tt:5 + tt])
                    inst.then_inc(c.sS.h, 1)
                if DBG_BAR:
                    c.barrier()

                vb2 = c.sV.reserve(1)

                @blk.vector
                def _(v):
                    v.wait_ge(c.sS.h, sb + 1)
                    for tt in range(4):
                        v.tensor_scalar(GTMP[:, tt * 8:(tt + 1) * 8],
                                        LG[:, tt * 8:(tt + 1) * 8],
                                        M[:, 8 + tt:9 + tt], None, ALU.is_ge)
                    for tt in range(4):
                        v.tensor_mul(W[:, tt * 8:(tt + 1) * 8],
                                     W[:, tt * 8:(tt + 1) * 8],
                                     GTMP[:, tt * 8:(tt + 1) * 8])
                    for tt in range(4):
                        v.reduce_sum(M[:, 12 + tt:13 + tt],
                                     W[:, tt * 8:(tt + 1) * 8], axis=AX.X)
                    for tt in range(4):
                        v.reciprocal(M[:, 12 + tt:13 + tt],
                                     M[:, 12 + tt:13 + tt])
                    for tt in range(4):
                        inst = v.tensor_scalar(W[:, tt * 8:(tt + 1) * 8],
                                               W[:, tt * 8:(tt + 1) * 8],
                                               M[:, 12 + tt:13 + tt], 0.5,
                                               ALU.mult, ALU.mult)
                    inst.then_inc(c.sV.h, 1)

                if DBG_BAR:
                    c.barrier()
                pb2 = c.sP.reserve(4)
                vb3 = c.sV.reserve(4)

                @blk.tensor
                def _(t):
                    t.wait_ge(c.sV.h, vb2 + 1)
                    for tt in range(4):
                        if tt >= 2:
                            t.wait_ge(c.sV.h, vb3 + tt - 1)
                        t.transpose(
                            ps_wt[:, (tt % 2) * 128:(tt % 2) * 128 + 128],
                            W[:, tt * 8:(tt + 1) * 8],
                            kc["ident_f32"][:]).then_inc(c.sP.h, 1)

                @blk.vector
                def _(v):
                    for tt in range(4):
                        v.wait_ge(c.sP.h, pb2 + tt + 1)
                        v.tensor_copy(
                            Wt[:, tt * 128:(tt + 1) * 128],
                            ps_wt[:, (tt % 2) * 128:(tt % 2) * 128 + 128]
                            ).then_inc(c.sV.h, 1)
                c.barrier()
                if dump_gate:
                    res_ = c.bufs["RES"]
                    vbD = c.sV.reserve(3)

                    @blk.vector
                    def _(v):
                        v.tensor_copy(res_[:, 0:32], W[:]).then_inc(c.sV.h, 1)
                        v.tensor_copy(res_[0:8, 40:552], Wt[:]).then_inc(c.sV.h, 1)
                        v.tensor_copy(res_[:, 600:632], LG[:]).then_inc(c.sV.h, 1)
                    c.barrier()

            with (
                nc.psum_tensor(uname("pf_wb"), [128, 512], F32) as ps_wb,
                nc.psum_tensor(uname("pf_h0"), [128, 512], F32) as ph0,
                nc.psum_tensor(uname("pf_h1"), [128, 512], F32) as ph1,
                nc.psum_tensor(uname("pf_y0"), [128, 512], F32) as py0,
                nc.psum_tensor(uname("pf_y1"), [128, 512], F32) as py1,
            ):
                gbase = c.sG.reserve(16 * 3 * len(experts))
                pe_incs = []
                for e in experts:
                    pe_incs += [11, 11, 16]
                pe_base = c.sP.reserve(sum(pe_incs))
                sc_base = c.sS.reserve(len(experts) * 11)
                v_counts = []
                for e in experts:
                    v_counts += [0, 22 if e < 8 else 11, 16]
                v_base = c.sV.reserve(sum(v_counts))

                def pe_target(p):
                    return pe_base + sum(pe_incs[:p + 1])

                def v_target(p):
                    return v_base + sum(v_counts[:p + 1])

                @blk.gpsimd
                def _(g):
                    for ei, e in enumerate(experts):
                        for wi, which in enumerate(("w1", "w3", "w2")):
                            p = ei * 3 + wi
                            if p >= 2:
                                g.wait_ge(c.sP.h, pe_target(p - 2))
                            g.dma_start(out=wslot[p % 2][:],
                                        in_=_w_ap_moe(io, e, which)
                                        ).then_inc(c.sG.h, 16)

                @blk.tensor
                def _(t):
                    for ei, e in enumerate(experts):
                        p = ei * 3
                        t.wait_ge(c.sG.h, gbase + 16 * (p + 1))
                        if e < 8:
                            t.matmul(ps_wb[:],
                                     kc["onehot8"][:, e * 128:(e + 1) * 128],
                                     Wt[:], start=True, stop=True)
                        slot = wslot[p % 2]
                        for it in range(11):
                            ph = (ph0, ph1)[it % 2]
                            if it >= 2:
                                t.wait_ge(c.sS.h, sc_base + 11 * ei + it - 1)
                            for ch2 in range(NCH):
                                inst = t.matmul(
                                    ph[:],
                                    slot[:, ch2 * 1408 + it * 128:
                                         ch2 * 1408 + it * 128 + 128],
                                    ln[:, ch2 * T:(ch2 + 1) * T],
                                    start=(ch2 == 0), stop=(ch2 == NCH - 1))
                            inst.then_inc(c.sP.h, 1)
                        p = ei * 3 + 1
                        t.wait_ge(c.sG.h, gbase + 16 * (p + 1))
                        t.wait_ge(c.sS.h, sc_base + 11 * (ei + 1))
                        slot = wslot[p % 2]
                        per = 2 if e < 8 else 1
                        for it in range(11):
                            ph = (ph0, ph1)[it % 2]
                            if it >= 2:
                                t.wait_ge(c.sV.h,
                                          v_target(p - 1) + per * (it - 1))
                            for ch2 in range(NCH):
                                inst = t.matmul(
                                    ph[:],
                                    slot[:, ch2 * 1408 + it * 128:
                                         ch2 * 1408 + it * 128 + 128],
                                    ln[:, ch2 * T:(ch2 + 1) * T],
                                    start=(ch2 == 0), stop=(ch2 == NCH - 1))
                            inst.then_inc(c.sP.h, 1)
                        p = ei * 3 + 2
                        t.wait_ge(c.sG.h, gbase + 16 * (p + 1))
                        t.wait_ge(c.sV.h, v_target(p - 1))
                        slot = wslot[p % 2]
                        for ot in range(NCH):
                            py = (py0, py1)[ot % 2]
                            if ot >= 2:
                                t.wait_ge(c.sV.h, v_target(p - 1) + ot - 1)
                            for it in range(11):
                                inst = t.matmul(
                                    py[:],
                                    slot[:, it * 2048 + ot * 128:
                                         it * 2048 + ot * 128 + 128],
                                    hs[:, it * T:(it + 1) * T],
                                    start=(it == 0), stop=(it == 10))
                            inst.then_inc(c.sP.h, 1)

                @blk.scalar
                def _(s):
                    for ei, e in enumerate(experts):
                        base_p = pe_base + sum(pe_incs[:ei * 3])
                        for it in range(11):
                            s.wait_ge(c.sP.h, base_p + it + 1)
                            ph = (ph0, ph1)[it % 2]
                            s.activation(hs[:, it * T:(it + 1) * T], ph[:],
                                         AF.Silu).then_inc(c.sS.h, 1)

                @blk.vector
                def _(v):
                    for ei, e in enumerate(experts):
                        p = ei * 3 + 1
                        base_p = pe_base + sum(pe_incs[:p])
                        for it in range(11):
                            v.wait_ge(c.sP.h, base_p + it + 1)
                            ph = (ph0, ph1)[it % 2]
                            v.tensor_mul(hs[:, it * T:(it + 1) * T],
                                         hs[:, it * T:(it + 1) * T],
                                         ph[:]).then_inc(c.sV.h, 1)
                            if e < 8:
                                v.tensor_mul(hs[:, it * T:(it + 1) * T],
                                             hs[:, it * T:(it + 1) * T],
                                             ps_wb[:]).then_inc(c.sV.h, 1)
                        p = ei * 3 + 2
                        base_p = pe_base + sum(pe_incs[:p])
                        osc = 1.0 if e < 8 else 0.5
                        for ot in range(NCH):
                            v.wait_ge(c.sP.h, base_p + ot + 1)
                            py = (py0, py1)[ot % 2]
                            v.scalar_tensor_tensor(
                                res[:, ot * TOK + tok0:ot * TOK + tok0 + T],
                                py[:], osc,
                                res[:, ot * TOK + tok0:ot * TOK + tok0 + T],
                                ALU.mult, ALU.add).then_inc(c.sV.h, 1)
                c.barrier()


def emit_l0attn(c):
    nc, blk = c.nc, c.blk
    io, kc, res = c.ios, c.bufs["KC"], c.bufs["RES"]
    qscr = io["qscr"]
    kv_in = io["kv_in"]
    kv_out = io["kv_out"]
    GROUPS = [[0, 1, 2, 3], [4, 5, 6, 7]]

    with (
        nc.sbuf_tensor(uname("at_ln"), [128, NCH * TOK], BF16) as ln,
        nc.sbuf_tensor(uname("at_kf"), [128, 4 * TOK], BF16) as kf,
        nc.sbuf_tensor(uname("at_vt"), [128, 8 * 512], BF16) as vt,
        nc.sbuf_tensor(uname("at_wkv"), [128, 2 * NCH * 512], BF16) as wkv,
        nc.sbuf_tensor(uname("at_wq0"), [128, NCH * 128], BF16) as wq0,
        nc.sbuf_tensor(uname("at_wq1"), [128, NCH * 128], BF16) as wq1,
        nc.sbuf_tensor(uname("at_qt"), [128, 2 * TOK], BF16) as qt,
        nc.sbuf_tensor(uname("at_sq"), [128, TOK], BF16) as sqb,
        nc.sbuf_tensor(uname("at_row"), [1, TOK], F32) as row,
    ):
        emit_ln(c, 0, 512, ln, ln_off=0, ln_stride=TOK)
        emit_ln(c, 512, 512, ln, ln_off=512, ln_stride=TOK)
        import contextlib as _ctl
        _stkA = _ctl.ExitStack()
        pq0 = _stkA.enter_context(nc.psum_tensor(uname("at_pq0"), [128, TOK], F32))
        pq1 = _stkA.enter_context(nc.psum_tensor(uname("at_pq1"), [128, TOK], F32))
        psm = _stkA.enter_context(nc.psum_tensor(uname("at_psm"), [1, TOK], F32))
        pbc = _stkA.enter_context(nc.psum_tensor(uname("at_pbc"), [128, TOK], F32))

        wq_slots = (wq0, wq1)
        gb = c.sG.reserve(16 * (2 + 16 + 2))
        cb = c.sC.reserve(1)

        pe_b = c.sP.reserve(4 * 3 + 8 + 16 * 3)
        v_b = c.sV.reserve(4 * 3 + 8 + 16 * 3)
        s_b = c.sS.reserve(2 * (4 + 16))
        w_b = c.sW.reserve(16 * 16)

        def peK(i, j):
            return pe_b + 3 * i + j

        def peV(i):
            return pe_b + 12 + i + 1

        def peQ(i, j):
            return pe_b + 20 + 3 * i + j

        def vK(i, j):
            return v_b + 3 * i + j

        def vV(i):
            return v_b + 12 + i + 1

        def vQ(i, j):
            return v_b + 20 + 3 * i + j

        def sK(i, j):
            return s_b + 2 * i + j

        def sQ_(i, j):
            return s_b + 8 + 2 * i + j

        @blk.gpsimd
        def _(g):
            g.dma_start(
                out=wkv[:, 0:NCH * 512].rearrange("p (a b) -> p a b", b=512),
                in_=io["L0_attn_Wk"].rearrange("(c p) o -> p c o", p=128)
            ).then_inc(c.sG.h, 16)
            g.dma_start(
                out=wkv[:, NCH * 512:].rearrange("p (a b) -> p a b", b=512),
                in_=io["L0_attn_Wv"].rearrange("(c p) o -> p c o", p=128)
            ).then_inc(c.sG.h, 16)
            g.wait_ge(c.sV.h, vV(7))
            g.dma_start(
                out=kv_in[0:512, :].rearrange("(i p) t -> p i t", p=128),
                in_=kf[:].rearrange("p (i t) -> p i t", t=TOK)
            ).then_inc(c.sG.h, 16)
            g.dma_start(
                out=kv_in[512:1024, :].rearrange("(a r) (h f) -> (r h) a f",
                                                 a=8, h=2),
                in_=vt[:].rearrange("p (a f) -> p a f", f=512)
            ).then_inc(c.sG.h, 16)
            g.wait_ge(c.sG.h, gb + 16 * 4)
            g.collective_compute(
                "AllGather", ALU.bypass, replica_groups=GROUPS,
                ins=[kv_in], outs=[kv_out]).then_inc(c.sC.h, 1)
            for h in range(16):
                if h >= 2:
                    g.wait_ge(c.sP.h, peQ(h - 2, 1))
                g.dma_start(
                    out=wq_slots[h % 2][:].rearrange("p (a b) -> p a b", b=128),
                    in_=io["L0_attn_Wq"][:, h * 128:(h + 1) * 128]
                    .rearrange("(c p) o -> p c o", p=128)).then_inc(c.sG.h, 16)

        @blk.tensor
        def _(t):
            t.wait_ge(c.sG.h, gb + 16 * 2)
            for i in range(4):
                if i >= 2:
                    t.wait_ge(c.sV.h, vK(i - 2, 1))
                for ch in range(NCH):
                    for hh in range(2):
                        inst = t.matmul(
                            (pq0, pq1)[i % 2][:, hh * 512:(hh + 1) * 512],
                            wkv[:, ch * 512 + i * 128:ch * 512 + i * 128 + 128],
                            ln[:, ch * TOK + hh * 512:ch * TOK + (hh + 1) * 512],
                            start=(ch == 0), stop=(ch == NCH - 1))
                inst.then_inc(c.sP.h, 1)
                t.wait_ge(c.sS.h, sK(i, 1))
                if i >= 1:
                    t.wait_ge(c.sV.h, vK(i - 1, 2))
                for hh in range(2):
                    inst = t.matmul(psm[0:1, hh * 512:(hh + 1) * 512],
                                    kc["ones_col_bf16"][:],
                                    sqb[:, hh * 512:(hh + 1) * 512],
                                    start=True, stop=True)
                inst.then_inc(c.sP.h, 1)
                t.wait_ge(c.sS.h, sK(i, 2))
                if i >= 1:
                    t.wait_ge(c.sV.h, vK(i - 1, 3))
                for hh in range(2):
                    inst = t.matmul(pbc[:, hh * 512:(hh + 1) * 512],
                                    kc["ones_row_f32"][:],
                                    row[0:1, hh * 512:(hh + 1) * 512],
                                    start=True, stop=True)
                inst.then_inc(c.sP.h, 1)
            for a in range(8):
                pv = (pq0, pq1)[a % 2]
                if a >= 2:
                    t.wait_ge(c.sV.h, vV(a - 2))
                else:
                    t.wait_ge(c.sV.h, vK(2 + a, 1))
                for ch in range(NCH):
                    inst = t.matmul(
                        pv[:, 0:512],
                        ln[:, ch * TOK + a * 128:ch * TOK + a * 128 + 128],
                        wkv[:, NCH * 512 + ch * 512:NCH * 512 + (ch + 1) * 512],
                        start=(ch == 0), stop=(ch == NCH - 1))
                inst.then_inc(c.sP.h, 1)
            for h in range(16):
                pq = (pq0, pq1)[h % 2]
                t.wait_ge(c.sG.h, gb + 16 * (4 + h + 1))
                if h >= 2:
                    t.wait_ge(c.sV.h, vQ(h - 2, 1))
                else:
                    t.wait_ge(c.sV.h, vV(6 + h))
                for ch in range(NCH):
                    for hh in range(2):
                        inst = t.matmul(
                            pq[:, hh * 512:(hh + 1) * 512],
                            wq_slots[h % 2][:, ch * 128:(ch + 1) * 128],
                            ln[:, ch * TOK + hh * 512:ch * TOK + (hh + 1) * 512],
                            start=(ch == 0), stop=(ch == NCH - 1))
                inst.then_inc(c.sP.h, 1)
                t.wait_ge(c.sS.h, sQ_(h, 1))
                if h >= 1:
                    t.wait_ge(c.sV.h, vQ(h - 1, 2))
                for hh in range(2):
                    inst = t.matmul(psm[0:1, hh * 512:(hh + 1) * 512],
                                    kc["ones_col_bf16"][:],
                                    sqb[:, hh * 512:(hh + 1) * 512],
                                    start=True, stop=True)
                inst.then_inc(c.sP.h, 1)
                t.wait_ge(c.sS.h, sQ_(h, 2))
                if h >= 1:
                    t.wait_ge(c.sV.h, vQ(h - 1, 3))
                for hh in range(2):
                    inst = t.matmul(pbc[:, hh * 512:(hh + 1) * 512],
                                    kc["ones_row_f32"][:],
                                    row[0:1, hh * 512:(hh + 1) * 512],
                                    start=True, stop=True)
                inst.then_inc(c.sP.h, 1)

        @blk.vector
        def _(v):
            for i in range(4):
                v.wait_ge(c.sP.h, peK(i, 1))
                v.tensor_copy(kf[:, i * TOK:(i + 1) * TOK],
                              (pq0, pq1)[i % 2][:]).then_inc(c.sV.h, 1)
                v.wait_ge(c.sP.h, peK(i, 2))
                if i >= 1:
                    v.wait_ge(c.sS.h, sK(i - 1, 2))
                v.tensor_copy(row[:], psm[:]).then_inc(c.sV.h, 1)
                v.wait_ge(c.sP.h, peK(i, 3))
                v.tensor_mul(kf[:, i * TOK:(i + 1) * TOK],
                             kf[:, i * TOK:(i + 1) * TOK],
                             pbc[:]).then_inc(c.sV.h, 1)
            for a in range(8):
                v.wait_ge(c.sP.h, peV(a))
                v.tensor_copy(vt[:, a * 512:(a + 1) * 512],
                              (pq0, pq1)[a % 2][:, 0:512]).then_inc(c.sV.h, 1)
            for h in range(16):
                pq = (pq0, pq1)[h % 2]
                v.wait_ge(c.sP.h, peQ(h, 1))
                if h >= 2:
                    v.wait_ge(c.sW.h, w_b + 16 * (h - 1))
                v.tensor_copy(qt[:, (h % 2) * TOK:(h % 2 + 1) * TOK],
                              pq[:]).then_inc(c.sV.h, 1)
                v.wait_ge(c.sP.h, peQ(h, 2))
                if h >= 1:
                    v.wait_ge(c.sS.h, sQ_(h - 1, 2))
                v.tensor_copy(row[:], psm[:]).then_inc(c.sV.h, 1)
                v.wait_ge(c.sP.h, peQ(h, 3))
                v.tensor_mul(qt[:, (h % 2) * TOK:(h % 2 + 1) * TOK],
                             qt[:, (h % 2) * TOK:(h % 2 + 1) * TOK],
                             pbc[:]).then_inc(c.sV.h, 1)

        @blk.scalar
        def _(s):
            for i in range(4):
                s.wait_ge(c.sV.h, vK(i, 1))
                if i >= 1:
                    s.wait_ge(c.sP.h, peK(i - 1, 2))
                s.activation(sqb[:], kf[:, i * TOK:(i + 1) * TOK],
                             AF.Square).then_inc(c.sS.h, 1)
                s.wait_ge(c.sV.h, vK(i, 2))
                s.activation(row[:], row[:], AF.Ln, scale=1.0 / 128.0,
                             bias=kc["eps_a"][0:1, :])
                s.activation(row[:], row[:], AF.Exp,
                             scale=-0.5).then_inc(c.sS.h, 1)
            for h in range(16):
                s.wait_ge(c.sV.h, vQ(h, 1))
                if h >= 1:
                    s.wait_ge(c.sP.h, peQ(h - 1, 2))
                elif True:
                    s.wait_ge(c.sP.h, peK(3, 2))
                s.activation(sqb[:], qt[:, (h % 2) * TOK:(h % 2 + 1) * TOK],
                             AF.Square).then_inc(c.sS.h, 1)
                s.wait_ge(c.sV.h, vQ(h, 2))
                s.activation(row[:], row[:], AF.Ln, scale=1.0 / 128.0,
                             bias=kc["eps_a"][0:1, :])
                s.activation(row[:], row[:], AF.Exp,
                             scale=-0.5).then_inc(c.sS.h, 1)

        @blk.sync
        def _(s):
            for h in range(16):
                s.wait_ge(c.sV.h, vQ(h, 3))
                s.dma_start(out=qscr[h],
                            in_=qt[:, (h % 2) * TOK:(h % 2 + 1) * TOK]
                            ).then_inc(c.sW.h, 16)
        c.barrier()
        _stkA.close()

    with (
        nc.sbuf_tensor(uname("at_KF"), [128, 4 * 4096], BF16) as KF,
        nc.sbuf_tensor(uname("at_VF"), [128, 32 * 512], BF16) as VF,
        nc.sbuf_tensor(uname("at_qg"), [128, 8 * TOK], BF16) as qg,
        nc.sbuf_tensor(uname("at_OT"), [128, 8 * TOK], BF16) as OT,
        nc.sbuf_tensor(uname("at_PT"), [128, 2 * TOK], BF16) as PT,
        nc.sbuf_tensor(uname("at_bcs"), [128, TOK], F32) as bcs,
        nc.sbuf_tensor(uname("at_row2"), [1, TOK], F32) as row2,
        nc.sbuf_tensor(uname("at_wo0"), [128, 8 * 128], BF16) as wo0,
        nc.sbuf_tensor(uname("at_wo1"), [128, 8 * 128], BF16) as wo1,
        nc.psum_tensor(uname("at_st0"), [128, TOK], F32) as st0,
        nc.psum_tensor(uname("at_st1"), [128, TOK], F32) as st1,
        nc.psum_tensor(uname("at_po"), [128, TOK], F32) as po,
        nc.psum_tensor(uname("at_ps2"), [1, TOK], F32) as ps2,
    ):
        wo_slots = (wo0, wo1)
        kv_b = c.sW.reserve(16 * 8)

        @blk.sync
        def _(s):
            s.wait_ge(c.sC.h, c.sC.n)
            for m in range(4):
                s.dma_start(
                    out=KF[:].rearrange("p (i t) -> p i t", t=4096)
                    [:, :, m * TOK:(m + 1) * TOK],
                    in_=kv_out[m * 1024:m * 1024 + 512, :]
                    .rearrange("(i p) t -> p i t", p=128)).then_inc(c.sW.h, 16)
                s.dma_start(
                    out=VF[:].rearrange("p (a f) -> p a f", f=512)
                    [:, m * 8:(m + 1) * 8, :],
                    in_=kv_out[m * 1024 + 512:(m + 1) * 1024, :]
                    .rearrange("(a r) (h f) -> (r h) a f", a=8, h=2)
                    ).then_inc(c.sW.h, 16)

        for grp in range(2):
            qb2 = c.sW.reserve(16)
            pe_now = c.sP.n

            @blk.sync
            def _(s):
                s.wait_ge(c.sP.h, pe_now)
                s.dma_start(out=qg[:].rearrange("p (h t) -> p h t", t=TOK),
                            in_=qscr[grp * 8:(grp + 1) * 8]
                            .rearrange("h p t -> p h t")
                            ).then_inc(c.sW.h, 16)

            pe_b2 = c.sP.reserve(8 * 65)
            s_b2 = c.sS.reserve(8 * 32)
            v_b2 = c.sV.reserve(8 * 4)

            def peH(hl, j):
                return pe_b2 + 65 * hl + j

            def sH(hl, j):
                return s_b2 + 32 * hl + j

            def vH(hl, j):
                return v_b2 + 4 * hl + j

            @blk.tensor
            def _(t):
                t.wait_ge(c.sW.h, kv_b + 16 * 8)
                t.wait_ge(c.sW.h, qb2 + 16)
                t.wait_ge(c.sV.h, v_b2)
                for hl in range(8):
                    H = grp * 8 + hl
                    kvh = H // 4
                    KFh = KF[:].rearrange("p (i t) -> p i t", t=4096)
                    if hl >= 1:
                        t.wait_ge(c.sV.h, vH(hl - 1, 4))
                    for kt in range(32):
                        st = (st0, st1)[kt % 2]
                        if kt >= 2:
                            t.wait_ge(c.sS.h, sH(hl, kt - 1))
                        elif hl >= 1:
                            t.wait_ge(c.sS.h, sH(hl - 1, 31 + kt))
                        for hh in range(2):
                            inst = t.matmul(
                                st[:, hh * 512:(hh + 1) * 512],
                                KFh[:, kvh, kt * 128:(kt + 1) * 128],
                                qg[:, hl * TOK + hh * 512:hl * TOK + (hh + 1) * 512],
                                start=True, stop=True)
                        inst.then_inc(c.sP.h, 1)
                        t.wait_ge(c.sS.h, sH(hl, kt + 1))
                        VFt = VF[:].rearrange("p (a f) -> p a f", f=512)
                        for hh in range(2):
                            t.matmul(
                                po[:, hh * 512:(hh + 1) * 512],
                                VFt[:, kt, kvh * 128:(kvh + 1) * 128],
                                PT[:, (kt % 2) * TOK + hh * 512:
                                   (kt % 2) * TOK + (hh + 1) * 512],
                                start=(kt == 0), stop=(kt == 31))
                        for hh in range(2):
                            inst = t.matmul(
                                ps2[0:1, hh * 512:(hh + 1) * 512],
                                kc["ones_col_bf16"][:],
                                PT[:, (kt % 2) * TOK + hh * 512:
                                   (kt % 2) * TOK + (hh + 1) * 512],
                                start=(kt == 0), stop=(kt == 31))
                        inst.then_inc(c.sP.h, 1)
                    t.wait_ge(c.sV.h, vH(hl, 2))
                    for hh in range(2):
                        inst = t.matmul(st0[:, hh * 512:(hh + 1) * 512],
                                        kc["ones_row_f32"][:],
                                        row2[0:1, hh * 512:(hh + 1) * 512],
                                        start=True, stop=True)
                    inst.then_inc(c.sP.h, 1)

            @blk.scalar
            def _(s):
                for hl in range(8):
                    for kt in range(32):
                        s.wait_ge(c.sP.h, peH(hl, 2 * kt + 1))
                        st = (st0, st1)[kt % 2]
                        s.activation(PT[:, (kt % 2) * TOK:(kt % 2 + 1) * TOK],
                                     st[:], AF.Exp,
                                     scale=float(1.0 / np.sqrt(128.0))
                                     ).then_inc(c.sS.h, 1)

            @blk.vector
            def _(v):
                for hl in range(8):
                    v.wait_ge(c.sP.h, peH(hl, 64))
                    v.tensor_copy(row2[:], ps2[:]).then_inc(c.sV.h, 1)
                    v.reciprocal(row2[:], row2[:]).then_inc(c.sV.h, 1)
                    v.wait_ge(c.sP.h, peH(hl, 65))
                    v.tensor_copy(bcs[:], st0[:]).then_inc(c.sV.h, 1)
                    v.tensor_mul(OT[:, hl * TOK:(hl + 1) * TOK], po[:],
                                 bcs[:]).then_inc(c.sV.h, 1)

            ob = c.sG.reserve(16 * 16)
            pe_b3 = c.sP.reserve(16)
            v_b3 = c.sV.reserve(16)

            @blk.gpsimd
            def _(g):
                for ot in range(16):
                    if ot >= 2:
                        g.wait_ge(c.sP.h, pe_b3 + ot - 1)
                    g.dma_start(
                        out=wo_slots[ot % 2][:].rearrange("p (a b) -> p a b",
                                                          b=128),
                        in_=io["L0_attn_Wo"][grp * 1024:(grp + 1) * 1024,
                                             ot * 128:(ot + 1) * 128]
                        .rearrange("(c p) o -> p c o", p=128)
                        ).then_inc(c.sG.h, 16)

            @blk.tensor
            def _(t):
                t.wait_ge(c.sV.h, v_b2 + 32)
                for ot in range(16):
                    st = (st0, st1)[ot % 2]
                    t.wait_ge(c.sG.h, ob + 16 * (ot + 1))
                    if ot >= 2:
                        t.wait_ge(c.sV.h, v_b3 + ot - 1)
                    for ch in range(8):
                        for hh in range(2):
                            inst = t.matmul(
                                st[:, hh * 512:(hh + 1) * 512],
                                wo_slots[ot % 2][:, ch * 128:(ch + 1) * 128],
                                OT[:, ch * TOK + hh * 512:ch * TOK + (hh + 1) * 512],
                                start=(ch == 0), stop=(ch == 7))
                    inst.then_inc(c.sP.h, 1)

            @blk.vector
            def _(v):
                for ot in range(16):
                    v.wait_ge(c.sP.h, pe_b3 + ot + 1)
                    st = (st0, st1)[ot % 2]
                    v.tensor_add(res[:, ot * TOK:(ot + 1) * TOK],
                                 res[:, ot * TOK:(ot + 1) * TOK],
                                 st[:]).then_inc(c.sV.h, 1)
            c.barrier()
    c.barrier()


def emit_l1ffn(c, ln, wg_slot, wu_slot, wd_slot, G, FUSE, sdh):
    nc, blk = c.nc, c.blk
    io, kc, res = c.ios, c.bufs["KC"], c.bufs["RES"]
    NS = 16

    emit_ln(c, 0, 512, ln, ln_off=0, ln_stride=1024)
    emit_ln(c, 512, 512, ln, ln_off=512, ln_stride=1024)

    ps_ctx = (
        nc.psum_tensor(uname("f1_pg0"), [128, 1024], F32),
        nc.psum_tensor(uname("f1_pg1"), [128, 1024], F32),
        nc.psum_tensor(uname("f1_pu0"), [128, 1024], F32),
        nc.psum_tensor(uname("f1_pu1"), [128, 1024], F32),
    )
    c.ps_pg0 = c.stack2_enter(ps_ctx[0]) if False else None
    import contextlib
    _stk = contextlib.ExitStack()
    c.ps_pg0 = _stk.enter_context(ps_ctx[0])
    c.ps_pg1 = _stk.enter_context(ps_ctx[1])
    c.ps_pu0 = _stk.enter_context(ps_ctx[2])
    c.ps_pu1 = _stk.enter_context(ps_ctx[3])

    vb0 = c.sV.reserve(1)

    @blk.vector
    def _(v):
        v.tensor_scalar(sdh[:], kc["L1_ffn_s_down"][:], 0.5, None,
                        ALU.mult).then_inc(c.sV.h, 1)

    gbase = c.sG.reserve(16 * 3 * NS)
    pe_base = c.sP.reserve(24 * NS)
    sc_base = c.sS.reserve(4 * NS)
    v_base = c.sV.reserve(20 * NS)

    def pe_slice(j):
        return pe_base + 24 * (j + 1)

    def pe_gateup(j, k):
        return pe_base + 24 * j + k

    def pe_down(j, ot):
        return pe_base + 24 * j + 8 + ot + 1

    def v_slice(j):
        return v_base + 20 * (j + 1)

    def v_fuse(j, k):
        return v_base + 20 * j + k

    def v_down(j, ot):
        return v_base + 20 * j + 4 + ot + 1

    @blk.gpsimd
    def _(g):
        for j in range(NS):
            if j >= 2:
                g.wait_ge(c.sP.h, pe_slice(j - 2))
            g.dma_start(
                out=wg_slot[j % 2][:].rearrange("p (a b) -> p a b", b=512),
                in_=io["L1_ffn_w_gate"][:, j * 512:(j + 1) * 512]
                .rearrange("(c p) o -> p c o", p=128)).then_inc(c.sG.h, 16)
            g.dma_start(
                out=wu_slot[j % 2][:].rearrange("p (a b) -> p a b", b=512),
                in_=io["L1_ffn_w_up"][:, j * 512:(j + 1) * 512]
                .rearrange("(c p) o -> p c o", p=128)).then_inc(c.sG.h, 16)
            if j >= 1:
                g.wait_ge(c.sP.h, pe_slice(j - 1))
            g.dma_start(
                out=wd_slot[:].rearrange("p (a b) -> p a b", b=2048),
                in_=io["L1_ffn_w_down"][j * 512:(j + 1) * 512, :]
                .rearrange("(c p) o -> p c o", p=128)).then_inc(c.sG.h, 16)

    @blk.tensor
    def _(t):
        t.wait_ge(c.sV.h, vb0 + 1)
        for j in range(NS):
            wg, wu = wg_slot[j % 2], wu_slot[j % 2]
            t.wait_ge(c.sG.h, gbase + 16 * (3 * j + 2))
            if j >= 1:
                t.wait_ge(c.sV.h, v_slice(j - 1))
            for pt in range(4):
                pg = (c.ps_pg0, c.ps_pg1)[pt % 2]
                pu = (c.ps_pu0, c.ps_pu1)[pt % 2]
                if pt >= 2:
                    t.wait_ge(c.sS.h, sc_base + 4 * j + pt - 1)
                    t.wait_ge(c.sV.h, v_fuse(j, pt - 1))
                for ch in range(NCH):
                    for hh in range(2):
                        inst = t.matmul(
                            pg[:, hh * 512:(hh + 1) * 512],
                            wg[:, ch * 512 + pt * 128:ch * 512 + pt * 128 + 128],
                            ln[:, ch * TOK + hh * 512:ch * TOK + (hh + 1) * 512],
                            start=(ch == 0), stop=(ch == NCH - 1))
                inst.then_inc(c.sP.h, 1)
                for ch in range(NCH):
                    for hh in range(2):
                        inst = t.matmul(
                            pu[:, hh * 512:(hh + 1) * 512],
                            wu[:, ch * 512 + pt * 128:ch * 512 + pt * 128 + 128],
                            ln[:, ch * TOK + hh * 512:ch * TOK + (hh + 1) * 512],
                            start=(ch == 0), stop=(ch == NCH - 1))
                inst.then_inc(c.sP.h, 1)
            t.wait_ge(c.sG.h, gbase + 16 * (3 * j + 3))
            t.wait_ge(c.sV.h, v_fuse(j, 4))
            for ot in range(NCH):
                pd = (c.ps_pg0, c.ps_pg1)[ot % 2]
                if ot >= 2:
                    t.wait_ge(c.sV.h, v_down(j, ot - 2))
                for ic in range(4):
                    for hh in range(2):
                        inst = t.matmul(
                            pd[:, hh * 512:(hh + 1) * 512],
                            wd_slot[:, ic * 2048 + ot * 128:
                                    ic * 2048 + ot * 128 + 128],
                            FUSE[:, ic * TOK + hh * 512:
                                 ic * TOK + (hh + 1) * 512],
                            start=(ic == 0), stop=(ic == 3))
                inst.then_inc(c.sP.h, 1)

    @blk.scalar
    def _(s):
        for j in range(NS):
            for pt in range(4):
                s.wait_ge(c.sP.h, pe_gateup(j, 2 * pt + 1))
                k = 4 * j + pt
                if k >= 2:
                    jj, pp = divmod(k - 2, 4)
                    s.wait_ge(c.sV.h, v_fuse(jj, pp + 1))
                pg = (c.ps_pg0, c.ps_pg1)[pt % 2]
                s.activation(G[:, (pt % 2) * TOK:(pt % 2 + 1) * TOK], pg[:],
                             AF.Silu,
                             scale=kc["L1_ffn_s_gate"][:, k:k + 1]
                             ).then_inc(c.sS.h, 1)

    @blk.vector
    def _(v):
        for j in range(NS):
            for pt in range(4):
                v.wait_ge(c.sP.h, pe_gateup(j, 2 * pt + 2))
                v.wait_ge(c.sS.h, sc_base + 4 * j + pt + 1)
                pu = (c.ps_pu0, c.ps_pu1)[pt % 2]
                v.scalar_tensor_tensor(
                    FUSE[:, pt * TOK:(pt + 1) * TOK], pu[:],
                    kc["L1_ffn_s_up"][:, 4 * j + pt:4 * j + pt + 1],
                    G[:, (pt % 2) * TOK:(pt % 2 + 1) * TOK],
                    ALU.mult, ALU.mult).then_inc(c.sV.h, 1)
            for ot in range(NCH):
                v.wait_ge(c.sP.h, pe_down(j, ot))
                pd = (c.ps_pg0, c.ps_pg1)[ot % 2]
                v.scalar_tensor_tensor(
                    res[:, ot * TOK:(ot + 1) * TOK], pd[:],
                    sdh[:, ot:ot + 1],
                    res[:, ot * TOK:(ot + 1) * TOK],
                    ALU.mult, ALU.add).then_inc(c.sV.h, 1)
    c.barrier()
    _stk.close()




class Chain:

    def __init__(self, c):
        self.c = c
        self.tok = {"P": 0, "V": 0, "S": 0, "G": 0, "W": 0}
        self.seen = {k: {} for k in self.tok}

    def _waits(self, e, skip):
        c = self.c
        sems = {"P": c.sP, "V": c.sV, "S": c.sS, "G": c.sG, "W": c.sW}
        seen = self.seen[skip]
        for k, v in self.tok.items():
            if k != skip and v and seen.get(k, 0) < v:
                e.wait_ge(sems[k].h, v)
                seen[k] = v

    def p(self, fn):
        c = self.c

        def sec(t):
            self._waits(t, "P")
            inst = fn(t)
            inst.then_inc(c.sP.h, 1)
        c.blk.tensor(sec)
        c.sP.n += 1
        self.tok["P"] = c.sP.n

    def v(self, fn):
        c = self.c

        def sec(vv):
            self._waits(vv, "V")
            inst = fn(vv)
            inst.then_inc(c.sV.h, 1)
        c.blk.vector(sec)
        c.sV.n += 1
        self.tok["V"] = c.sV.n

    def s(self, fn):
        c = self.c

        def sec(ss):
            self._waits(ss, "S")
            inst = fn(ss)
            inst.then_inc(c.sS.h, 1)
        c.blk.scalar(sec)
        c.sS.n += 1
        self.tok["S"] = c.sS.n

    def g(self, fn, ndma=1):
        c = self.c

        def sec(gg):
            self._waits(gg, "G")
            inst = fn(gg)
            inst.then_inc(c.sG.h, 16)
        c.blk.gpsimd(sec)
        c.sG.n += 16 * ndma
        self.tok["G"] = c.sG.n

    def w(self, fn, ndma=1):
        c = self.c

        def sec(ss):
            self._waits(ss, "W")
            inst = fn(ss)
            inst.then_inc(c.sW.h, 16)
        c.blk.sync(sec)
        c.sW.n += 16 * ndma
        self.tok["W"] = c.sW.n


def emit_l1attn(c):
    nc, blk = c.nc, c.blk
    io, kc, res = c.ios, c.bufs["KC"], c.bufs["RES"]
    SEQ = 4096
    QS = float(128.0 ** -0.5)

    with (
        nc.sbuf_tensor(uname("la_ln"), [128, NCH * TOK], BF16) as ln,
        nc.sbuf_tensor(uname("la_w0"), [128, NCH * 128], BF16) as w0,
        nc.sbuf_tensor(uname("la_w1"), [128, NCH * 128], BF16) as w1,
        nc.sbuf_tensor(uname("la_mx0"), [128, TOK], BF16) as mx0,
        nc.sbuf_tensor(uname("la_mx1"), [128, TOK], BF16) as mx1,
        nc.sbuf_tensor(uname("la_ba"), [128, TOK], F32) as bav,
    ):
        emit_ln(c, 0, 512, ln, ln_off=0, ln_stride=TOK)
        emit_ln(c, 512, 512, ln, ln_off=512, ln_stride=TOK)
        wsl, msl = (w0, w1), (mx0, mx1)

        with (
            nc.psum_tensor(uname("la_pm0"), [128, TOK], F32) as pm0,
            nc.psum_tensor(uname("la_pm1"), [128, TOK], F32) as pm1,
            nc.psum_tensor(uname("la_pba"), [64, TOK], F32) as pba,
        ):
            psl = (pm0, pm1)
            gb = c.sG.reserve(16 * 97)
            pe_b = c.sP.reserve(97)
            v_b = c.sV.reserve(97)
            sw_b = c.sW.reserve(16 * 97)
            wq = io["L1_attn_W_qkvz"]

            def chcols(ch):
                if ch < 64:
                    hk, s_ = ch // 4, ch % 4
                    return 768 * hk + (s_ * 128 if s_ < 2
                                       else 256 + (s_ - 2) * 128)
                zz = ch - 64
                return 768 * (zz // 2) + 512 + (zz % 2) * 128

            @blk.gpsimd
            def _(g):
                for ch in range(96):
                    if ch >= 2:
                        g.wait_ge(c.sP.h, pe_b + ch - 1)
                    g.dma_start(
                        out=wsl[ch % 2][:].rearrange("p (a b) -> p a b", b=128),
                        in_=wq[:, chcols(ch):chcols(ch) + 128]
                        .rearrange("(cc p) o -> p cc o", p=128)
                        ).then_inc(c.sG.h, 16)
                g.wait_ge(c.sP.h, pe_b + 96)
                g.dma_start(
                    out=wsl[0][:, 0:16 * 64].rearrange("p (a b) -> p a b",
                                                       b=64),
                    in_=io["L1_attn_W_ba"]
                    .rearrange("(cc p) o -> p cc o", p=128)
                    ).then_inc(c.sG.h, 16)

            @blk.tensor
            def _(t):
                for ch in range(96):
                    pm = psl[ch % 2]
                    t.wait_ge(c.sG.h, gb + 16 * (ch + 1))
                    if ch >= 2:
                        t.wait_ge(c.sV.h, v_b + ch - 1)
                    for cc in range(NCH):
                        for hh in range(2):
                            inst = t.matmul(
                                pm[:, hh * 512:(hh + 1) * 512],
                                wsl[ch % 2][:, cc * 128:(cc + 1) * 128],
                                ln[:, cc * TOK + hh * 512:
                                   cc * TOK + (hh + 1) * 512],
                                start=(cc == 0), stop=(cc == NCH - 1))
                    inst.then_inc(c.sP.h, 1)
                t.wait_ge(c.sG.h, gb + 16 * 97)
                t.wait_ge(c.sV.h, v_b + 95)
                for cc in range(NCH):
                    for hh in range(2):
                        inst = t.matmul(
                            pba[:, hh * 512:(hh + 1) * 512],
                            wsl[0][:, cc * 64:(cc + 1) * 64],
                            ln[:, cc * TOK + hh * 512:
                               cc * TOK + (hh + 1) * 512],
                            start=(cc == 0), stop=(cc == NCH - 1))
                inst.then_inc(c.sP.h, 1)

            @blk.vector
            def _(v):
                for ch in range(96):
                    v.wait_ge(c.sP.h, pe_b + ch + 1)
                    if ch >= 2:
                        v.wait_ge(c.sW.h, sw_b + 16 * (ch - 1))
                    v.tensor_copy(msl[ch % 2][:],
                                  psl[ch % 2][:]).then_inc(c.sV.h, 1)
                v.wait_ge(c.sP.h, pe_b + 97)
                v.tensor_copy(bav[0:64, :], pba[:]).then_inc(c.sV.h, 1)

            @blk.sync
            def _(s):
                for ch in range(96):
                    s.wait_ge(c.sV.h, v_b + ch + 1)
                    s.dma_start(
                        out=io["cc1_in"][ch * 64:(ch + 1) * 64, :]
                        .rearrange("r (a k) -> (r a) k", a=2),
                        in_=msl[ch % 2][:]).then_inc(c.sW.h, 16)
                s.wait_ge(c.sV.h, v_b + 97)
                s.dma_start(out=io["ba_scr"], in_=bav[0:64, :]
                            ).then_inc(c.sW.h, 16)
        c.barrier()

        with (
            nc.sbuf_tensor(uname("la_bg"), [64, TOK], F32) as bg,
            nc.sbuf_tensor(uname("la_na"), [32, 1], F32) as negA,
        ):
            ch0 = Chain(c)
            def split_ba(s):
                for h in range(16):
                    inst = s.dma_start(out=bg[2 * h:2 * h + 2, :],
                                       in_=io["ba_scr"][4 * h:4 * h + 2, :])
                    if h < 15:
                        inst.then_inc(c.sW.h, 16)
                return inst
            ch0.w(split_ba, ndma=16)

            def split_ba2(s):
                for h in range(16):
                    inst = s.dma_start(
                        out=bg[32 + 2 * h:32 + 2 * h + 2, :],
                        in_=io["ba_scr"][4 * h + 2:4 * h + 4, :])
                    if h < 15:
                        inst.then_inc(c.sW.h, 16)
                return inst
            ch0.w(split_ba2, ndma=16)
            ch0.s(lambda s: s.activation(negA[:], kc["L1_attn_A_log"][:],
                                         AF.Exp))
            ch0.v(lambda v: v.tensor_scalar(negA[:], negA[:], -1.0, None,
                                            ALU.mult))
            ch0.s(lambda s: s.activation(bg[0:32, :], bg[0:32, :], AF.Exp,
                                         scale=-1.0))
            ch0.v(lambda v: v.tensor_scalar(bg[0:32, :], bg[0:32, :], 1.0,
                                            None, ALU.add))
            ch0.v(lambda v: v.reciprocal(bg[0:32, :], bg[0:32, :]))
            ch0.v(lambda v: v.tensor_scalar(bg[32:64, :], bg[32:64, :],
                                            kc["L1_attn_dt_bias"][:], None,
                                            ALU.add))
            ch0.s(lambda s: s.activation(bav[32:64, :], bg[32:64, :],
                                         AF.Abs))
            ch0.s(lambda s: s.activation(bav[32:64, :], bav[32:64, :],
                                         AF.Exp, scale=-1.0))
            ch0.s(lambda s: s.activation(bav[32:64, :], bav[32:64, :], AF.Ln,
                                         bias=kc["one_col"][0:32, :]))
            ch0.v(lambda v: v.tensor_scalar_max(bg[32:64, :], bg[32:64, :],
                                                0.0))
            ch0.v(lambda v: v.tensor_add(bg[32:64, :], bg[32:64, :],
                                         bav[32:64, :]))
            ch0.v(lambda v: v.tensor_scalar(bg[32:64, :], bg[32:64, :],
                                            negA[0:32, :], None, ALU.mult))
            ch0.w(lambda s: s.dma_start(out=io["ccbg_in"], in_=bg[0:64, :]))
        c.barrier()

    ccb = c.sC.reserve(2)

    @blk.gpsimd
    def _(g):
        g.collective_compute(
            "AllGather", ALU.bypass, replica_groups=[list(range(8))],
            ins=[io["cc1_in"]], outs=[io["cc1_out"]]).then_inc(c.sC.h, 1)
        g.collective_compute(
            "AllGather", ALU.bypass, replica_groups=[list(range(8))],
            ins=[io["ccbg_in"]], outs=[io["ccbg_out"]]).then_inc(c.sC.h, 1)
    c.barrier()

    cc1f = io["cc1_out"].rearrange("r (a k) -> (r a) k", a=2)
    ccbgf = io["ccbg_out"]
    cwf = io["L1_attn_conv_w"]
    bascrf = io["ba_scr"].rearrange("r (k one) -> (r k) one", one=1)

    import contextlib as _ctl2
    _stk2 = _ctl2.ExitStack()
    QN = _stk2.enter_context(nc.sbuf_tensor(uname("lb_QN"), [128, SEQ], BF16))
    KN = _stk2.enter_context(nc.sbuf_tensor(uname("lb_KN"), [128, SEQ], BF16))
    V0 = _stk2.enter_context(nc.sbuf_tensor(uname("lb_V0"), [128, SEQ], BF16))
    V1 = _stk2.enter_context(nc.sbuf_tensor(uname("lb_V1"), [128, SEQ], BF16))
    Z0 = _stk2.enter_context(nc.sbuf_tensor(uname("lb_Z0"), [128, SEQ], BF16))
    Z1 = _stk2.enter_context(nc.sbuf_tensor(uname("lb_Z1"), [128, SEQ], BF16))
    OG0 = _stk2.enter_context(nc.sbuf_tensor(uname("lb_OG0"), [128, SEQ], BF16))
    OG1 = _stk2.enter_context(nc.sbuf_tensor(uname("lb_OG1"), [128, SEQ], BF16))
    PAD = _stk2.enter_context(nc.sbuf_tensor(uname("lb_PAD"), [128, SEQ + 3], BF16))
    C32 = _stk2.enter_context(nc.sbuf_tensor(uname("lb_C32"), [128, 512], F32))
    CB = _stk2.enter_context(nc.sbuf_tensor(uname("lb_CB"), [128, 512], F32))
    SQB = _stk2.enter_context(nc.sbuf_tensor(uname("lb_SQ"), [128, 1024], BF16))
    GROW = _stk2.enter_context(nc.sbuf_tensor(uname("lb_GROW"), [1, SEQ], F32))
    BROW = _stk2.enter_context(nc.sbuf_tensor(uname("lb_BROW"), [1, SEQ], F32))
    TROW = _stk2.enter_context(nc.sbuf_tensor(uname("lb_TROW"), [1, 1024], F32))
    CW = _stk2.enter_context(nc.sbuf_tensor(uname("lb_CW"), [128, 4], F32))
    T32 = _stk2.enter_context(nc.sbuf_tensor(uname("lb_T32"), [32, 128], F32))
    GT = _stk2.enter_context(nc.sbuf_tensor(uname("lb_GT"), [128, 2 * 6 * 32], F32))
    NWB = _stk2.enter_context(nc.sbuf_tensor(uname("lb_NWB"), [128, 128], F32))
    SC = _stk2.enter_context(nc.sbuf_tensor(uname("lb_SC"), [128, 2 * 2048], BF16))
    SBF = _stk2.enter_context(nc.sbuf_tensor(uname("lb_SBF"), [128, 2 * 128], BF16))
    S32 = _stk2.enter_context(nc.sbuf_tensor(uname("lb_S32"), [128, 2 * 128], F32))
    O32 = _stk2.enter_context(nc.sbuf_tensor(uname("lb_O32"), [128, 2 * 128], F32))
    ZSC = _stk2.enter_context(nc.sbuf_tensor(uname("lb_ZSC"), [128, 2 * 128], F32))
    R1 = _stk2.enter_context(nc.sbuf_tensor(uname("lb_R1"), [128, 4], F32))
    ROW4 = _stk2.enter_context(nc.sbuf_tensor(uname("lb_R4"), [4, 1024], F32))
    pT0 = _stk2.enter_context(nc.psum_tensor(uname("lb_pT0"), [128, 640], BF16))
    pT1 = _stk2.enter_context(nc.psum_tensor(uname("lb_pT1"), [128, 640], BF16))
    pA0 = _stk2.enter_context(nc.psum_tensor(uname("lb_pA0"), [128, 512], F32))
    pA1 = _stk2.enter_context(nc.psum_tensor(uname("lb_pA1"), [128, 512], F32))
    pB0 = _stk2.enter_context(nc.psum_tensor(uname("lb_pB0"), [128, 512], F32))
    pB1 = _stk2.enter_context(nc.psum_tensor(uname("lb_pB1"), [128, 512], F32))
    if True:

        pTs, pAs, pBs = (pT0, pT1), (pA0, pA1), (pB0, pB1)
        Vs, Zs, OGs = (V0, V1), (Z0, Z1), (OG0, OG1)
        GTW = 6 * 32

        def gtc(s_, which, n):
            o = s_ * GTW + which * 32 + n
            return GT[:, o:o + 1]

        def gta(s_, which):
            o = s_ * GTW + which * 32
            return GT[:, o:o + 32]

        def sc(s_, slot, w=128):
            o = s_ * 2048 + slot * 128
            return SC[:, o:o + w]

        gidx1, gidx2, gidx3 = kc["g1idx"], kc["g2idx"], kc["g3idx"]

        chN = Chain(c)
        chN.p(lambda t: t.matmul(pA0[:, 0:128], kc["ones_row_f32"][:],
                                 kc["L1_attn_norm_w"][:], start=True,
                                 stop=True))
        chN.v(lambda v: v.tensor_copy(NWB[:], pA0[:, 0:128]))
        chN.v(lambda v: v.memset(PAD[:, 0:3], 0.0))
        c.barrier()

        def load_piece(ch, colbase, cwcol, dst):
            if cwcol is not None:
                ch.g(lambda g: g.indirect_dma_start(
                    out=CW[:], out_offset=None, in_=cwf,
                    in_offset=IndirectOffsetOnAxis(
                        gidx3[:, cwcol:cwcol + 1], 0)))
            for m in range(4):
                ch.g(lambda g, m=m: g.indirect_dma_start(
                    out=dst[:, m * 1024:(m + 1) * 1024], out_offset=None,
                    in_=cc1f, in_offset=IndirectOffsetOnAxis(
                        gidx1[:, colbase + m:colbase + m + 1], 0)))

        def conv_silu(ch, dst):
            W_ = 512
            for i in range(SEQ // W_):
                o = 3 + i * W_

                def cv(v, o=o):
                    v.tensor_scalar(C32[:], PAD[:, o:o + W_], CW[:, 3:4],
                                    None, ALU.mult)
                    v.scalar_tensor_tensor(C32[:], PAD[:, o - 1:o - 1 + W_],
                                           CW[:, 2:3], C32[:], ALU.mult,
                                           ALU.add)
                    v.scalar_tensor_tensor(C32[:], PAD[:, o - 2:o - 2 + W_],
                                           CW[:, 1:2], C32[:], ALU.mult,
                                           ALU.add)
                    return v.scalar_tensor_tensor(
                        C32[:], PAD[:, o - 3:o - 3 + W_], CW[:, 0:1], C32[:],
                        ALU.mult, ALU.add)
                ch.v(cv)
                ch.s(lambda s: s.activation(CB[:], C32[:], AF.Exp,
                                            scale=-1.0))
                ch.v(lambda v: v.tensor_scalar(CB[:], CB[:], 1.0, None,
                                               ALU.add))
                ch.v(lambda v: v.reciprocal(CB[:], CB[:]))
                ch.v(lambda v, i=i: v.tensor_mul(
                    dst[:, i * W_:(i + 1) * W_], C32[:], CB[:]))

        def l2norm_piece(ch, dst, qscale):
            for q4 in range(4):
                ch.s(lambda s, q4=q4: s.activation(
                    SQB[:], dst[:, q4 * 1024:(q4 + 1) * 1024], AF.Square))

                def red(t, q4=q4):
                    t.matmul(pA0[0:1, 0:512], kc["ones_col_bf16"][:],
                             SQB[:, 0:512], start=True, stop=True)
                    return t.matmul(pB0[0:1, 0:512], kc["ones_col_bf16"][:],
                                    SQB[:, 512:1024], start=True, stop=True)
                ch.p(red)
                ch.v(lambda v: v.tensor_copy(TROW[0:1, 0:512],
                                             pA0[0:1, 0:512]))
                ch.v(lambda v: v.tensor_copy(TROW[0:1, 512:1024],
                                             pB0[0:1, 0:512]))
                ch.s(lambda s: s.activation(TROW[0:1, 0:1024],
                                            TROW[0:1, 0:1024], AF.Ln,
                                            bias=kc["eps_a"][0:1, :]))
                ch.s(lambda s: s.activation(TROW[0:1, 0:1024],
                                            TROW[0:1, 0:1024], AF.Exp,
                                            scale=-0.5))
                if qscale:
                    ch.v(lambda v: v.tensor_scalar(TROW[0:1, 0:1024],
                                                   TROW[0:1, 0:1024], QS,
                                                   None, ALU.mult))
                for j in range(2):
                    ch.p(lambda t, j=j: t.matmul(
                        pB0[:, 0:512], kc["ones_row_f32"][:],
                        TROW[0:1, j * 512:(j + 1) * 512], start=True,
                        stop=True))
                    ch.v(lambda v, q4=q4, j=j: v.tensor_mul(
                        dst[:, q4 * 1024 + j * 512:q4 * 1024 + (j + 1) * 512],
                        dst[:, q4 * 1024 + j * 512:q4 * 1024 + (j + 1) * 512],
                        pB0[:, 0:512]))

        def unit_rows(ch, s_, u):
            ch.g(lambda g: g.indirect_dma_start(
                out=ROW4[:], out_offset=None, in_=ccbgf,
                in_offset=IndirectOffsetOnAxis(gidx2[:, 8 + u:9 + u], 0)))
            ch.w(lambda s: s.dma_start(out=io["ba_scr"][0:4, :],
                                       in_=ROW4[:]))
            ch.w(lambda s: s.dma_start(
                out=GROW.rearrange("o (r t) -> o r t", r=4),
                in_=io["ba_scr"][0:4, :]
                .rearrange("(one r) t -> one r t", one=1)))
            ch.g(lambda g: g.indirect_dma_start(
                out=ROW4[:], out_offset=None, in_=ccbgf,
                in_offset=IndirectOffsetOnAxis(gidx2[:, u:u + 1], 0)))
            ch.w(lambda s: s.dma_start(out=io["ba_scr"][4:8, :],
                                       in_=ROW4[:]))
            ch.w(lambda s: s.dma_start(
                out=BROW.rearrange("o (r t) -> o r t", r=4),
                in_=io["ba_scr"][4:8, :]
                .rearrange("(one r) t -> one r t", one=1)))

            def scan(v):
                for n in range(32):
                    inst = v.tensor_tensor_scan(
                        GROW[0:1, n * 128:(n + 1) * 128],
                        GROW[0:1, n * 128:(n + 1) * 128],
                        GROW[0:1, n * 128:(n + 1) * 128],
                        0.0, ALU.add, ALU.bypass)
                return inst
            ch.v(scan)
            ch.w(lambda s: s.dma_start(
                out=io["ba_scr"][0:4, :].rearrange("(one r) t -> one r t",
                                                   one=1),
                in_=GROW.rearrange("o (r t) -> o r t", r=4)))
            ch.w(lambda s: s.dma_start(
                out=io["ba_scr"][4:8, :].rearrange("(one r) t -> one r t",
                                                   one=1),
                in_=BROW.rearrange("o (r t) -> o r t", r=4)))
            ch.w(lambda s: s.dma_start(
                out=T32[:], in_=io["ba_scr"][0:4, :]
                .rearrange("(a1) (a2 b) -> (a1 a2) b", a2=8)))
            ch.p(lambda t: t.transpose(pAs[s_][:, 0:32], T32[:],
                                       kc["ident_f32"][0:32, 0:32]))
            ch.v(lambda v, s_=s_: v.tensor_copy(gta(s_, 0), pAs[s_][:, 0:32]))
            ch.w(lambda s: s.dma_start(
                out=T32[:], in_=io["ba_scr"][4:8, :]
                .rearrange("(a1) (a2 b) -> (a1 a2) b", a2=8)))
            ch.p(lambda t, s_=s_: t.transpose(pAs[s_][:, 0:32], T32[:],
                                              kc["ident_f32"][0:32, 0:32]))
            ch.v(lambda v, s_=s_: v.tensor_copy(gta(s_, 1), pAs[s_][:, 0:32]))
            ch.s(lambda s, s_=s_: s.activation(gta(s_, 2), gta(s_, 0),
                                               AF.Exp))
            ch.v(lambda v, s_=s_: v.tensor_mul(gta(s_, 4), gta(s_, 1),
                                               gta(s_, 2)))
            gv = GROW.rearrange("o (n t) -> o n t", t=128)
            ch.s(lambda s: s.activation(TROW[0:1, 0:32],
                                        gv[0:1, :, 127:128]
                                        .rearrange("o n t -> o (n t)"),
                                        AF.Exp))
            ch.v(lambda v: v.tensor_copy(TROW[0:1, 32:64],
                                         gv[0:1, :, 127:128]
                                         .rearrange("o n t -> o (n t)")))

            def bc_gl(t, s_=s_):
                t.matmul(pAs[s_][:, 0:32], kc["ones_row_f32"][:],
                         TROW[0:1, 0:32], start=True, stop=True)
                return t.matmul(pAs[s_][:, 32:64], kc["ones_row_f32"][:],
                                TROW[0:1, 32:64], start=True, stop=True)
            ch.p(bc_gl)
            ch.v(lambda v, s_=s_: v.tensor_copy(gta(s_, 5), pAs[s_][:, 0:32]))
            ch.v(lambda v, s_=s_: v.tensor_copy(gta(s_, 3),
                                                pAs[s_][:, 32:64]))
            ch.v(lambda v, s_=s_: v.memset(
                S32[:, s_ * 128:(s_ + 1) * 128], 0.0))
            ch.v(lambda v, s_=s_: v.memset(
                SBF[:, s_ * 128:(s_ + 1) * 128], 0.0))

        def unit_chunk(ch, s_, n):
            pA, pB, pT = pAs[s_], pBs[s_], pTs[s_]
            Vv, Zz, OGg = Vs[s_], Zs[s_], OGs[s_]
            nsl = slice(n * 128, (n + 1) * 128)
            Sb = SBF[:, s_ * 128:(s_ + 1) * 128]
            Sf = S32[:, s_ * 128:(s_ + 1) * 128]
            Of = O32[:, s_ * 128:(s_ + 1) * 128]
            Zf = ZSC[:, s_ * 128:(s_ + 1) * 128]
            r1 = R1[:, s_:s_ + 1]

            kd = R1[:, 2 + s_:3 + s_]
            Dm = Zf
            D2m = Of
            Dx = ZSC[:, (1 - s_) * 128:(2 - s_) * 128]

            ch.s(lambda s: s.activation(TROW[0:1, 0:128], GROW[0:1, nsl],
                                        AF.Exp))
            ch.v(lambda v: v.tensor_mul(TROW[0:1, 256:384], BROW[0:1, nsl],
                                        TROW[0:1, 0:128]))

            def bc4(t):
                t.matmul(pA[:, 0:128], kc["ones_row_f32"][:],
                         TROW[0:1, 0:128], start=True, stop=True)
                t.matmul(pA[:, 128:256], kc["ones_row_f32"][:],
                         BROW[0:1, nsl], start=True, stop=True)
                t.matmul(pA[:, 256:384], kc["ones_row_f32"][:],
                         TROW[0:1, 256:384], start=True, stop=True)
                return t.matmul(pA[:, 384:512], kc["ones_row_f32"][:],
                                GROW[0:1, nsl], start=True, stop=True)
            ch.p(bc4)
            ch.v(lambda v: v.tensor_mul(sc(s_, 9), QN[:, nsl], pA[:, 0:128]))
            ch.v(lambda v: v.tensor_mul(sc(s_, 10), KN[:, nsl],
                                        pA[:, 128:256]))
            ch.v(lambda v: v.tensor_mul(sc(s_, 11), KN[:, nsl],
                                        pA[:, 256:384]))
            ch.v(lambda v: v.tensor_scalar(Dm, pA[:, 384:512],
                                           gtc(s_, 0, n), None,
                                           ALU.subtract))
            ch.v(lambda v: v.tensor_scalar_max(D2m, Dm, 0.0))
            ch.v(lambda v: v.tensor_scalar(Dm, Dm, 0.0, None, ALU.min))
            ch.s(lambda s: s.activation(Dm, Dm, AF.Exp))
            ch.s(lambda s: s.activation(D2m, D2m, AF.Exp, scale=-1.0))
            ch.p(lambda t: t.transpose(pT[:, 0:128], KN[:, nsl],
                                       kc["ident_bf16"][:]))
            ch.v(lambda v: v.tensor_copy(sc(s_, 14), pT[:, 0:128]))
            ch.p(lambda t: t.transpose(pT[:, 128:256], Vv[:, nsl],
                                       kc["ident_bf16"][:]))
            ch.v(lambda v: v.tensor_scalar(sc(s_, 0), pT[:, 128:256],
                                           gtc(s_, 1, n), None, ALU.mult))
            ch.v(lambda v: v.tensor_scalar(sc(s_, 1), sc(s_, 14),
                                           gtc(s_, 4, n), None, ALU.mult))
            ch.v(lambda v: v.tensor_sub(kd, gtc(s_, 3, n), gtc(s_, 0, n)))
            ch.s(lambda s: s.activation(kd, kd, AF.Exp))
            ch.v(lambda v: v.tensor_scalar(sc(s_, 5), sc(s_, 14), kd,
                                           None, ALU.mult))
            ch.p(lambda t: t.matmul(pA[:, 384:512], KN[:, nsl], sc(s_, 10),
                                    start=True, stop=True))
            ch.v(lambda v: v.tensor_mul(Dx, pA[:, 384:512], Dm))
            ch.v(lambda v: v.tensor_mul(sc(s_, 2), Dx,
                                        kc["mask_su_neg"][:]))
            ch.p(lambda t: t.matmul(pB[:, 0:128], sc(s_, 10), KN[:, nsl],
                                    start=True, stop=True))
            ch.v(lambda v: v.tensor_mul(Dx, pB[:, 0:128], D2m))
            ch.v(lambda v: v.tensor_mul(sc(s_, 3), Dx,
                                        kc["mask_sl_neg"][:]))
            ch.p(lambda t: t.matmul(pB[:, 128:256], KN[:, nsl], QN[:, nsl],
                                    start=True, stop=True))
            ch.v(lambda v: v.tensor_mul(Dx, pB[:, 128:256], Dm))
            ch.v(lambda v: v.tensor_mul(sc(s_, 4), Dx,
                                        kc["mask_u"][:]))
            ch.v(lambda v: v.tensor_copy(sc(s_, 6), sc(s_, 3)))
            ch.v(lambda v: v.tensor_copy(sc(s_, 7), sc(s_, 2)))
            ch.p(lambda t: t.matmul(pA[:, 0:256], sc(s_, 2), sc(s_, 0, 256),
                                    start=True, stop=True))
            ch.v(lambda v: v.tensor_add(sc(s_, 0, 256), sc(s_, 0, 256),
                                        pA[:, 0:256]))
            for lv in range(6):
                def sq(t, lv=lv):
                    inst = t.matmul(pB[:, 256:384], sc(s_, 7), sc(s_, 6),
                                    start=True, stop=True)
                    if lv < 5:
                        inst = t.matmul(pB[:, 384:512], sc(s_, 6), sc(s_, 7),
                                        start=True, stop=True)
                    return inst
                ch.p(sq)
                ch.v(lambda v: v.tensor_copy(sc(s_, 6), pB[:, 256:384]))
                if lv < 5:
                    ch.v(lambda v: v.tensor_copy(sc(s_, 7), pB[:, 384:512]))
                    ch.p(lambda t: t.matmul(pA[:, 0:256], sc(s_, 7),
                                            sc(s_, 0, 256), start=True,
                                            stop=True))
                else:
                    ch.p(lambda t: t.matmul(pA[:, 0:256], sc(s_, 6),
                                            sc(s_, 0, 256), start=True,
                                            stop=True))
                ch.v(lambda v: v.tensor_add(sc(s_, 0, 256), sc(s_, 0, 256),
                                            pA[:, 0:256]))
            ch.p(lambda t: t.transpose(pT[:, 256:384], sc(s_, 1),
                                       kc["ident_bf16"][:]))
            ch.v(lambda v: v.tensor_copy(sc(s_, 8), pT[:, 256:384]))
            ch.p(lambda t: t.matmul(pA[:, 384:512], sc(s_, 8), Sb,
                                    start=True, stop=True))
            ch.v(lambda v: v.tensor_sub(sc(s_, 15), sc(s_, 0),
                                        pA[:, 384:512]))
            def omm(t):
                t.matmul(pB[:, 256:384], sc(s_, 9), Sb, start=True,
                         stop=False)
                return t.matmul(pB[:, 256:384], sc(s_, 4), sc(s_, 15),
                                start=False, stop=True)
            ch.p(omm)
            ch.p(lambda t: t.matmul(pB[:, 384:512], sc(s_, 5), sc(s_, 15),
                                    start=True, stop=True))
            ch.v(lambda v: v.tensor_scalar(Sf, Sf, gtc(s_, 5, n), None,
                                           ALU.mult))
            ch.v(lambda v: v.tensor_add(Sf, Sf, pB[:, 384:512]))
            ch.v(lambda v: v.tensor_copy(Sb, Sf))
            ch.p(lambda t: t.transpose(pT[:, 384:512], Zz[:, nsl],
                                       kc["ident_bf16"][:]))
            ch.s(lambda s: s.activation(Zf, pT[:, 384:512], AF.Exp,
                                        scale=-1.0))
            ch.v(lambda v: v.tensor_scalar(Zf, Zf, 1.0, None, ALU.add))
            ch.v(lambda v: v.reciprocal(Zf, Zf))
            ch.v(lambda v: v.tensor_mul(sc(s_, 12), pT[:, 384:512], Zf))
            ch.v(lambda v: v.tensor_mul(Of, pB[:, 256:384], sc(s_, 12)))
            ch.s(lambda s: s.activation(Zf, Of, AF.Square))
            ch.v(lambda v: v.reduce_sum(r1, Zf, axis=AX.X))
            ch.s(lambda s: s.activation(r1, r1, AF.Ln, scale=1.0 / 128.0,
                                        bias=kc["eps_a"][:]))
            ch.s(lambda s: s.activation(r1, r1, AF.Exp, scale=-0.5))
            ch.v(lambda v: v.tensor_scalar(Of, Of, r1, None, ALU.mult))
            ch.v(lambda v: v.tensor_mul(sc(s_, 13), Of, NWB[:]))
            ch.p(lambda t: t.transpose(pT[:, 512:640], sc(s_, 13),
                                       kc["ident_bf16"][:]))
            ch.v(lambda v: v.tensor_copy(OGg[:, nsl], pT[:, 512:640]))

        def dbg_dump(tile, width=SEQ):
            c.barrier()
            chD = Chain(c)
            chD.v(lambda v: v.tensor_copy(res[:, 0:width], tile[:, 0:width]))
            c.barrier()
            _stk2.close()

        if L1DBG[0] in ("cc1", "cc1o"):
            chD = Chain(c)
            srcap = (io["cc1_in"] if L1DBG[0] == "cc1" else io["cc1_out"])
            chD.w(lambda s: s.dma_start(
                out=PAD[:, 0:1024],
                in_=srcap[0:64, :].rearrange("r (a k) -> (r a) k", a=2)))
            chD.v(lambda v: v.tensor_copy(res[:, 0:1024], PAD[:, 0:1024]))
            c.barrier()
            _stk2.close()
            return
        for hkl in range(4):
            chL = Chain(c)
            load_piece(chL, 4 * hkl, hkl, PAD[:, 3:3 + SEQ])
            if L1DBG[0] == "raw":
                dbg_dump(PAD[:, 3:3 + SEQ])
                return
            conv_silu(chL, QN)
            if L1DBG[0] == "conv":
                dbg_dump(QN)
                return
            l2norm_piece(chL, QN, qscale=True)
            if L1DBG[0] == "qn":
                dbg_dump(QN)
                return
            load_piece(chL, 16 + 4 * hkl, 4 + hkl, PAD[:, 3:3 + SEQ])
            conv_silu(chL, KN)
            l2norm_piece(chL, KN, qscale=False)
            for s_ in range(2):
                u = 2 * hkl + s_
                load_piece(chL, 32 + 4 * u, 8 + u, PAD[:, 3:3 + SEQ])
                conv_silu(chL, Vs[s_])
                load_piece(chL, 64 + 4 * u, None, Zs[s_][:, 0:SEQ])
            c.barrier()
            for s_ in range(2):
                chU = Chain(c)
                unit_rows(chU, s_, 2 * hkl + s_)
                c.barrier()
                if L1DBG[0] == "rows" and s_ == 0:
                    dbg_dump(GT, width=GTW)
                    return
                for n in range(32):
                    unit_chunk(chU, s_, n)
                    if n == 0 and s_ == 0 and L1DBG[0] == "ck0":
                        c.barrier()
                        chD = Chain(c)
                        chD.v(lambda v: v.tensor_copy(res[:, 0:2048],
                                                      SC[:, 0:2048]))
                        chD.v(lambda v: v.tensor_copy(res[:, 2048:2176],
                                                      S32[:, 0:128]))
                        chD.v(lambda v: v.tensor_copy(res[:, 2176:2304],
                                                      OG0[:, 0:128]))
                        c.barrier()
                        _stk2.close()
                        return
                c.barrier()
            wspill = c.sW.reserve(2 * 16)

            @blk.sync
            def _(s, hkl=hkl):
                for s_ in range(2):
                    u = 2 * hkl + s_
                    s.dma_start(
                        out=io["cc2_in"][u * 256:(u + 1) * 256, :]
                        .rearrange("r (a k) -> (r a) k", a=2),
                        in_=OGs[s_][:]).then_inc(c.sW.h, 16)
            c.barrier()

    _stk2.close()
    cc2b = c.sC.reserve(1)

    @blk.gpsimd
    def _(g):
        g.collective_compute(
            "AllGather", ALU.bypass, replica_groups=[list(range(8))],
            ins=[io["cc2_in"]], outs=[io["cc2_out"]]).then_inc(c.sC.h, 1)
    c.barrier()

    cc2f = io["cc2_out"].rearrange("r (a k) -> (r a) k", a=2)
    with (
        nc.sbuf_tensor(uname("lc_og"), [128, 8 * TOK], BF16) as OGC,
        nc.sbuf_tensor(uname("lc_wo"), [128, 8 * 2048], BF16) as WOF,
        nc.psum_tensor(uname("lc_py0"), [128, TOK], F32) as py0,
        nc.psum_tensor(uname("lc_py1"), [128, TOK], F32) as py1,
        nc.psum_tensor(uname("lc_py2"), [128, TOK], F32) as py2,
    ):
        gidx4 = kc["g4idx"]
        pys = (py0, py1, py2)
        for grp in range(4):
            gb4 = c.sG.reserve(16 * 16)
            pe4 = c.sP.reserve(16)
            v4 = c.sV.reserve(16)

            @blk.gpsimd
            def _(g, grp=grp):
                for i in range(8):
                    hv = grp * 8 + i
                    g.indirect_dma_start(
                        out=OGC[:, i * TOK:(i + 1) * TOK], out_offset=None,
                        in_=cc2f, in_offset=IndirectOffsetOnAxis(
                            gidx4[:, hv:hv + 1], 0)).then_inc(c.sG.h, 16)
                for i in range(8):
                    hv = grp * 8 + i
                    g.dma_start(
                        out=WOF[:, i * 2048:(i + 1) * 2048]
                        .rearrange("p (a b) -> p a b", b=128),
                        in_=io["L1_attn_W_out"][hv * 128:(hv + 1) * 128, :]
                        .rearrange("(cc p) o -> p cc o", p=128)
                        ).then_inc(c.sG.h, 16)

            @blk.tensor
            def _(t, grp=grp, gb4=gb4, v4=v4):
                t.wait_ge(c.sG.h, gb4 + 16 * 16)
                for ot in range(NCH):
                    py = pys[ot % 3]
                    if ot >= 3:
                        t.wait_ge(c.sV.h, v4 + ot - 2)
                    for i in range(8):
                        for hh in range(2):
                            inst = t.matmul(
                                py[:, hh * 512:(hh + 1) * 512],
                                WOF[:, i * 2048 + ot * 128:
                                    i * 2048 + (ot + 1) * 128],
                                OGC[:, i * TOK + hh * 512:
                                    i * TOK + (hh + 1) * 512],
                                start=(i == 0), stop=(i == 7))
                    inst.then_inc(c.sP.h, 1)

            @blk.vector
            def _(v, grp=grp, pe4=pe4, v4=v4):
                for ot in range(NCH):
                    v.wait_ge(c.sP.h, pe4 + ot + 1)
                    py = pys[ot % 3]
                    v.tensor_add(res[:, ot * TOK:(ot + 1) * TOK],
                                 res[:, ot * TOK:(ot + 1) * TOK],
                                 py[:]).then_inc(c.sV.h, 1)
            c.barrier()


def emit_epilogue(c):
    nc, blk = c.nc, c.blk
    res, io = c.bufs["RES"], c.ios
    ob = c.sO.reserve(16)

    @blk.sync
    def _(s):
        s.dma_start(out=io["outT"].rearrange("(c p) t -> p c t", p=128),
                    in_=res[:].rearrange("p (c t) -> p c t", t=TOK)
                    ).then_inc(c.sO.h, 16)
        s.wait_ge(c.sO.h, ob + 16)



PHASES_ALL = ["l0ffn_a", "l0attn", "l0ffn_b", "l1ffn_a", "l1attn", "l1ffn_b"]
PHASES_DONE = ["l0ffn_a", "l0attn", "l0ffn_b", "l1ffn_a", "l1attn",
               "l1ffn_b"]


def build_program(phases=None):
    if phases is None:
        phases = PHASES_DONE
    _uid[0] = 0
    nc = bass.Bass("TRN2", target_bir_lowering=False, debug=False,
                   detect_race_conditions=RACE_DETECT[0])
    io = {}
    io["x0T"] = nc.dram_tensor("x0T", [D, TOK], F32, kind="ExternalInput").ap()
    for nm, (shp, dt_) in CONST_SPECS.items():
        io[nm] = nc.dram_tensor(nm, shp, dt_, kind="ExternalInput").ap()
    for nm, (shp, dt_) in WEIGHT_SPECS.items():
        io[nm] = nc.dram_tensor(nm, shp, dt_, kind="ExternalInput").ap()
    io["outT"] = nc.dram_tensor("outT", [D, TOK], F32,
                                kind="ExternalOutput").ap()
    io["qscr"] = nc.dram_tensor("qscr", [16, 128, TOK], BF16,
                                kind="Internal").ap()
    io["kv_in"] = nc.dram_tensor("kv_in", [1024, 1024], BF16,
                                 kind="Internal").ap()
    io["kv_out"] = nc.dram_tensor("kv_out", [4096, 1024], BF16,
                                  kind="Internal").ap()
    io["cc1_in"] = nc.dram_tensor("cc1_in", [6208, 2048], BF16,
                                  kind="Internal").ap()
    io["cc1_out"] = nc.dram_tensor("cc1_out", [8 * 6208, 2048], BF16,
                                   kind="Internal").ap()
    io["ccbg_in"] = nc.dram_tensor("ccbg_in", [64, 1024], F32,
                                   kind="Internal").ap()
    io["ccbg_out"] = nc.dram_tensor("ccbg_out", [512, 1024], F32,
                                    kind="Internal").ap()
    io["cc2_in"] = nc.dram_tensor("cc2_in", [2048, 2048], BF16,
                                  kind="Internal").ap()
    io["cc2_out"] = nc.dram_tensor("cc2_out", [16384, 2048], BF16,
                                   kind="Internal").ap()
    io["ba_scr"] = nc.dram_tensor("ba_scr", [64, 1024], F32,
                                  kind="Internal").ap()

    with ExitStack() as stack:
        res = stack.enter_context(nc.sbuf_tensor("RES", [128, NCH * TOK], F32))
        kc = {}
        for nm in KC_NAMES:
            shp, dt_ = (CONST_SPECS.get(nm) or WEIGHT_SPECS[nm])
            kc[nm] = stack.enter_context(nc.sbuf_tensor("kc_" + nm, shp, dt_))
        gwt = stack.enter_context(
            nc.sbuf_tensor("gate_wT", [128, NCH * 8], BF16))

        blk = stack.enter_context(nc.Block())
        c = Ctx(nc, blk, stack)
        c.ios = io
        c.bufs = {"RES": res, "KC": kc, "gate_wT": gwt}

        emit_prologue(c)

        for ph in phases:
            if ph in ("l0ffn_a", "l0ffn_b"):
                with (
                    nc.sbuf_tensor(uname("ffn_ln"), [128, NCH * 512], BF16) as ln,
                    nc.sbuf_tensor(uname("ffn_w0"), [128, 22528], BF16) as w0,
                    nc.sbuf_tensor(uname("ffn_w1"), [128, 22528], BF16) as w1,
                    nc.sbuf_tensor(uname("ffn_hs"), [128, 11 * 512], BF16) as hs,
                ):
                    emit_l0ffn(c, ln, (w0, w1), hs)
            elif ph == "l0attn":
                emit_l0attn(c)
            elif ph == "l1attn":
                emit_l1attn(c)
            elif ph in ("l1ffn_a", "l1ffn_b"):
                with (
                    nc.sbuf_tensor(uname("f1_ln"), [128, NCH * TOK], BF16) as ln,
                    nc.sbuf_tensor(uname("f1_wg0"), [128, 16 * 512], BF16) as wg0,
                    nc.sbuf_tensor(uname("f1_wg1"), [128, 16 * 512], BF16) as wg1,
                    nc.sbuf_tensor(uname("f1_wu0"), [128, 16 * 512], BF16) as wu0,
                    nc.sbuf_tensor(uname("f1_wu1"), [128, 16 * 512], BF16) as wu1,
                    nc.sbuf_tensor(uname("f1_wd"), [128, 4 * 2048], BF16) as wd,
                    nc.sbuf_tensor(uname("f1_G"), [128, 2 * TOK], BF16) as G,
                    nc.sbuf_tensor(uname("f1_F"), [128, 4 * TOK], BF16) as FU,
                    nc.sbuf_tensor(uname("f1_sdh"), [128, 16], F32) as sdh,
                ):
                    emit_l1ffn(c, ln, (wg0, wg1), (wu0, wu1), wd, G, FU, sdh)
            else:
                raise NotImplementedError(ph)

        emit_epilogue(c)
    return nc



def prep_weight_inputs(inputs):
    w = {}
    w["L0_attn_Wq"] = np.ascontiguousarray(
        inputs["L0_attn_Wq"].reshape(2048, 2048))
    w["L0_attn_Wk"] = np.ascontiguousarray(
        inputs["L0_attn_Wk"].reshape(2048, 512))
    w["L0_attn_Wv"] = np.ascontiguousarray(
        inputs["L0_attn_Wv"].reshape(2048, 512))
    w["L0_attn_Wo"] = np.ascontiguousarray(inputs["L0_attn_Wo"])
    for nm in ("gate_w", "w1", "w3", "w2", "sw1", "sw3", "sw2"):
        w[f"L0_ffn_{nm}"] = np.ascontiguousarray(inputs[f"L0_ffn_{nm}"])
    w["L1_attn_W_qkvz"] = np.ascontiguousarray(inputs["L1_attn_W_qkvz"])
    w["L1_attn_W_ba"] = np.ascontiguousarray(inputs["L1_attn_W_ba"])
    w["L1_attn_conv_w"] = np.ascontiguousarray(inputs["L1_attn_conv_w"])
    w["L1_attn_dt_bias"] = np.ascontiguousarray(
        inputs["L1_attn_dt_bias"].reshape(32, 1))
    w["L1_attn_A_log"] = np.ascontiguousarray(
        inputs["L1_attn_A_log"].reshape(32, 1))
    w["L1_attn_norm_w"] = np.ascontiguousarray(
        inputs["L1_attn_norm_w"].reshape(1, 128))
    w["L1_attn_W_out"] = np.ascontiguousarray(inputs["L1_attn_W_out"])
    for nm in ("w_gate", "w_up", "w_down"):
        w[f"L1_ffn_{nm}"] = np.ascontiguousarray(inputs[f"L1_ffn_{nm}"])
    w["L1_ffn_s_gate"] = np.ascontiguousarray(
        inputs["L1_ffn_s_gate"].reshape(64, 128).T)
    w["L1_ffn_s_up"] = np.ascontiguousarray(
        inputs["L1_ffn_s_up"].reshape(64, 128).T)
    w["L1_ffn_s_down"] = np.ascontiguousarray(
        inputs["L1_ffn_s_down"].reshape(16, 128).T)
    return w


def make_in_maps(inputs, phases=None):
    x = inputs["L0_attn_x_q"]
    w = prep_weight_inputs(inputs)
    in_maps = []
    for core in range(NCORE):
        b, p = core // 4, core % 4
        xs = np.ascontiguousarray(x[b, p * TOK:(p + 1) * TOK, :].T)
        m = {"x0T": xs}
        m.update(make_const_inputs(core))
        m.update(w)
        in_maps.append(m)
    return in_maps


def kernel(**inputs):
    nc = build_program()
    in_maps = make_in_maps(inputs)
    res = run_bass_kernel_spmd(nc, in_maps, core_ids=list(range(NCORE)))
    out = np.empty((2, 4096, 2048), np.float32)
    for core in range(NCORE):
        b, p = core // 4, core % 4
        out[b, p * TOK:(p + 1) * TOK, :] = res.results[core]["outT"].T
    return out


# version 03 (final)
# speedup vs baseline: 1.2114x; 1.0093x over previous
import numpy as np
import ml_dtypes
from contextlib import ExitStack

import concourse.bass as bass
import concourse.mybir as mybir
from concourse.bass_utils import run_bass_kernel_spmd
from concourse.bass import IndirectOffsetOnAxis

F32 = mybir.dt.float32
BF16 = mybir.dt.bfloat16
I8 = mybir.dt.int8
I32 = mybir.dt.int32
AF = mybir.ActivationFunctionType
ALU = mybir.AluOpType
AX = mybir.AxisListType

NCORE = 8
TOK = 1024
D = 2048
NCH = 16
LN_EPS = 1e-5

_uid = [0]
RACE_DETECT = [False]
L1DBG = [None]


def uname(s):
    _uid[0] += 1
    return f"{s}_{_uid[0]}"


class Sem:
    def __init__(self, stack, nc, name):
        self.h = stack.enter_context(nc.semaphore(name))
        self.n = 0

    def reserve(self, k):
        base = self.n
        self.n += k
        return base


class Ctx:
    def __init__(self, nc, blk, stack):
        self.nc = nc
        self.blk = blk
        self.stack = stack
        self.sW = Sem(stack, nc, "sW")
        self.sG = Sem(stack, nc, "sG")
        self.sO = Sem(stack, nc, "sO")
        self.sP = Sem(stack, nc, "sP")
        self.sV = Sem(stack, nc, "sV")
        self.sS = Sem(stack, nc, "sS")
        self.sQ = Sem(stack, nc, "sQ")
        self.sC = Sem(stack, nc, "sC")
        self.sB = Sem(stack, nc, "sB")
        self.ios = {}
        self.bufs = {}

    def barrier(self):
        base = self.sB.reserve(5)
        target = base + 5
        waits = [(s.h, s.n) for s in (self.sW, self.sG, self.sO, self.sC)
                 if s.n > 0]

        def f(e):
            e.drain().then_inc(self.sB.h, 1)
            e.wait_ge(self.sB.h, target)
            for h, n in waits:
                e.wait_ge(h, n)

        for sec in (self.blk.sync, self.blk.tensor, self.blk.vector,
                    self.blk.scalar, self.blk.gpsimd):
            sec(f)



def _bf16(x):
    return np.ascontiguousarray(x).astype(ml_dtypes.bfloat16)


def make_const_inputs(core):
    pos = core % 4
    ii = np.arange(128)
    c = {}
    c["ident_bf16"] = _bf16(np.eye(128, dtype=np.float32))
    c["ident_f32"] = np.eye(128, dtype=np.float32)
    c["ones_col_bf16"] = _bf16(np.ones((128, 1), np.float32))
    c["ones_row_f32"] = np.ones((1, 128), np.float32)
    c["ones_row_bf16"] = _bf16(np.ones((1, 128), np.float32))
    oh = np.zeros((8, 8 * 128), np.float32)
    for e in range(8):
        oh[e, e * 128:(e + 1) * 128] = 1.0
    c["onehot8"] = oh
    c["eps_ln"] = np.full((128, 1), 1e-5, np.float32)
    c["eps_a"] = np.full((128, 1), 1e-6, np.float32)
    c["mask_su"] = (ii[None, :] > ii[:, None]).astype(np.float32)
    c["mask_u"] = (ii[None, :] >= ii[:, None]).astype(np.float32)
    c["mask_su_neg"] = -c["mask_su"]
    c["mask_sl_neg"] = -(ii[None, :] < ii[:, None]).astype(np.float32)
    c["one_col"] = np.ones((128, 1), np.float32)
    b = core // 4
    rloc = core % 4
    BLK = 6208 * 2048
    hks = [4 * rloc + i for i in range(4)]
    hvs = [8 * rloc + i for i in range(8)]

    def mixrow(m, ch, p):
        return (b * 4 + m) * 12416 + ch * 128 + p

    g1 = np.zeros((128, 96), np.int32)
    col = 0
    pieces = []
    for hk in hks:
        pieces.append(4 * hk + 0)
    for hk in hks:
        pieces.append(4 * hk + 1)
    for hv in hvs:
        pieces.append(4 * (hv // 2) + 2 + (hv % 2))
    for hv in hvs:
        pieces.append(64 + 2 * (hv // 2) + (hv % 2))
    for ch in pieces:
        for m in range(4):
            g1[:, col] = mixrow(m, ch, ii)
            col += 1
    c["g1idx"] = g1
    g2 = np.zeros((4, 16), np.int32)
    for ui, hv in enumerate(hvs):
        for m in range(4):
            g2[m, ui] = (b * 4 + m) * 64 + hv
            g2[m, 8 + ui] = (b * 4 + m) * 64 + 32 + hv
    c["g2idx"] = g2
    g3 = np.zeros((128, 16), np.int32)
    col = 0
    for hk in hks:
        g3[:, col] = hk * 512 + ii
        col += 1
    for hk in hks:
        g3[:, col] = hk * 512 + 128 + ii
        col += 1
    for hv in hvs:
        g3[:, col] = (hv // 2) * 512 + 256 + (hv % 2) * 128 + ii
        col += 1
    c["g3idx"] = g3
    g4 = np.zeros((128, 32), np.int32)
    for hv in range(32):
        rc = b * 4 + hv // 8
        u = hv % 8
        g4[:, hv] = rc * 4096 + u * 512 + ii * 4 + rloc
    c["g4idx"] = g4
    big = np.int32(1 << 28)
    hid = np.full((128, 1), big, np.int32)
    if pos > 0:
        hid[:, 0] = (core - 1) * 128 + ii
    c["halo_idx"] = hid
    for s in range(3):
        j = pos - 3 + s
        cid = np.full((128, 1), big, np.int32)
        if 0 <= j < pos:
            cid[:, 0] = ((core // 4) * 4 + j) * 128 + ii
        c[f"comp_idx{s}"] = cid
    return c


CONST_SPECS = {
    "ident_bf16": ([128, 128], BF16),
    "ident_f32": ([128, 128], F32),
    "ones_col_bf16": ([128, 1], BF16),
    "ones_row_f32": ([1, 128], F32),
    "ones_row_bf16": ([1, 128], BF16),
    "onehot8": ([8, 8 * 128], F32),
    "eps_ln": ([128, 1], F32),
    "eps_a": ([128, 1], F32),
    "mask_su": ([128, 128], F32),
    "mask_u": ([128, 128], F32),
    "mask_su_neg": ([128, 128], F32),
    "mask_sl_neg": ([128, 128], F32),
    "one_col": ([128, 1], F32),
    "g1idx": ([128, 96], I32),
    "g2idx": ([4, 16], I32),
    "g3idx": ([128, 16], I32),
    "g4idx": ([128, 32], I32),
    "halo_idx": ([128, 1], I32),
    "comp_idx0": ([128, 1], I32),
    "comp_idx1": ([128, 1], I32),
    "comp_idx2": ([128, 1], I32),
}

WEIGHT_SPECS = {
    "L0_attn_Wq": ([2048, 2048], F32),
    "L0_attn_Wk": ([2048, 512], F32),
    "L0_attn_Wv": ([2048, 512], F32),
    "L0_attn_Wo": ([2048, 2048], F32),
    "L0_ffn_gate_w": ([8, 2048], F32),
    "L0_ffn_w1": ([8, 2048, 1408], F32),
    "L0_ffn_w3": ([8, 2048, 1408], F32),
    "L0_ffn_w2": ([8, 1408, 2048], F32),
    "L0_ffn_sw1": ([2048, 2816], F32),
    "L0_ffn_sw3": ([2048, 2816], F32),
    "L0_ffn_sw2": ([2816, 2048], F32),
    "L1_attn_W_qkvz": ([2048, 12288], F32),
    "L1_attn_W_ba": ([2048, 64], F32),
    "L1_attn_conv_w": ([8192, 4], F32),
    "L1_attn_dt_bias": ([32, 1], F32),
    "L1_attn_A_log": ([32, 1], F32),
    "L1_attn_norm_w": ([1, 128], F32),
    "L1_attn_W_out": ([4096, 2048], F32),
    "L1_ffn_w_gate": ([2048, 8192], I8),
    "L1_ffn_w_up": ([2048, 8192], I8),
    "L1_ffn_w_down": ([8192, 2048], I8),
    "L1_ffn_s_gate": ([128, 64], F32),
    "L1_ffn_s_up": ([128, 64], F32),
    "L1_ffn_s_down": ([128, 16], F32),
}

KC_NAMES = ["ident_bf16", "ident_f32", "ones_col_bf16", "ones_row_f32",
            "ones_row_bf16", "onehot8", "eps_ln", "eps_a", "mask_su",
            "mask_u", "mask_su_neg", "mask_sl_neg", "one_col",
            "g1idx", "g2idx", "g3idx", "g4idx", "halo_idx",
            "comp_idx0", "comp_idx1", "comp_idx2",
            "L1_attn_dt_bias", "L1_attn_A_log", "L1_attn_norm_w",
            "L1_ffn_s_gate", "L1_ffn_s_up", "L1_ffn_s_down"]



def emit_prologue(c):
    nc, blk = c.nc, c.blk
    io, kc = c.ios, c.bufs["KC"]
    res = c.bufs["RES"]
    gwt = c.bufs["gate_wT"]

    wb = c.sW.reserve(16 * (1 + len(KC_NAMES)))
    gb = c.sG.reserve(16)

    @blk.sync
    def _(s):
        s.dma_start(out=res[:].rearrange("p (c t) -> p c t", t=TOK),
                    in_=io["x0T"].rearrange("(c p) t -> p c t", p=128)
                    ).then_inc(c.sW.h, 16)
        for nm in KC_NAMES:
            s.dma_start(out=kc[nm][:], in_=io[nm]).then_inc(c.sW.h, 16)

    pb = c.sP.reserve(NCH)
    vb = c.sV.reserve(NCH)
    with (
        nc.sbuf_tensor(uname("gate_w_bf"), [8, 2048], BF16) as gw,
        nc.psum_tensor(uname("pgt"), [128, 16], BF16) as pgt,
    ):
        @blk.gpsimd
        def _(g):
            g.dma_start(out=gw[:], in_=io["L0_ffn_gate_w"]).then_inc(c.sG.h, 16)

        @blk.tensor
        def _(t):
            t.wait_ge(c.sG.h, gb + 16)
            t.wait_ge(c.sW.h, wb + 16 * (1 + len(KC_NAMES)))
            for ch in range(NCH):
                if ch >= 2:
                    t.wait_ge(c.sV.h, vb + ch - 1)
                sl = (ch % 2) * 8
                t.transpose(pgt[:, sl:sl + 8], gw[:, ch * 128:(ch + 1) * 128],
                            kc["ident_bf16"][0:8, 0:8]).then_inc(c.sP.h, 1)

        @blk.vector
        def _(v):
            for ch in range(NCH):
                v.wait_ge(c.sP.h, pb + ch + 1)
                sl = (ch % 2) * 8
                v.tensor_copy(gwt[:, ch * 8:(ch + 1) * 8],
                              pgt[:, sl:sl + 8]).then_inc(c.sV.h, 1)
    c.barrier()


def emit_ln(c, tok0, T, ln, ln_off=0, ln_stride=None):
    assert T <= 512
    if ln_stride is None:
        ln_stride = T
    nc, blk = c.nc, c.blk
    kc, res = c.bufs["KC"], c.bufs["RES"]

    def L(ch):
        return ln[:, ln_off + ch * ln_stride: ln_off + ch * ln_stride + T]

    with (
        nc.sbuf_tensor(uname("ln_sq"), [128, 2 * T], BF16) as sq,
        nc.sbuf_tensor(uname("ln_mb"), [128, T], F32) as mb,
        nc.sbuf_tensor(uname("ln_rb"), [128, T], F32) as rb,
        nc.sbuf_tensor(uname("ln_tp"), [128, T], F32) as tmp,
        nc.psum_tensor(uname("ln_red"), [1, 2 * T], F32) as ps_red,
        nc.psum_tensor(uname("ln_bc"), [128, 2 * T], F32) as ps_bc,
    ):
        qb = c.sQ.reserve(NCH)
        sb = c.sS.reserve(NCH)
        pb = c.sP.reserve(NCH)

        @blk.gpsimd
        def _(g):
            for ch in range(NCH):
                g.tensor_copy(L(ch),
                              res[:, ch * TOK + tok0:ch * TOK + tok0 + T]
                              ).then_inc(c.sQ.h, 1)

        @blk.scalar
        def _(s):
            for ch in range(NCH):
                s.wait_ge(c.sQ.h, qb + ch + 1)
                if ch >= 2:
                    s.wait_ge(c.sP.h, pb + ch - 1)
                s.activation(sq[:, (ch % 2) * T:(ch % 2) * T + T], L(ch),
                             AF.Square).then_inc(c.sS.h, 1)

        @blk.tensor
        def _(t):
            t.wait_ge(c.sQ.h, qb + NCH)
            for ch in range(NCH):
                t.matmul(ps_red[0:1, 0:T], kc["ones_col_bf16"][:], L(ch),
                         start=(ch == 0), stop=(ch == NCH - 1))
            for ch in range(NCH):
                t.wait_ge(c.sS.h, sb + ch + 1)
                t.matmul(ps_red[0:1, T:2 * T], kc["ones_col_bf16"][:],
                         sq[:, (ch % 2) * T:(ch % 2) * T + T],
                         start=(ch == 0), stop=(ch == NCH - 1)
                         ).then_inc(c.sP.h, 1)

        vb = c.sV.reserve(3)
        sb2 = c.sS.reserve(1)

        @blk.vector
        def _(v):
            v.wait_ge(c.sP.h, pb + NCH)
            v.tensor_scalar(mb[0:1, 0:T], ps_red[0:1, 0:T], -1.0 / D, None,
                            ALU.mult)
            v.tensor_scalar(rb[0:1, 0:T], ps_red[0:1, 0:T], 1.0 / D,
                            None, ALU.mult).then_inc(c.sV.h, 1)
            v.wait_ge(c.sV.h, vb + 1)
            v.tensor_mul(rb[0:1, 0:T], rb[0:1, 0:T],
                         rb[0:1, 0:T]).then_inc(c.sV.h, 1)
            v.wait_ge(c.sV.h, vb + 2)
            v.scalar_tensor_tensor(rb[0:1, 0:T], ps_red[0:1, T:2 * T],
                                   1.0 / D, rb[0:1, 0:T],
                                   ALU.mult, ALU.subtract).then_inc(c.sV.h, 1)

        @blk.scalar
        def _(s):
            s.wait_ge(c.sV.h, vb + 3)
            s.activation(rb[0:1, 0:T], rb[0:1, 0:T],
                         AF.Ln, bias=kc["eps_ln"][0:1, :])
            s.activation(rb[0:1, 0:T], rb[0:1, 0:T],
                         AF.Exp, scale=-0.5).then_inc(c.sS.h, 1)

        pb2 = c.sP.reserve(1)

        @blk.tensor
        def _(t):
            t.wait_ge(c.sS.h, sb2 + 1)
            t.matmul(ps_bc[:, 0:T], kc["ones_row_f32"][:], mb[0:1, 0:T],
                     start=True, stop=True)
            t.matmul(ps_bc[:, T:2 * T], kc["ones_row_f32"][:],
                     rb[0:1, 0:T],
                     start=True, stop=True).then_inc(c.sP.h, 1)

        vb2 = c.sV.reserve(2 + 2 * NCH)

        @blk.vector
        def _(v):
            v.wait_ge(c.sP.h, pb2 + 1)
            v.tensor_copy(mb[:], ps_bc[:, 0:T]).then_inc(c.sV.h, 1)
            v.tensor_copy(rb[:], ps_bc[:, T:2 * T]).then_inc(c.sV.h, 1)
            for ch in range(NCH):
                v.tensor_add(tmp[:], res[:, ch * TOK + tok0:ch * TOK + tok0 + T],
                             mb[:]).then_inc(c.sV.h, 1)
                v.tensor_mul(L(ch), tmp[:], rb[:]).then_inc(c.sV.h, 1)
        c.barrier()


def _w_ap_moe(io, e, which):
    if e < 8:
        a = io[f"L0_ffn_{which}"][e]
    else:
        j = e - 8
        if which == "w1":
            a = io["L0_ffn_sw1"][:, j * 1408:(j + 1) * 1408]
        elif which == "w3":
            a = io["L0_ffn_sw3"][:, j * 1408:(j + 1) * 1408]
        else:
            a = io["L0_ffn_sw2"][j * 1408:(j + 1) * 1408, :]
    return a.rearrange("(c p) i -> p c i", p=128)


def emit_l0ffn(c, ln, wslot, hs, experts=tuple(range(10)), halves=(0, 1), dump_gate=False):
    nc, blk = c.nc, c.blk
    io, kc, res = c.ios, c.bufs["KC"], c.bufs["RES"]
    gwt = c.bufs["gate_wT"]
    T = 512

    with (
        nc.sbuf_tensor(uname("gt_lg"), [128, 32], F32) as LG,
        nc.sbuf_tensor(uname("gt_m"), [128, 16], F32) as M,
        nc.sbuf_tensor(uname("gt_w"), [128, 32], F32) as W,
        nc.sbuf_tensor(uname("gt_wt"), [8, 512], F32) as Wt,
        nc.sbuf_tensor(uname("gt_tp"), [128, 32], F32) as GTMP,
    ):
        for half in halves:
            tok0 = half * T
            emit_ln(c, tok0, T, ln)

            with (
                nc.psum_tensor(uname("pf_lg"), [128, 32], F32) as ps_lg,
                nc.psum_tensor(uname("pf_wt"), [8, 256], F32) as ps_wt,
            ):
                pb = c.sP.reserve(1)

                @blk.tensor
                def _(t):
                    for tt in range(4):
                        for ch in range(NCH):
                            inst = t.matmul(
                                ps_lg[:, tt * 8:(tt + 1) * 8],
                                ln[:, ch * T + tt * 128:ch * T + tt * 128 + 128],
                                gwt[:, ch * 8:(ch + 1) * 8],
                                start=(ch == 0), stop=(ch == NCH - 1))
                    inst.then_inc(c.sP.h, 1)

                DBG_BAR = False
                vb = c.sV.reserve(1)
                sb = c.sS.reserve(1)

                @blk.vector
                def _(v):
                    v.wait_ge(c.sP.h, pb + 1)
                    v.tensor_copy(LG[:], ps_lg[:])
                    for tt in range(4):
                        v.reduce_max(M[:, tt:tt + 1],
                                     LG[:, tt * 8:(tt + 1) * 8], axis=AX.X)
                    for tt in range(4):
                        v.tensor_scalar(M[:, 4 + tt:5 + tt], M[:, tt:tt + 1],
                                        -1.0, None, ALU.mult)
                    for tt in range(4):
                        v.tensor_scalar(GTMP[:, tt * 8:(tt + 1) * 8],
                                        LG[:, tt * 8:(tt + 1) * 8],
                                        M[:, tt:tt + 1], None, ALU.is_ge)
                    for tt in range(4):
                        v.scalar_tensor_tensor(GTMP[:, tt * 8:(tt + 1) * 8],
                                               GTMP[:, tt * 8:(tt + 1) * 8],
                                               -30000.0,
                                               LG[:, tt * 8:(tt + 1) * 8],
                                               ALU.mult, ALU.add)
                    for tt in range(4):
                        inst = v.reduce_max(M[:, 8 + tt:9 + tt],
                                            GTMP[:, tt * 8:(tt + 1) * 8],
                                            axis=AX.X)
                    inst.then_inc(c.sV.h, 1)
                if DBG_BAR:
                    c.barrier()

                @blk.scalar
                def _(s):
                    s.wait_ge(c.sV.h, vb + 1)
                    for tt in range(4):
                        inst = s.activation(W[:, tt * 8:(tt + 1) * 8],
                                            LG[:, tt * 8:(tt + 1) * 8], AF.Exp,
                                            bias=M[:, 4 + tt:5 + tt])
                    inst.then_inc(c.sS.h, 1)
                if DBG_BAR:
                    c.barrier()

                vb2 = c.sV.reserve(1)

                @blk.vector
                def _(v):
                    v.wait_ge(c.sS.h, sb + 1)
                    for tt in range(4):
                        v.tensor_scalar(GTMP[:, tt * 8:(tt + 1) * 8],
                                        LG[:, tt * 8:(tt + 1) * 8],
                                        M[:, 8 + tt:9 + tt], None, ALU.is_ge)
                    for tt in range(4):
                        v.tensor_mul(W[:, tt * 8:(tt + 1) * 8],
                                     W[:, tt * 8:(tt + 1) * 8],
                                     GTMP[:, tt * 8:(tt + 1) * 8])
                    for tt in range(4):
                        v.reduce_sum(M[:, 12 + tt:13 + tt],
                                     W[:, tt * 8:(tt + 1) * 8], axis=AX.X)
                    for tt in range(4):
                        v.reciprocal(M[:, 12 + tt:13 + tt],
                                     M[:, 12 + tt:13 + tt])
                    for tt in range(4):
                        inst = v.tensor_scalar(W[:, tt * 8:(tt + 1) * 8],
                                               W[:, tt * 8:(tt + 1) * 8],
                                               M[:, 12 + tt:13 + tt], 0.5,
                                               ALU.mult, ALU.mult)
                    inst.then_inc(c.sV.h, 1)

                if DBG_BAR:
                    c.barrier()
                pb2 = c.sP.reserve(4)
                vb3 = c.sV.reserve(4)

                @blk.tensor
                def _(t):
                    t.wait_ge(c.sV.h, vb2 + 1)
                    for tt in range(4):
                        if tt >= 2:
                            t.wait_ge(c.sV.h, vb3 + tt - 1)
                        t.transpose(
                            ps_wt[:, (tt % 2) * 128:(tt % 2) * 128 + 128],
                            W[:, tt * 8:(tt + 1) * 8],
                            kc["ident_f32"][:]).then_inc(c.sP.h, 1)

                @blk.vector
                def _(v):
                    for tt in range(4):
                        v.wait_ge(c.sP.h, pb2 + tt + 1)
                        v.tensor_copy(
                            Wt[:, tt * 128:(tt + 1) * 128],
                            ps_wt[:, (tt % 2) * 128:(tt % 2) * 128 + 128]
                            ).then_inc(c.sV.h, 1)
                c.barrier()
                if dump_gate:
                    res_ = c.bufs["RES"]
                    vbD = c.sV.reserve(3)

                    @blk.vector
                    def _(v):
                        v.tensor_copy(res_[:, 0:32], W[:]).then_inc(c.sV.h, 1)
                        v.tensor_copy(res_[0:8, 40:552], Wt[:]).then_inc(c.sV.h, 1)
                        v.tensor_copy(res_[:, 600:632], LG[:]).then_inc(c.sV.h, 1)
                    c.barrier()

            with (
                nc.psum_tensor(uname("pf_wb"), [128, 512], F32) as ps_wb,
                nc.psum_tensor(uname("pf_h0"), [128, 512], F32) as ph0,
                nc.psum_tensor(uname("pf_h1"), [128, 512], F32) as ph1,
                nc.psum_tensor(uname("pf_y0"), [128, 512], F32) as py0,
                nc.psum_tensor(uname("pf_y1"), [128, 512], F32) as py1,
            ):
                gbase = c.sG.reserve(16 * 3 * len(experts))
                pe_incs = []
                for e in experts:
                    pe_incs += [11, 11, 16]
                pe_base = c.sP.reserve(sum(pe_incs))
                sc_base = c.sS.reserve(len(experts) * 11)
                v_counts = []
                for e in experts:
                    v_counts += [0, 22 if e < 8 else 11, 16]
                v_base = c.sV.reserve(sum(v_counts))

                def pe_target(p):
                    return pe_base + sum(pe_incs[:p + 1])

                def v_target(p):
                    return v_base + sum(v_counts[:p + 1])

                @blk.gpsimd
                def _(g):
                    for ei, e in enumerate(experts):
                        for wi, which in enumerate(("w1", "w3", "w2")):
                            p = ei * 3 + wi
                            if p >= 2:
                                g.wait_ge(c.sP.h, pe_target(p - 2))
                            g.dma_start(out=wslot[p % 2][:],
                                        in_=_w_ap_moe(io, e, which)
                                        ).then_inc(c.sG.h, 16)

                @blk.tensor
                def _(t):
                    for ei, e in enumerate(experts):
                        p = ei * 3
                        t.wait_ge(c.sG.h, gbase + 16 * (p + 1))
                        if e < 8:
                            t.matmul(ps_wb[:],
                                     kc["onehot8"][:, e * 128:(e + 1) * 128],
                                     Wt[:], start=True, stop=True)
                        slot = wslot[p % 2]
                        for it in range(11):
                            ph = (ph0, ph1)[it % 2]
                            if it >= 2:
                                t.wait_ge(c.sS.h, sc_base + 11 * ei + it - 1)
                            for ch2 in range(NCH):
                                inst = t.matmul(
                                    ph[:],
                                    slot[:, ch2 * 1408 + it * 128:
                                         ch2 * 1408 + it * 128 + 128],
                                    ln[:, ch2 * T:(ch2 + 1) * T],
                                    start=(ch2 == 0), stop=(ch2 == NCH - 1))
                            inst.then_inc(c.sP.h, 1)
                        p = ei * 3 + 1
                        t.wait_ge(c.sG.h, gbase + 16 * (p + 1))
                        t.wait_ge(c.sS.h, sc_base + 11 * (ei + 1))
                        slot = wslot[p % 2]
                        per = 2 if e < 8 else 1
                        for it in range(11):
                            ph = (ph0, ph1)[it % 2]
                            if it >= 2:
                                t.wait_ge(c.sV.h,
                                          v_target(p - 1) + per * (it - 1))
                            for ch2 in range(NCH):
                                inst = t.matmul(
                                    ph[:],
                                    slot[:, ch2 * 1408 + it * 128:
                                         ch2 * 1408 + it * 128 + 128],
                                    ln[:, ch2 * T:(ch2 + 1) * T],
                                    start=(ch2 == 0), stop=(ch2 == NCH - 1))
                            inst.then_inc(c.sP.h, 1)
                        p = ei * 3 + 2
                        t.wait_ge(c.sG.h, gbase + 16 * (p + 1))
                        t.wait_ge(c.sV.h, v_target(p - 1))
                        slot = wslot[p % 2]
                        for ot in range(NCH):
                            py = (py0, py1)[ot % 2]
                            if ot >= 2:
                                t.wait_ge(c.sV.h, v_target(p - 1) + ot - 1)
                            for it in range(11):
                                inst = t.matmul(
                                    py[:],
                                    slot[:, it * 2048 + ot * 128:
                                         it * 2048 + ot * 128 + 128],
                                    hs[:, it * T:(it + 1) * T],
                                    start=(it == 0), stop=(it == 10))
                            inst.then_inc(c.sP.h, 1)

                @blk.scalar
                def _(s):
                    for ei, e in enumerate(experts):
                        base_p = pe_base + sum(pe_incs[:ei * 3])
                        for it in range(11):
                            s.wait_ge(c.sP.h, base_p + it + 1)
                            ph = (ph0, ph1)[it % 2]
                            s.activation(hs[:, it * T:(it + 1) * T], ph[:],
                                         AF.Silu).then_inc(c.sS.h, 1)

                @blk.vector
                def _(v):
                    for ei, e in enumerate(experts):
                        p = ei * 3 + 1
                        base_p = pe_base + sum(pe_incs[:p])
                        for it in range(11):
                            v.wait_ge(c.sP.h, base_p + it + 1)
                            ph = (ph0, ph1)[it % 2]
                            v.tensor_mul(hs[:, it * T:(it + 1) * T],
                                         hs[:, it * T:(it + 1) * T],
                                         ph[:]).then_inc(c.sV.h, 1)
                            if e < 8:
                                v.tensor_mul(hs[:, it * T:(it + 1) * T],
                                             hs[:, it * T:(it + 1) * T],
                                             ps_wb[:]).then_inc(c.sV.h, 1)
                        p = ei * 3 + 2
                        base_p = pe_base + sum(pe_incs[:p])
                        osc = 1.0 if e < 8 else 0.5
                        for ot in range(NCH):
                            v.wait_ge(c.sP.h, base_p + ot + 1)
                            py = (py0, py1)[ot % 2]
                            v.scalar_tensor_tensor(
                                res[:, ot * TOK + tok0:ot * TOK + tok0 + T],
                                py[:], osc,
                                res[:, ot * TOK + tok0:ot * TOK + tok0 + T],
                                ALU.mult, ALU.add).then_inc(c.sV.h, 1)
                c.barrier()


def emit_l0attn(c):
    nc, blk = c.nc, c.blk
    io, kc, res = c.ios, c.bufs["KC"], c.bufs["RES"]
    qscr = io["qscr"]
    kv_in = io["kv_in"]
    kv_out = io["kv_out"]
    GROUPS = [[0, 1, 2, 3], [4, 5, 6, 7]]

    with (
        nc.sbuf_tensor(uname("at_ln"), [128, NCH * TOK], BF16) as ln,
        nc.sbuf_tensor(uname("at_kf"), [128, 4 * TOK], BF16) as kf,
        nc.sbuf_tensor(uname("at_vt"), [128, 8 * 512], BF16) as vt,
        nc.sbuf_tensor(uname("at_wkv"), [128, 2 * NCH * 512], BF16) as wkv,
        nc.sbuf_tensor(uname("at_wq0"), [128, NCH * 128], BF16) as wq0,
        nc.sbuf_tensor(uname("at_wq1"), [128, NCH * 128], BF16) as wq1,
        nc.sbuf_tensor(uname("at_qt"), [128, 2 * TOK], BF16) as qt,
        nc.sbuf_tensor(uname("at_sq"), [128, TOK], BF16) as sqb,
        nc.sbuf_tensor(uname("at_row"), [1, TOK], F32) as row,
    ):
        emit_ln(c, 0, 512, ln, ln_off=0, ln_stride=TOK)
        emit_ln(c, 512, 512, ln, ln_off=512, ln_stride=TOK)
        import contextlib as _ctl
        _stkA = _ctl.ExitStack()
        pq0 = _stkA.enter_context(nc.psum_tensor(uname("at_pq0"), [128, TOK], F32))
        pq1 = _stkA.enter_context(nc.psum_tensor(uname("at_pq1"), [128, TOK], F32))
        psm = _stkA.enter_context(nc.psum_tensor(uname("at_psm"), [1, TOK], F32))
        pbc = _stkA.enter_context(nc.psum_tensor(uname("at_pbc"), [128, TOK], F32))

        wq_slots = (wq0, wq1)
        gb = c.sG.reserve(16 * (2 + 16 + 2))
        cb = c.sC.reserve(1)

        pe_b = c.sP.reserve(4 * 3 + 8 + 16 * 3)
        v_b = c.sV.reserve(4 * 3 + 8 + 16 * 3)
        s_b = c.sS.reserve(2 * (4 + 16))
        w_b = c.sW.reserve(16 * 16)

        def peK(i, j):
            return pe_b + 3 * i + j

        def peV(i):
            return pe_b + 12 + i + 1

        def peQ(i, j):
            return pe_b + 20 + 3 * i + j

        def vK(i, j):
            return v_b + 3 * i + j

        def vV(i):
            return v_b + 12 + i + 1

        def vQ(i, j):
            return v_b + 20 + 3 * i + j

        def sK(i, j):
            return s_b + 2 * i + j

        def sQ_(i, j):
            return s_b + 8 + 2 * i + j

        @blk.gpsimd
        def _(g):
            g.dma_start(
                out=wkv[:, 0:NCH * 512].rearrange("p (a b) -> p a b", b=512),
                in_=io["L0_attn_Wk"].rearrange("(c p) o -> p c o", p=128)
            ).then_inc(c.sG.h, 16)
            g.dma_start(
                out=wkv[:, NCH * 512:].rearrange("p (a b) -> p a b", b=512),
                in_=io["L0_attn_Wv"].rearrange("(c p) o -> p c o", p=128)
            ).then_inc(c.sG.h, 16)
            g.wait_ge(c.sV.h, vV(7))
            g.dma_start(
                out=kv_in[0:512, :].rearrange("(i p) t -> p i t", p=128),
                in_=kf[:].rearrange("p (i t) -> p i t", t=TOK)
            ).then_inc(c.sG.h, 16)
            g.dma_start(
                out=kv_in[512:1024, :].rearrange("(a r) (h f) -> (r h) a f",
                                                 a=8, h=2),
                in_=vt[:].rearrange("p (a f) -> p a f", f=512)
            ).then_inc(c.sG.h, 16)
            g.wait_ge(c.sG.h, gb + 16 * 4)
            g.collective_compute(
                "AllGather", ALU.bypass, replica_groups=GROUPS,
                ins=[kv_in], outs=[kv_out]).then_inc(c.sC.h, 1)
            for h in range(16):
                if h >= 2:
                    g.wait_ge(c.sP.h, peQ(h - 2, 1))
                g.dma_start(
                    out=wq_slots[h % 2][:].rearrange("p (a b) -> p a b", b=128),
                    in_=io["L0_attn_Wq"][:, h * 128:(h + 1) * 128]
                    .rearrange("(c p) o -> p c o", p=128)).then_inc(c.sG.h, 16)

        @blk.tensor
        def _(t):
            t.wait_ge(c.sG.h, gb + 16 * 2)
            for i in range(4):
                if i >= 2:
                    t.wait_ge(c.sV.h, vK(i - 2, 1))
                for ch in range(NCH):
                    for hh in range(2):
                        inst = t.matmul(
                            (pq0, pq1)[i % 2][:, hh * 512:(hh + 1) * 512],
                            wkv[:, ch * 512 + i * 128:ch * 512 + i * 128 + 128],
                            ln[:, ch * TOK + hh * 512:ch * TOK + (hh + 1) * 512],
                            start=(ch == 0), stop=(ch == NCH - 1))
                inst.then_inc(c.sP.h, 1)
                t.wait_ge(c.sS.h, sK(i, 1))
                if i >= 1:
                    t.wait_ge(c.sV.h, vK(i - 1, 2))
                for hh in range(2):
                    inst = t.matmul(psm[0:1, hh * 512:(hh + 1) * 512],
                                    kc["ones_col_bf16"][:],
                                    sqb[:, hh * 512:(hh + 1) * 512],
                                    start=True, stop=True)
                inst.then_inc(c.sP.h, 1)
                t.wait_ge(c.sS.h, sK(i, 2))
                if i >= 1:
                    t.wait_ge(c.sV.h, vK(i - 1, 3))
                for hh in range(2):
                    inst = t.matmul(pbc[:, hh * 512:(hh + 1) * 512],
                                    kc["ones_row_f32"][:],
                                    row[0:1, hh * 512:(hh + 1) * 512],
                                    start=True, stop=True)
                inst.then_inc(c.sP.h, 1)
            for a in range(8):
                pv = (pq0, pq1)[a % 2]
                if a >= 2:
                    t.wait_ge(c.sV.h, vV(a - 2))
                else:
                    t.wait_ge(c.sV.h, vK(2 + a, 1))
                for ch in range(NCH):
                    inst = t.matmul(
                        pv[:, 0:512],
                        ln[:, ch * TOK + a * 128:ch * TOK + a * 128 + 128],
                        wkv[:, NCH * 512 + ch * 512:NCH * 512 + (ch + 1) * 512],
                        start=(ch == 0), stop=(ch == NCH - 1))
                inst.then_inc(c.sP.h, 1)
            for h in range(16):
                pq = (pq0, pq1)[h % 2]
                t.wait_ge(c.sG.h, gb + 16 * (4 + h + 1))
                if h >= 2:
                    t.wait_ge(c.sV.h, vQ(h - 2, 1))
                else:
                    t.wait_ge(c.sV.h, vV(6 + h))
                for ch in range(NCH):
                    for hh in range(2):
                        inst = t.matmul(
                            pq[:, hh * 512:(hh + 1) * 512],
                            wq_slots[h % 2][:, ch * 128:(ch + 1) * 128],
                            ln[:, ch * TOK + hh * 512:ch * TOK + (hh + 1) * 512],
                            start=(ch == 0), stop=(ch == NCH - 1))
                inst.then_inc(c.sP.h, 1)
                t.wait_ge(c.sS.h, sQ_(h, 1))
                if h >= 1:
                    t.wait_ge(c.sV.h, vQ(h - 1, 2))
                for hh in range(2):
                    inst = t.matmul(psm[0:1, hh * 512:(hh + 1) * 512],
                                    kc["ones_col_bf16"][:],
                                    sqb[:, hh * 512:(hh + 1) * 512],
                                    start=True, stop=True)
                inst.then_inc(c.sP.h, 1)
                t.wait_ge(c.sS.h, sQ_(h, 2))
                if h >= 1:
                    t.wait_ge(c.sV.h, vQ(h - 1, 3))
                for hh in range(2):
                    inst = t.matmul(pbc[:, hh * 512:(hh + 1) * 512],
                                    kc["ones_row_f32"][:],
                                    row[0:1, hh * 512:(hh + 1) * 512],
                                    start=True, stop=True)
                inst.then_inc(c.sP.h, 1)

        @blk.vector
        def _(v):
            for i in range(4):
                v.wait_ge(c.sP.h, peK(i, 1))
                v.tensor_copy(kf[:, i * TOK:(i + 1) * TOK],
                              (pq0, pq1)[i % 2][:]).then_inc(c.sV.h, 1)
                v.wait_ge(c.sP.h, peK(i, 2))
                if i >= 1:
                    v.wait_ge(c.sS.h, sK(i - 1, 2))
                v.tensor_copy(row[:], psm[:]).then_inc(c.sV.h, 1)
                v.wait_ge(c.sP.h, peK(i, 3))
                v.tensor_mul(kf[:, i * TOK:(i + 1) * TOK],
                             kf[:, i * TOK:(i + 1) * TOK],
                             pbc[:]).then_inc(c.sV.h, 1)
            for a in range(8):
                v.wait_ge(c.sP.h, peV(a))
                v.tensor_copy(vt[:, a * 512:(a + 1) * 512],
                              (pq0, pq1)[a % 2][:, 0:512]).then_inc(c.sV.h, 1)
            for h in range(16):
                pq = (pq0, pq1)[h % 2]
                v.wait_ge(c.sP.h, peQ(h, 1))
                if h >= 2:
                    v.wait_ge(c.sW.h, w_b + 16 * (h - 1))
                v.tensor_copy(qt[:, (h % 2) * TOK:(h % 2 + 1) * TOK],
                              pq[:]).then_inc(c.sV.h, 1)
                v.wait_ge(c.sP.h, peQ(h, 2))
                if h >= 1:
                    v.wait_ge(c.sS.h, sQ_(h - 1, 2))
                v.tensor_copy(row[:], psm[:]).then_inc(c.sV.h, 1)
                v.wait_ge(c.sP.h, peQ(h, 3))
                v.tensor_mul(qt[:, (h % 2) * TOK:(h % 2 + 1) * TOK],
                             qt[:, (h % 2) * TOK:(h % 2 + 1) * TOK],
                             pbc[:]).then_inc(c.sV.h, 1)

        @blk.scalar
        def _(s):
            for i in range(4):
                s.wait_ge(c.sV.h, vK(i, 1))
                if i >= 1:
                    s.wait_ge(c.sP.h, peK(i - 1, 2))
                s.activation(sqb[:], kf[:, i * TOK:(i + 1) * TOK],
                             AF.Square).then_inc(c.sS.h, 1)
                s.wait_ge(c.sV.h, vK(i, 2))
                s.activation(row[:], row[:], AF.Ln, scale=1.0 / 128.0,
                             bias=kc["eps_a"][0:1, :])
                s.activation(row[:], row[:], AF.Exp,
                             scale=-0.5).then_inc(c.sS.h, 1)
            for h in range(16):
                s.wait_ge(c.sV.h, vQ(h, 1))
                if h >= 1:
                    s.wait_ge(c.sP.h, peQ(h - 1, 2))
                elif True:
                    s.wait_ge(c.sP.h, peK(3, 2))
                s.activation(sqb[:], qt[:, (h % 2) * TOK:(h % 2 + 1) * TOK],
                             AF.Square).then_inc(c.sS.h, 1)
                s.wait_ge(c.sV.h, vQ(h, 2))
                s.activation(row[:], row[:], AF.Ln, scale=1.0 / 128.0,
                             bias=kc["eps_a"][0:1, :])
                s.activation(row[:], row[:], AF.Exp,
                             scale=-0.5).then_inc(c.sS.h, 1)

        @blk.sync
        def _(s):
            for h in range(16):
                s.wait_ge(c.sV.h, vQ(h, 3))
                s.dma_start(out=qscr[h],
                            in_=qt[:, (h % 2) * TOK:(h % 2 + 1) * TOK]
                            ).then_inc(c.sW.h, 16)
        c.barrier()
        _stkA.close()

    with (
        nc.sbuf_tensor(uname("at_KF"), [128, 4 * 4096], BF16) as KF,
        nc.sbuf_tensor(uname("at_VF"), [128, 32 * 512], BF16) as VF,
        nc.sbuf_tensor(uname("at_qg"), [128, 8 * TOK], BF16) as qg,
        nc.sbuf_tensor(uname("at_OT"), [128, 8 * TOK], BF16) as OT,
        nc.sbuf_tensor(uname("at_PT"), [128, 2 * TOK], BF16) as PT,
        nc.sbuf_tensor(uname("at_bcs"), [128, TOK], F32) as bcs,
        nc.sbuf_tensor(uname("at_row2"), [1, TOK], F32) as row2,
        nc.sbuf_tensor(uname("at_wo0"), [128, 8 * 128], BF16) as wo0,
        nc.sbuf_tensor(uname("at_wo1"), [128, 8 * 128], BF16) as wo1,
        nc.psum_tensor(uname("at_st0"), [128, TOK], F32) as st0,
        nc.psum_tensor(uname("at_st1"), [128, TOK], F32) as st1,
        nc.psum_tensor(uname("at_po"), [128, TOK], F32) as po,
        nc.psum_tensor(uname("at_ps2"), [1, TOK], F32) as ps2,
    ):
        wo_slots = (wo0, wo1)
        kv_b = c.sW.reserve(16 * 8)

        @blk.sync
        def _(s):
            s.wait_ge(c.sC.h, c.sC.n)
            for m in range(4):
                s.dma_start(
                    out=KF[:].rearrange("p (i t) -> p i t", t=4096)
                    [:, :, m * TOK:(m + 1) * TOK],
                    in_=kv_out[m * 1024:m * 1024 + 512, :]
                    .rearrange("(i p) t -> p i t", p=128)).then_inc(c.sW.h, 16)
                s.dma_start(
                    out=VF[:].rearrange("p (a f) -> p a f", f=512)
                    [:, m * 8:(m + 1) * 8, :],
                    in_=kv_out[m * 1024 + 512:(m + 1) * 1024, :]
                    .rearrange("(a r) (h f) -> (r h) a f", a=8, h=2)
                    ).then_inc(c.sW.h, 16)

        for grp in range(2):
            qb2 = c.sW.reserve(16)
            pe_now = c.sP.n

            @blk.sync
            def _(s):
                s.wait_ge(c.sP.h, pe_now)
                s.dma_start(out=qg[:].rearrange("p (h t) -> p h t", t=TOK),
                            in_=qscr[grp * 8:(grp + 1) * 8]
                            .rearrange("h p t -> p h t")
                            ).then_inc(c.sW.h, 16)

            pe_b2 = c.sP.reserve(8 * 65)
            s_b2 = c.sS.reserve(8 * 32)
            v_b2 = c.sV.reserve(8 * 4)

            def peH(hl, j):
                return pe_b2 + 65 * hl + j

            def sH(hl, j):
                return s_b2 + 32 * hl + j

            def vH(hl, j):
                return v_b2 + 4 * hl + j

            @blk.tensor
            def _(t):
                t.wait_ge(c.sW.h, kv_b + 16 * 8)
                t.wait_ge(c.sW.h, qb2 + 16)
                t.wait_ge(c.sV.h, v_b2)
                for hl in range(8):
                    H = grp * 8 + hl
                    kvh = H // 4
                    KFh = KF[:].rearrange("p (i t) -> p i t", t=4096)
                    if hl >= 1:
                        t.wait_ge(c.sV.h, vH(hl - 1, 4))
                    for kt in range(32):
                        st = (st0, st1)[kt % 2]
                        if kt >= 2:
                            t.wait_ge(c.sS.h, sH(hl, kt - 1))
                        elif hl >= 1:
                            t.wait_ge(c.sS.h, sH(hl - 1, 31 + kt))
                        for hh in range(2):
                            inst = t.matmul(
                                st[:, hh * 512:(hh + 1) * 512],
                                KFh[:, kvh, kt * 128:(kt + 1) * 128],
                                qg[:, hl * TOK + hh * 512:hl * TOK + (hh + 1) * 512],
                                start=True, stop=True)
                        inst.then_inc(c.sP.h, 1)
                        t.wait_ge(c.sS.h, sH(hl, kt + 1))
                        VFt = VF[:].rearrange("p (a f) -> p a f", f=512)
                        for hh in range(2):
                            t.matmul(
                                po[:, hh * 512:(hh + 1) * 512],
                                VFt[:, kt, kvh * 128:(kvh + 1) * 128],
                                PT[:, (kt % 2) * TOK + hh * 512:
                                   (kt % 2) * TOK + (hh + 1) * 512],
                                start=(kt == 0), stop=(kt == 31))
                        for hh in range(2):
                            inst = t.matmul(
                                ps2[0:1, hh * 512:(hh + 1) * 512],
                                kc["ones_col_bf16"][:],
                                PT[:, (kt % 2) * TOK + hh * 512:
                                   (kt % 2) * TOK + (hh + 1) * 512],
                                start=(kt == 0), stop=(kt == 31))
                        inst.then_inc(c.sP.h, 1)
                    t.wait_ge(c.sV.h, vH(hl, 2))
                    for hh in range(2):
                        inst = t.matmul(st0[:, hh * 512:(hh + 1) * 512],
                                        kc["ones_row_f32"][:],
                                        row2[0:1, hh * 512:(hh + 1) * 512],
                                        start=True, stop=True)
                    inst.then_inc(c.sP.h, 1)

            @blk.scalar
            def _(s):
                for hl in range(8):
                    for kt in range(32):
                        s.wait_ge(c.sP.h, peH(hl, 2 * kt + 1))
                        st = (st0, st1)[kt % 2]
                        s.activation(PT[:, (kt % 2) * TOK:(kt % 2 + 1) * TOK],
                                     st[:], AF.Exp,
                                     scale=float(1.0 / np.sqrt(128.0))
                                     ).then_inc(c.sS.h, 1)

            @blk.vector
            def _(v):
                for hl in range(8):
                    v.wait_ge(c.sP.h, peH(hl, 64))
                    v.tensor_copy(row2[:], ps2[:]).then_inc(c.sV.h, 1)
                    v.reciprocal(row2[:], row2[:]).then_inc(c.sV.h, 1)
                    v.wait_ge(c.sP.h, peH(hl, 65))
                    v.tensor_copy(bcs[:], st0[:]).then_inc(c.sV.h, 1)
                    v.tensor_mul(OT[:, hl * TOK:(hl + 1) * TOK], po[:],
                                 bcs[:]).then_inc(c.sV.h, 1)

            ob = c.sG.reserve(16 * 16)
            pe_b3 = c.sP.reserve(16)
            v_b3 = c.sV.reserve(16)

            @blk.gpsimd
            def _(g):
                for ot in range(16):
                    if ot >= 2:
                        g.wait_ge(c.sP.h, pe_b3 + ot - 1)
                    g.dma_start(
                        out=wo_slots[ot % 2][:].rearrange("p (a b) -> p a b",
                                                          b=128),
                        in_=io["L0_attn_Wo"][grp * 1024:(grp + 1) * 1024,
                                             ot * 128:(ot + 1) * 128]
                        .rearrange("(c p) o -> p c o", p=128)
                        ).then_inc(c.sG.h, 16)

            @blk.tensor
            def _(t):
                t.wait_ge(c.sV.h, v_b2 + 32)
                for ot in range(16):
                    st = (st0, st1)[ot % 2]
                    t.wait_ge(c.sG.h, ob + 16 * (ot + 1))
                    if ot >= 2:
                        t.wait_ge(c.sV.h, v_b3 + ot - 1)
                    for ch in range(8):
                        for hh in range(2):
                            inst = t.matmul(
                                st[:, hh * 512:(hh + 1) * 512],
                                wo_slots[ot % 2][:, ch * 128:(ch + 1) * 128],
                                OT[:, ch * TOK + hh * 512:ch * TOK + (hh + 1) * 512],
                                start=(ch == 0), stop=(ch == 7))
                    inst.then_inc(c.sP.h, 1)

            @blk.vector
            def _(v):
                for ot in range(16):
                    v.wait_ge(c.sP.h, pe_b3 + ot + 1)
                    st = (st0, st1)[ot % 2]
                    v.tensor_add(res[:, ot * TOK:(ot + 1) * TOK],
                                 res[:, ot * TOK:(ot + 1) * TOK],
                                 st[:]).then_inc(c.sV.h, 1)
            c.barrier()
    c.barrier()


def emit_l1ffn(c, ln, wg_slot, wu_slot, wd_slot, G, FUSE, sdh):
    nc, blk = c.nc, c.blk
    io, kc, res = c.ios, c.bufs["KC"], c.bufs["RES"]
    NS = 16

    emit_ln(c, 0, 512, ln, ln_off=0, ln_stride=1024)
    emit_ln(c, 512, 512, ln, ln_off=512, ln_stride=1024)

    ps_ctx = (
        nc.psum_tensor(uname("f1_pg0"), [128, 1024], F32),
        nc.psum_tensor(uname("f1_pg1"), [128, 1024], F32),
        nc.psum_tensor(uname("f1_pu0"), [128, 1024], F32),
        nc.psum_tensor(uname("f1_pu1"), [128, 1024], F32),
    )
    c.ps_pg0 = c.stack2_enter(ps_ctx[0]) if False else None
    import contextlib
    _stk = contextlib.ExitStack()
    c.ps_pg0 = _stk.enter_context(ps_ctx[0])
    c.ps_pg1 = _stk.enter_context(ps_ctx[1])
    c.ps_pu0 = _stk.enter_context(ps_ctx[2])
    c.ps_pu1 = _stk.enter_context(ps_ctx[3])

    vb0 = c.sV.reserve(1)

    @blk.vector
    def _(v):
        v.tensor_scalar(sdh[:], kc["L1_ffn_s_down"][:], 0.5, None,
                        ALU.mult).then_inc(c.sV.h, 1)

    gbase = c.sG.reserve(16 * 3 * NS)
    pe_base = c.sP.reserve(24 * NS)
    sc_base = c.sS.reserve(4 * NS)
    v_base = c.sV.reserve(20 * NS)

    def pe_slice(j):
        return pe_base + 24 * (j + 1)

    def pe_gateup(j, k):
        return pe_base + 24 * j + k

    def pe_down(j, ot):
        return pe_base + 24 * j + 8 + ot + 1

    def v_slice(j):
        return v_base + 20 * (j + 1)

    def v_fuse(j, k):
        return v_base + 20 * j + k

    def v_down(j, ot):
        return v_base + 20 * j + 4 + ot + 1

    @blk.gpsimd
    def _(g):
        for j in range(NS):
            if j >= 2:
                g.wait_ge(c.sP.h, pe_slice(j - 2))
            g.dma_start(
                out=wg_slot[j % 2][:].rearrange("p (a b) -> p a b", b=512),
                in_=io["L1_ffn_w_gate"][:, j * 512:(j + 1) * 512]
                .rearrange("(c p) o -> p c o", p=128)).then_inc(c.sG.h, 16)
            g.dma_start(
                out=wu_slot[j % 2][:].rearrange("p (a b) -> p a b", b=512),
                in_=io["L1_ffn_w_up"][:, j * 512:(j + 1) * 512]
                .rearrange("(c p) o -> p c o", p=128)).then_inc(c.sG.h, 16)
            if j >= 1:
                g.wait_ge(c.sP.h, pe_slice(j - 1))
            g.dma_start(
                out=wd_slot[:].rearrange("p (a b) -> p a b", b=2048),
                in_=io["L1_ffn_w_down"][j * 512:(j + 1) * 512, :]
                .rearrange("(c p) o -> p c o", p=128)).then_inc(c.sG.h, 16)

    @blk.tensor
    def _(t):
        t.wait_ge(c.sV.h, vb0 + 1)
        for j in range(NS):
            wg, wu = wg_slot[j % 2], wu_slot[j % 2]
            t.wait_ge(c.sG.h, gbase + 16 * (3 * j + 2))
            if j >= 1:
                t.wait_ge(c.sV.h, v_slice(j - 1))
            for pt in range(4):
                pg = (c.ps_pg0, c.ps_pg1)[pt % 2]
                pu = (c.ps_pu0, c.ps_pu1)[pt % 2]
                if pt >= 2:
                    t.wait_ge(c.sS.h, sc_base + 4 * j + pt - 1)
                    t.wait_ge(c.sV.h, v_fuse(j, pt - 1))
                for ch in range(NCH):
                    for hh in range(2):
                        inst = t.matmul(
                            pg[:, hh * 512:(hh + 1) * 512],
                            wg[:, ch * 512 + pt * 128:ch * 512 + pt * 128 + 128],
                            ln[:, ch * TOK + hh * 512:ch * TOK + (hh + 1) * 512],
                            start=(ch == 0), stop=(ch == NCH - 1))
                inst.then_inc(c.sP.h, 1)
                for ch in range(NCH):
                    for hh in range(2):
                        inst = t.matmul(
                            pu[:, hh * 512:(hh + 1) * 512],
                            wu[:, ch * 512 + pt * 128:ch * 512 + pt * 128 + 128],
                            ln[:, ch * TOK + hh * 512:ch * TOK + (hh + 1) * 512],
                            start=(ch == 0), stop=(ch == NCH - 1))
                inst.then_inc(c.sP.h, 1)
            t.wait_ge(c.sG.h, gbase + 16 * (3 * j + 3))
            t.wait_ge(c.sV.h, v_fuse(j, 4))
            for ot in range(NCH):
                pd = (c.ps_pg0, c.ps_pg1)[ot % 2]
                if ot >= 2:
                    t.wait_ge(c.sV.h, v_down(j, ot - 2))
                for ic in range(4):
                    for hh in range(2):
                        inst = t.matmul(
                            pd[:, hh * 512:(hh + 1) * 512],
                            wd_slot[:, ic * 2048 + ot * 128:
                                    ic * 2048 + ot * 128 + 128],
                            FUSE[:, ic * TOK + hh * 512:
                                 ic * TOK + (hh + 1) * 512],
                            start=(ic == 0), stop=(ic == 3))
                inst.then_inc(c.sP.h, 1)

    @blk.scalar
    def _(s):
        for j in range(NS):
            for pt in range(4):
                s.wait_ge(c.sP.h, pe_gateup(j, 2 * pt + 1))
                k = 4 * j + pt
                if k >= 2:
                    jj, pp = divmod(k - 2, 4)
                    s.wait_ge(c.sV.h, v_fuse(jj, pp + 1))
                pg = (c.ps_pg0, c.ps_pg1)[pt % 2]
                s.activation(G[:, (pt % 2) * TOK:(pt % 2 + 1) * TOK], pg[:],
                             AF.Silu,
                             scale=kc["L1_ffn_s_gate"][:, k:k + 1]
                             ).then_inc(c.sS.h, 1)

    @blk.vector
    def _(v):
        for j in range(NS):
            for pt in range(4):
                v.wait_ge(c.sP.h, pe_gateup(j, 2 * pt + 2))
                v.wait_ge(c.sS.h, sc_base + 4 * j + pt + 1)
                pu = (c.ps_pu0, c.ps_pu1)[pt % 2]
                v.scalar_tensor_tensor(
                    FUSE[:, pt * TOK:(pt + 1) * TOK], pu[:],
                    kc["L1_ffn_s_up"][:, 4 * j + pt:4 * j + pt + 1],
                    G[:, (pt % 2) * TOK:(pt % 2 + 1) * TOK],
                    ALU.mult, ALU.mult).then_inc(c.sV.h, 1)
            for ot in range(NCH):
                v.wait_ge(c.sP.h, pe_down(j, ot))
                pd = (c.ps_pg0, c.ps_pg1)[ot % 2]
                v.scalar_tensor_tensor(
                    res[:, ot * TOK:(ot + 1) * TOK], pd[:],
                    sdh[:, ot:ot + 1],
                    res[:, ot * TOK:(ot + 1) * TOK],
                    ALU.mult, ALU.add).then_inc(c.sV.h, 1)
    c.barrier()
    _stk.close()




class Chain:

    def __init__(self, c):
        self.c = c
        self.tok = {"P": 0, "V": 0, "S": 0, "G": 0, "W": 0}
        self.seen = {k: {} for k in self.tok}

    def _waits(self, e, skip):
        c = self.c
        sems = {"P": c.sP, "V": c.sV, "S": c.sS, "G": c.sG, "W": c.sW}
        seen = self.seen[skip]
        for k, v in self.tok.items():
            if k != skip and v and seen.get(k, 0) < v:
                e.wait_ge(sems[k].h, v)
                seen[k] = v

    def p(self, fn):
        c = self.c

        def sec(t):
            self._waits(t, "P")
            inst = fn(t)
            inst.then_inc(c.sP.h, 1)
        c.blk.tensor(sec)
        c.sP.n += 1
        self.tok["P"] = c.sP.n

    def v(self, fn):
        c = self.c

        def sec(vv):
            self._waits(vv, "V")
            inst = fn(vv)
            inst.then_inc(c.sV.h, 1)
        c.blk.vector(sec)
        c.sV.n += 1
        self.tok["V"] = c.sV.n

    def s(self, fn):
        c = self.c

        def sec(ss):
            self._waits(ss, "S")
            inst = fn(ss)
            inst.then_inc(c.sS.h, 1)
        c.blk.scalar(sec)
        c.sS.n += 1
        self.tok["S"] = c.sS.n

    def g(self, fn, ndma=1):
        c = self.c

        def sec(gg):
            self._waits(gg, "G")
            inst = fn(gg)
            inst.then_inc(c.sG.h, 16)
        c.blk.gpsimd(sec)
        c.sG.n += 16 * ndma
        self.tok["G"] = c.sG.n

    def w(self, fn, ndma=1):
        c = self.c

        def sec(ss):
            self._waits(ss, "W")
            inst = fn(ss)
            inst.then_inc(c.sW.h, 16)
        c.blk.sync(sec)
        c.sW.n += 16 * ndma
        self.tok["W"] = c.sW.n


def emit_l1attn(c):
    nc, blk = c.nc, c.blk
    io, kc, res = c.ios, c.bufs["KC"], c.bufs["RES"]
    SEQ = 4096
    QS = float(128.0 ** -0.5)

    with (
        nc.sbuf_tensor(uname("la_ln"), [128, NCH * TOK], BF16) as ln,
        nc.sbuf_tensor(uname("la_w0"), [128, NCH * 128], BF16) as w0,
        nc.sbuf_tensor(uname("la_w1"), [128, NCH * 128], BF16) as w1,
        nc.sbuf_tensor(uname("la_mx0"), [128, TOK], BF16) as mx0,
        nc.sbuf_tensor(uname("la_mx1"), [128, TOK], BF16) as mx1,
        nc.sbuf_tensor(uname("la_ba"), [128, TOK], F32) as bav,
    ):
        emit_ln(c, 0, 512, ln, ln_off=0, ln_stride=TOK)
        emit_ln(c, 512, 512, ln, ln_off=512, ln_stride=TOK)
        wsl, msl = (w0, w1), (mx0, mx1)

        with (
            nc.psum_tensor(uname("la_pm0"), [128, TOK], F32) as pm0,
            nc.psum_tensor(uname("la_pm1"), [128, TOK], F32) as pm1,
            nc.psum_tensor(uname("la_pba"), [64, TOK], F32) as pba,
        ):
            psl = (pm0, pm1)
            gb = c.sG.reserve(16 * 97)
            pe_b = c.sP.reserve(97)
            v_b = c.sV.reserve(97)
            sw_b = c.sW.reserve(16 * 97)
            wq = io["L1_attn_W_qkvz"]

            def chcols(ch):
                if ch < 64:
                    hk, s_ = ch // 4, ch % 4
                    return 768 * hk + (s_ * 128 if s_ < 2
                                       else 256 + (s_ - 2) * 128)
                zz = ch - 64
                return 768 * (zz // 2) + 512 + (zz % 2) * 128

            @blk.gpsimd
            def _(g):
                for ch in range(96):
                    if ch >= 2:
                        g.wait_ge(c.sP.h, pe_b + ch - 1)
                    g.dma_start(
                        out=wsl[ch % 2][:].rearrange("p (a b) -> p a b", b=128),
                        in_=wq[:, chcols(ch):chcols(ch) + 128]
                        .rearrange("(cc p) o -> p cc o", p=128)
                        ).then_inc(c.sG.h, 16)
                g.wait_ge(c.sP.h, pe_b + 96)
                g.dma_start(
                    out=wsl[0][:, 0:16 * 64].rearrange("p (a b) -> p a b",
                                                       b=64),
                    in_=io["L1_attn_W_ba"]
                    .rearrange("(cc p) o -> p cc o", p=128)
                    ).then_inc(c.sG.h, 16)

            @blk.tensor
            def _(t):
                for ch in range(96):
                    pm = psl[ch % 2]
                    t.wait_ge(c.sG.h, gb + 16 * (ch + 1))
                    if ch >= 2:
                        t.wait_ge(c.sV.h, v_b + ch - 1)
                    for cc in range(NCH):
                        for hh in range(2):
                            inst = t.matmul(
                                pm[:, hh * 512:(hh + 1) * 512],
                                wsl[ch % 2][:, cc * 128:(cc + 1) * 128],
                                ln[:, cc * TOK + hh * 512:
                                   cc * TOK + (hh + 1) * 512],
                                start=(cc == 0), stop=(cc == NCH - 1))
                    inst.then_inc(c.sP.h, 1)
                t.wait_ge(c.sG.h, gb + 16 * 97)
                t.wait_ge(c.sV.h, v_b + 95)
                for cc in range(NCH):
                    for hh in range(2):
                        inst = t.matmul(
                            pba[:, hh * 512:(hh + 1) * 512],
                            wsl[0][:, cc * 64:(cc + 1) * 64],
                            ln[:, cc * TOK + hh * 512:
                               cc * TOK + (hh + 1) * 512],
                            start=(cc == 0), stop=(cc == NCH - 1))
                inst.then_inc(c.sP.h, 1)

            @blk.vector
            def _(v):
                for ch in range(96):
                    v.wait_ge(c.sP.h, pe_b + ch + 1)
                    if ch >= 2:
                        v.wait_ge(c.sW.h, sw_b + 16 * (ch - 1))
                    v.tensor_copy(msl[ch % 2][:],
                                  psl[ch % 2][:]).then_inc(c.sV.h, 1)
                v.wait_ge(c.sP.h, pe_b + 97)
                v.tensor_copy(bav[0:64, :], pba[:]).then_inc(c.sV.h, 1)

            @blk.sync
            def _(s):
                for ch in range(96):
                    s.wait_ge(c.sV.h, v_b + ch + 1)
                    s.dma_start(
                        out=io["cc1_in"][ch * 64:(ch + 1) * 64, :]
                        .rearrange("r (a k) -> (r a) k", a=2),
                        in_=msl[ch % 2][:]).then_inc(c.sW.h, 16)
                s.wait_ge(c.sV.h, v_b + 97)
                s.dma_start(out=io["ba_scr"], in_=bav[0:64, :]
                            ).then_inc(c.sW.h, 16)
        c.barrier()

        with (
            nc.sbuf_tensor(uname("la_bg"), [64, TOK], F32) as bg,
            nc.sbuf_tensor(uname("la_na"), [32, 1], F32) as negA,
        ):
            ch0 = Chain(c)
            def split_ba(s):
                for h in range(16):
                    inst = s.dma_start(out=bg[2 * h:2 * h + 2, :],
                                       in_=io["ba_scr"][4 * h:4 * h + 2, :])
                    if h < 15:
                        inst.then_inc(c.sW.h, 16)
                return inst
            ch0.w(split_ba, ndma=16)

            def split_ba2(s):
                for h in range(16):
                    inst = s.dma_start(
                        out=bg[32 + 2 * h:32 + 2 * h + 2, :],
                        in_=io["ba_scr"][4 * h + 2:4 * h + 4, :])
                    if h < 15:
                        inst.then_inc(c.sW.h, 16)
                return inst
            ch0.w(split_ba2, ndma=16)
            ch0.s(lambda s: s.activation(negA[:], kc["L1_attn_A_log"][:],
                                         AF.Exp))
            ch0.v(lambda v: v.tensor_scalar(negA[:], negA[:], -1.0, None,
                                            ALU.mult))
            ch0.s(lambda s: s.activation(bg[0:32, :], bg[0:32, :], AF.Exp,
                                         scale=-1.0))
            ch0.v(lambda v: v.tensor_scalar(bg[0:32, :], bg[0:32, :], 1.0,
                                            None, ALU.add))
            ch0.v(lambda v: v.reciprocal(bg[0:32, :], bg[0:32, :]))
            ch0.v(lambda v: v.tensor_scalar(bg[32:64, :], bg[32:64, :],
                                            kc["L1_attn_dt_bias"][:], None,
                                            ALU.add))
            ch0.s(lambda s: s.activation(bav[32:64, :], bg[32:64, :],
                                         AF.Abs))
            ch0.s(lambda s: s.activation(bav[32:64, :], bav[32:64, :],
                                         AF.Exp, scale=-1.0))
            ch0.s(lambda s: s.activation(bav[32:64, :], bav[32:64, :], AF.Ln,
                                         bias=kc["one_col"][0:32, :]))
            ch0.v(lambda v: v.tensor_scalar_max(bg[32:64, :], bg[32:64, :],
                                                0.0))
            ch0.v(lambda v: v.tensor_add(bg[32:64, :], bg[32:64, :],
                                         bav[32:64, :]))
            ch0.v(lambda v: v.tensor_scalar(bg[32:64, :], bg[32:64, :],
                                            negA[0:32, :], None, ALU.mult))
            ch0.w(lambda s: s.dma_start(out=io["ccbg_in"], in_=bg[0:64, :]))
        c.barrier()

    ccb = c.sC.reserve(2)

    @blk.gpsimd
    def _(g):
        g.collective_compute(
            "AllGather", ALU.bypass, replica_groups=[list(range(8))],
            ins=[io["cc1_in"]], outs=[io["cc1_out"]]).then_inc(c.sC.h, 1)
        g.collective_compute(
            "AllGather", ALU.bypass, replica_groups=[list(range(8))],
            ins=[io["ccbg_in"]], outs=[io["ccbg_out"]]).then_inc(c.sC.h, 1)
    c.barrier()

    cc1f = io["cc1_out"].rearrange("r (a k) -> (r a) k", a=2)
    ccbgf = io["ccbg_out"]
    cwf = io["L1_attn_conv_w"]
    bascrf = io["ba_scr"].rearrange("r (k one) -> (r k) one", one=1)

    import contextlib as _ctl2
    _stk2 = _ctl2.ExitStack()
    QN = _stk2.enter_context(nc.sbuf_tensor(uname("lb_QN"), [128, SEQ], BF16))
    KN = _stk2.enter_context(nc.sbuf_tensor(uname("lb_KN"), [128, SEQ], BF16))
    V0 = _stk2.enter_context(nc.sbuf_tensor(uname("lb_V0"), [128, SEQ], BF16))
    V1 = _stk2.enter_context(nc.sbuf_tensor(uname("lb_V1"), [128, SEQ], BF16))
    Z0 = _stk2.enter_context(nc.sbuf_tensor(uname("lb_Z0"), [128, SEQ], BF16))
    Z1 = _stk2.enter_context(nc.sbuf_tensor(uname("lb_Z1"), [128, SEQ], BF16))
    OG0 = _stk2.enter_context(nc.sbuf_tensor(uname("lb_OG0"), [128, SEQ], BF16))
    OG1 = _stk2.enter_context(nc.sbuf_tensor(uname("lb_OG1"), [128, SEQ], BF16))
    PAD = _stk2.enter_context(nc.sbuf_tensor(uname("lb_PAD"), [128, SEQ + 3], BF16))
    C32 = _stk2.enter_context(nc.sbuf_tensor(uname("lb_C32"), [128, 512], F32))
    CB = _stk2.enter_context(nc.sbuf_tensor(uname("lb_CB"), [128, 512], F32))
    SQB = _stk2.enter_context(nc.sbuf_tensor(uname("lb_SQ"), [128, 1024], BF16))
    GROW = _stk2.enter_context(nc.sbuf_tensor(uname("lb_GROW"), [1, SEQ], F32))
    BROW = _stk2.enter_context(nc.sbuf_tensor(uname("lb_BROW"), [1, SEQ], F32))
    TROW = _stk2.enter_context(nc.sbuf_tensor(uname("lb_TROW"), [1, 1024], F32))
    CW = _stk2.enter_context(nc.sbuf_tensor(uname("lb_CW"), [128, 4], F32))
    T32 = _stk2.enter_context(nc.sbuf_tensor(uname("lb_T32"), [32, 128], F32))
    GT = _stk2.enter_context(nc.sbuf_tensor(uname("lb_GT"), [128, 2 * 6 * 32], F32))
    NWB = _stk2.enter_context(nc.sbuf_tensor(uname("lb_NWB"), [128, 128], F32))
    SC = _stk2.enter_context(nc.sbuf_tensor(uname("lb_SC"), [128, 2 * 2048], BF16))
    SBF = _stk2.enter_context(nc.sbuf_tensor(uname("lb_SBF"), [128, 2 * 128], BF16))
    S32 = _stk2.enter_context(nc.sbuf_tensor(uname("lb_S32"), [128, 2 * 128], F32))
    O32 = _stk2.enter_context(nc.sbuf_tensor(uname("lb_O32"), [128, 2 * 128], F32))
    ZSC = _stk2.enter_context(nc.sbuf_tensor(uname("lb_ZSC"), [128, 2 * 128], F32))
    R1 = _stk2.enter_context(nc.sbuf_tensor(uname("lb_R1"), [128, 4], F32))
    ROW4 = _stk2.enter_context(nc.sbuf_tensor(uname("lb_R4"), [4, 1024], F32))
    pT0 = _stk2.enter_context(nc.psum_tensor(uname("lb_pT0"), [128, 640], BF16))
    pT1 = _stk2.enter_context(nc.psum_tensor(uname("lb_pT1"), [128, 640], BF16))
    pA0 = _stk2.enter_context(nc.psum_tensor(uname("lb_pA0"), [128, 512], F32))
    pA1 = _stk2.enter_context(nc.psum_tensor(uname("lb_pA1"), [128, 512], F32))
    pB0 = _stk2.enter_context(nc.psum_tensor(uname("lb_pB0"), [128, 512], F32))
    pB1 = _stk2.enter_context(nc.psum_tensor(uname("lb_pB1"), [128, 512], F32))
    if True:

        pTs, pAs, pBs = (pT0, pT1), (pA0, pA1), (pB0, pB1)
        Vs, Zs, OGs = (V0, V1), (Z0, Z1), (OG0, OG1)
        GTW = 6 * 32

        def gtc(s_, which, n):
            o = s_ * GTW + which * 32 + n
            return GT[:, o:o + 1]

        def gta(s_, which):
            o = s_ * GTW + which * 32
            return GT[:, o:o + 32]

        def sc(s_, slot, w=128):
            o = s_ * 2048 + slot * 128
            return SC[:, o:o + w]

        gidx1, gidx2, gidx3 = kc["g1idx"], kc["g2idx"], kc["g3idx"]

        chN = Chain(c)
        chN.p(lambda t: t.matmul(pA0[:, 0:128], kc["ones_row_f32"][:],
                                 kc["L1_attn_norm_w"][:], start=True,
                                 stop=True))
        chN.v(lambda v: v.tensor_copy(NWB[:], pA0[:, 0:128]))
        chN.v(lambda v: v.memset(PAD[:, 0:3], 0.0))
        c.barrier()

        def load_piece(ch, colbase, cwcol, dst):
            if cwcol is not None:
                ch.g(lambda g: g.indirect_dma_start(
                    out=CW[:], out_offset=None, in_=cwf,
                    in_offset=IndirectOffsetOnAxis(
                        gidx3[:, cwcol:cwcol + 1], 0)))
            for m in range(4):
                ch.g(lambda g, m=m: g.indirect_dma_start(
                    out=dst[:, m * 1024:(m + 1) * 1024], out_offset=None,
                    in_=cc1f, in_offset=IndirectOffsetOnAxis(
                        gidx1[:, colbase + m:colbase + m + 1], 0)))

        def conv_silu(ch, dst):
            W_ = 512
            for i in range(SEQ // W_):
                o = 3 + i * W_

                def cv(v, o=o):
                    v.tensor_scalar(C32[:], PAD[:, o:o + W_], CW[:, 3:4],
                                    None, ALU.mult)
                    v.scalar_tensor_tensor(C32[:], PAD[:, o - 1:o - 1 + W_],
                                           CW[:, 2:3], C32[:], ALU.mult,
                                           ALU.add)
                    v.scalar_tensor_tensor(C32[:], PAD[:, o - 2:o - 2 + W_],
                                           CW[:, 1:2], C32[:], ALU.mult,
                                           ALU.add)
                    return v.scalar_tensor_tensor(
                        C32[:], PAD[:, o - 3:o - 3 + W_], CW[:, 0:1], C32[:],
                        ALU.mult, ALU.add)
                ch.v(cv)
                ch.s(lambda s: s.activation(CB[:], C32[:], AF.Exp,
                                            scale=-1.0))
                ch.v(lambda v: v.tensor_scalar(CB[:], CB[:], 1.0, None,
                                               ALU.add))
                ch.v(lambda v: v.reciprocal(CB[:], CB[:]))
                ch.v(lambda v, i=i: v.tensor_mul(
                    dst[:, i * W_:(i + 1) * W_], C32[:], CB[:]))

        def l2norm_piece(ch, dst, qscale):
            for q4 in range(4):
                ch.s(lambda s, q4=q4: s.activation(
                    SQB[:], dst[:, q4 * 1024:(q4 + 1) * 1024], AF.Square))

                def red(t, q4=q4):
                    t.matmul(pA0[0:1, 0:512], kc["ones_col_bf16"][:],
                             SQB[:, 0:512], start=True, stop=True)
                    return t.matmul(pB0[0:1, 0:512], kc["ones_col_bf16"][:],
                                    SQB[:, 512:1024], start=True, stop=True)
                ch.p(red)
                ch.v(lambda v: v.tensor_copy(TROW[0:1, 0:512],
                                             pA0[0:1, 0:512]))
                ch.v(lambda v: v.tensor_copy(TROW[0:1, 512:1024],
                                             pB0[0:1, 0:512]))
                ch.s(lambda s: s.activation(TROW[0:1, 0:1024],
                                            TROW[0:1, 0:1024], AF.Ln,
                                            bias=kc["eps_a"][0:1, :]))
                ch.s(lambda s: s.activation(TROW[0:1, 0:1024],
                                            TROW[0:1, 0:1024], AF.Exp,
                                            scale=-0.5))
                if qscale:
                    ch.v(lambda v: v.tensor_scalar(TROW[0:1, 0:1024],
                                                   TROW[0:1, 0:1024], QS,
                                                   None, ALU.mult))
                for j in range(2):
                    ch.p(lambda t, j=j: t.matmul(
                        pB0[:, 0:512], kc["ones_row_f32"][:],
                        TROW[0:1, j * 512:(j + 1) * 512], start=True,
                        stop=True))
                    ch.v(lambda v, q4=q4, j=j: v.tensor_mul(
                        dst[:, q4 * 1024 + j * 512:q4 * 1024 + (j + 1) * 512],
                        dst[:, q4 * 1024 + j * 512:q4 * 1024 + (j + 1) * 512],
                        pB0[:, 0:512]))

        def unit_rows(ch, s_, u):
            ch.g(lambda g: g.indirect_dma_start(
                out=ROW4[:], out_offset=None, in_=ccbgf,
                in_offset=IndirectOffsetOnAxis(gidx2[:, 8 + u:9 + u], 0)))
            ch.w(lambda s: s.dma_start(out=io["ba_scr"][0:4, :],
                                       in_=ROW4[:]))
            ch.w(lambda s: s.dma_start(
                out=GROW.rearrange("o (r t) -> o r t", r=4),
                in_=io["ba_scr"][0:4, :]
                .rearrange("(one r) t -> one r t", one=1)))
            ch.g(lambda g: g.indirect_dma_start(
                out=ROW4[:], out_offset=None, in_=ccbgf,
                in_offset=IndirectOffsetOnAxis(gidx2[:, u:u + 1], 0)))
            ch.w(lambda s: s.dma_start(out=io["ba_scr"][4:8, :],
                                       in_=ROW4[:]))
            ch.w(lambda s: s.dma_start(
                out=BROW.rearrange("o (r t) -> o r t", r=4),
                in_=io["ba_scr"][4:8, :]
                .rearrange("(one r) t -> one r t", one=1)))

            def scan(v):
                for n in range(32):
                    inst = v.tensor_tensor_scan(
                        GROW[0:1, n * 128:(n + 1) * 128],
                        GROW[0:1, n * 128:(n + 1) * 128],
                        GROW[0:1, n * 128:(n + 1) * 128],
                        0.0, ALU.add, ALU.bypass)
                return inst
            ch.v(scan)
            ch.w(lambda s: s.dma_start(
                out=io["ba_scr"][0:4, :].rearrange("(one r) t -> one r t",
                                                   one=1),
                in_=GROW.rearrange("o (r t) -> o r t", r=4)))
            ch.w(lambda s: s.dma_start(
                out=io["ba_scr"][4:8, :].rearrange("(one r) t -> one r t",
                                                   one=1),
                in_=BROW.rearrange("o (r t) -> o r t", r=4)))
            ch.w(lambda s: s.dma_start(
                out=T32[:], in_=io["ba_scr"][0:4, :]
                .rearrange("(a1) (a2 b) -> (a1 a2) b", a2=8)))
            ch.p(lambda t: t.transpose(pAs[s_][:, 0:32], T32[:],
                                       kc["ident_f32"][0:32, 0:32]))
            ch.v(lambda v, s_=s_: v.tensor_copy(gta(s_, 0), pAs[s_][:, 0:32]))
            ch.w(lambda s: s.dma_start(
                out=T32[:], in_=io["ba_scr"][4:8, :]
                .rearrange("(a1) (a2 b) -> (a1 a2) b", a2=8)))
            ch.p(lambda t, s_=s_: t.transpose(pAs[s_][:, 0:32], T32[:],
                                              kc["ident_f32"][0:32, 0:32]))
            ch.v(lambda v, s_=s_: v.tensor_copy(gta(s_, 1), pAs[s_][:, 0:32]))
            ch.s(lambda s, s_=s_: s.activation(gta(s_, 2), gta(s_, 0),
                                               AF.Exp))
            ch.v(lambda v, s_=s_: v.tensor_mul(gta(s_, 4), gta(s_, 1),
                                               gta(s_, 2)))
            gv = GROW.rearrange("o (n t) -> o n t", t=128)
            ch.s(lambda s: s.activation(TROW[0:1, 0:32],
                                        gv[0:1, :, 127:128]
                                        .rearrange("o n t -> o (n t)"),
                                        AF.Exp))
            ch.v(lambda v: v.tensor_copy(TROW[0:1, 32:64],
                                         gv[0:1, :, 127:128]
                                         .rearrange("o n t -> o (n t)")))

            def bc_gl(t, s_=s_):
                t.matmul(pAs[s_][:, 0:32], kc["ones_row_f32"][:],
                         TROW[0:1, 0:32], start=True, stop=True)
                return t.matmul(pAs[s_][:, 32:64], kc["ones_row_f32"][:],
                                TROW[0:1, 32:64], start=True, stop=True)
            ch.p(bc_gl)
            ch.v(lambda v, s_=s_: v.tensor_copy(gta(s_, 5), pAs[s_][:, 0:32]))
            ch.v(lambda v, s_=s_: v.tensor_copy(gta(s_, 3),
                                                pAs[s_][:, 32:64]))
            ch.v(lambda v, s_=s_: v.memset(
                S32[:, s_ * 128:(s_ + 1) * 128], 0.0))
            ch.v(lambda v, s_=s_: v.memset(
                SBF[:, s_ * 128:(s_ + 1) * 128], 0.0))

        def unit_chunk(ch, s_, n):
            pA, pB, pT = pAs[s_], pBs[s_], pTs[s_]
            Vv, Zz, OGg = Vs[s_], Zs[s_], OGs[s_]
            nsl = slice(n * 128, (n + 1) * 128)
            Sb = SBF[:, s_ * 128:(s_ + 1) * 128]
            Sf = S32[:, s_ * 128:(s_ + 1) * 128]
            Of = O32[:, s_ * 128:(s_ + 1) * 128]
            Zf = ZSC[:, s_ * 128:(s_ + 1) * 128]
            r1 = R1[:, s_:s_ + 1]

            kd = R1[:, 2 + s_:3 + s_]
            Dm = Zf
            D2m = Of
            Dx = ZSC[:, (1 - s_) * 128:(2 - s_) * 128]

            ch.s(lambda s: s.activation(TROW[0:1, 0:128], GROW[0:1, nsl],
                                        AF.Exp))
            ch.v(lambda v: v.tensor_mul(TROW[0:1, 256:384], BROW[0:1, nsl],
                                        TROW[0:1, 0:128]))

            def bc4(t):
                t.matmul(pA[:, 0:128], kc["ones_row_f32"][:],
                         TROW[0:1, 0:128], start=True, stop=True)
                t.matmul(pA[:, 128:256], kc["ones_row_f32"][:],
                         BROW[0:1, nsl], start=True, stop=True)
                t.matmul(pA[:, 256:384], kc["ones_row_f32"][:],
                         TROW[0:1, 256:384], start=True, stop=True)
                return t.matmul(pA[:, 384:512], kc["ones_row_f32"][:],
                                GROW[0:1, nsl], start=True, stop=True)
            ch.p(bc4)
            ch.v(lambda v: v.tensor_mul(sc(s_, 9), QN[:, nsl], pA[:, 0:128]))
            ch.v(lambda v: v.tensor_mul(sc(s_, 10), KN[:, nsl],
                                        pA[:, 128:256]))
            ch.v(lambda v: v.tensor_mul(sc(s_, 11), KN[:, nsl],
                                        pA[:, 256:384]))
            ch.v(lambda v: v.tensor_scalar(Dm, pA[:, 384:512],
                                           gtc(s_, 0, n), None,
                                           ALU.subtract))
            ch.v(lambda v: v.tensor_scalar_max(D2m, Dm, 0.0))
            ch.v(lambda v: v.tensor_scalar(Dm, Dm, 0.0, None, ALU.min))
            ch.s(lambda s: s.activation(Dm, Dm, AF.Exp))
            ch.s(lambda s: s.activation(D2m, D2m, AF.Exp, scale=-1.0))
            ch.p(lambda t: t.transpose(pT[:, 0:128], KN[:, nsl],
                                       kc["ident_bf16"][:]))
            ch.p(lambda t: t.transpose(pT[:, 128:256], Vv[:, nsl],
                                       kc["ident_bf16"][:]))
            ch.v(lambda v: v.tensor_scalar(sc(s_, 0), pT[:, 128:256],
                                           gtc(s_, 1, n), None, ALU.mult))
            ch.v(lambda v: v.tensor_scalar(sc(s_, 1), pT[:, 0:128],
                                           gtc(s_, 4, n), None, ALU.mult))
            ch.v(lambda v: v.tensor_sub(kd, gtc(s_, 3, n), gtc(s_, 0, n)))
            ch.s(lambda s: s.activation(kd, kd, AF.Exp))
            ch.v(lambda v: v.tensor_scalar(sc(s_, 5), pT[:, 0:128], kd,
                                           None, ALU.mult))
            ch.p(lambda t: t.matmul(pA[:, 384:512], KN[:, nsl], sc(s_, 10),
                                    start=True, stop=True))
            ch.v(lambda v: v.tensor_mul(Dx, pA[:, 384:512], Dm))
            ch.v(lambda v: v.tensor_mul(sc(s_, 2), Dx,
                                        kc["mask_su_neg"][:]))
            ch.p(lambda t: t.matmul(pB[:, 0:128], sc(s_, 10), KN[:, nsl],
                                    start=True, stop=True))
            ch.v(lambda v: v.tensor_mul(Dx, pB[:, 0:128], D2m))
            ch.v(lambda v: v.tensor_mul(sc(s_, 3), Dx,
                                        kc["mask_sl_neg"][:]))
            ch.p(lambda t: t.matmul(pB[:, 128:256], KN[:, nsl], QN[:, nsl],
                                    start=True, stop=True))
            ch.v(lambda v: v.tensor_mul(Dx, pB[:, 128:256], Dm))
            ch.v(lambda v: v.tensor_mul(sc(s_, 4), Dx,
                                        kc["mask_u"][:]))
            ch.v(lambda v: v.tensor_copy(sc(s_, 6), sc(s_, 3)))
            ch.v(lambda v: v.tensor_copy(sc(s_, 7), sc(s_, 2)))
            ch.p(lambda t: t.matmul(pA[:, 0:256], sc(s_, 2), sc(s_, 0, 256),
                                    start=True, stop=True))
            ch.v(lambda v: v.tensor_add(sc(s_, 0, 256), sc(s_, 0, 256),
                                        pA[:, 0:256]))
            for lv in range(6):
                def sq(t, lv=lv):
                    inst = t.matmul(pB[:, 256:384], sc(s_, 7), sc(s_, 6),
                                    start=True, stop=True)
                    if lv < 5:
                        inst = t.matmul(pB[:, 384:512], sc(s_, 6), sc(s_, 7),
                                        start=True, stop=True)
                    return inst
                ch.p(sq)
                ch.v(lambda v: v.tensor_copy(sc(s_, 6), pB[:, 256:384]))
                if lv < 5:
                    ch.v(lambda v: v.tensor_copy(sc(s_, 7), pB[:, 384:512]))
                    ch.p(lambda t: t.matmul(pA[:, 0:256], sc(s_, 7),
                                            sc(s_, 0, 256), start=True,
                                            stop=True))
                else:
                    ch.p(lambda t: t.matmul(pA[:, 0:256], sc(s_, 6),
                                            sc(s_, 0, 256), start=True,
                                            stop=True))
                ch.v(lambda v: v.tensor_add(sc(s_, 0, 256), sc(s_, 0, 256),
                                            pA[:, 0:256]))
            ch.p(lambda t: t.transpose(pT[:, 256:384], sc(s_, 1),
                                       kc["ident_bf16"][:]))
            ch.v(lambda v: v.tensor_copy(sc(s_, 8), pT[:, 256:384]))
            ch.p(lambda t: t.matmul(pA[:, 384:512], sc(s_, 8), Sb,
                                    start=True, stop=True))
            ch.v(lambda v: v.tensor_sub(sc(s_, 15), sc(s_, 0),
                                        pA[:, 384:512]))
            def omm(t):
                t.matmul(pB[:, 256:384], sc(s_, 9), Sb, start=True,
                         stop=False)
                return t.matmul(pB[:, 256:384], sc(s_, 4), sc(s_, 15),
                                start=False, stop=True)
            ch.p(omm)
            ch.p(lambda t: t.matmul(pB[:, 384:512], sc(s_, 5), sc(s_, 15),
                                    start=True, stop=True))
            ch.v(lambda v: v.scalar_tensor_tensor(Sf, Sf, gtc(s_, 5, n),
                                                  pB[:, 384:512],
                                                  ALU.mult, ALU.add))
            ch.v(lambda v: v.tensor_copy(Sb, Sf))
            ch.p(lambda t: t.transpose(pT[:, 384:512], Zz[:, nsl],
                                       kc["ident_bf16"][:]))
            ch.s(lambda s: s.activation(Zf, pT[:, 384:512], AF.Exp,
                                        scale=-1.0))
            ch.v(lambda v: v.tensor_scalar(Zf, Zf, 1.0, None, ALU.add))
            ch.v(lambda v: v.reciprocal(Zf, Zf))
            ch.v(lambda v: v.tensor_mul(sc(s_, 12), pT[:, 384:512], Zf))
            ch.v(lambda v: v.tensor_mul(Of, pB[:, 256:384], sc(s_, 12)))
            ch.s(lambda s: s.activation(Zf, Of, AF.Square))
            ch.v(lambda v: v.reduce_sum(r1, Zf, axis=AX.X))
            ch.s(lambda s: s.activation(r1, r1, AF.Ln, scale=1.0 / 128.0,
                                        bias=kc["eps_a"][:]))
            ch.s(lambda s: s.activation(r1, r1, AF.Exp, scale=-0.5))
            ch.v(lambda v: v.scalar_tensor_tensor(sc(s_, 13), Of, r1,
                                                  NWB[:], ALU.mult,
                                                  ALU.mult))
            ch.p(lambda t: t.transpose(pT[:, 512:640], sc(s_, 13),
                                       kc["ident_bf16"][:]))
            ch.v(lambda v: v.tensor_copy(OGg[:, nsl], pT[:, 512:640]))

        def dbg_dump(tile, width=SEQ):
            c.barrier()
            chD = Chain(c)
            chD.v(lambda v: v.tensor_copy(res[:, 0:width], tile[:, 0:width]))
            c.barrier()
            _stk2.close()

        if L1DBG[0] in ("cc1", "cc1o"):
            chD = Chain(c)
            srcap = (io["cc1_in"] if L1DBG[0] == "cc1" else io["cc1_out"])
            chD.w(lambda s: s.dma_start(
                out=PAD[:, 0:1024],
                in_=srcap[0:64, :].rearrange("r (a k) -> (r a) k", a=2)))
            chD.v(lambda v: v.tensor_copy(res[:, 0:1024], PAD[:, 0:1024]))
            c.barrier()
            _stk2.close()
            return
        for hkl in range(4):
            chL = Chain(c)
            load_piece(chL, 4 * hkl, hkl, PAD[:, 3:3 + SEQ])
            if L1DBG[0] == "raw":
                dbg_dump(PAD[:, 3:3 + SEQ])
                return
            conv_silu(chL, QN)
            if L1DBG[0] == "conv":
                dbg_dump(QN)
                return
            l2norm_piece(chL, QN, qscale=True)
            if L1DBG[0] == "qn":
                dbg_dump(QN)
                return
            load_piece(chL, 16 + 4 * hkl, 4 + hkl, PAD[:, 3:3 + SEQ])
            conv_silu(chL, KN)
            l2norm_piece(chL, KN, qscale=False)
            for s_ in range(2):
                u = 2 * hkl + s_
                load_piece(chL, 32 + 4 * u, 8 + u, PAD[:, 3:3 + SEQ])
                conv_silu(chL, Vs[s_])
                load_piece(chL, 64 + 4 * u, None, Zs[s_][:, 0:SEQ])
            c.barrier()
            for s_ in range(2):
                chU = Chain(c)
                unit_rows(chU, s_, 2 * hkl + s_)
                c.barrier()
                if L1DBG[0] == "rows" and s_ == 0:
                    dbg_dump(GT, width=GTW)
                    return
                for n in range(32):
                    unit_chunk(chU, s_, n)
                    if n == 0 and s_ == 0 and L1DBG[0] == "ck0":
                        c.barrier()
                        chD = Chain(c)
                        chD.v(lambda v: v.tensor_copy(res[:, 0:2048],
                                                      SC[:, 0:2048]))
                        chD.v(lambda v: v.tensor_copy(res[:, 2048:2176],
                                                      S32[:, 0:128]))
                        chD.v(lambda v: v.tensor_copy(res[:, 2176:2304],
                                                      OG0[:, 0:128]))
                        c.barrier()
                        _stk2.close()
                        return
                c.barrier()
            wspill = c.sW.reserve(2 * 16)

            @blk.sync
            def _(s, hkl=hkl):
                for s_ in range(2):
                    u = 2 * hkl + s_
                    s.dma_start(
                        out=io["cc2_in"][u * 256:(u + 1) * 256, :]
                        .rearrange("r (a k) -> (r a) k", a=2),
                        in_=OGs[s_][:]).then_inc(c.sW.h, 16)
            c.barrier()

    _stk2.close()
    cc2b = c.sC.reserve(1)

    @blk.gpsimd
    def _(g):
        g.collective_compute(
            "AllGather", ALU.bypass, replica_groups=[list(range(8))],
            ins=[io["cc2_in"]], outs=[io["cc2_out"]]).then_inc(c.sC.h, 1)
    c.barrier()

    cc2f = io["cc2_out"].rearrange("r (a k) -> (r a) k", a=2)
    with (
        nc.sbuf_tensor(uname("lc_og"), [128, 8 * TOK], BF16) as OGC,
        nc.sbuf_tensor(uname("lc_wo"), [128, 8 * 2048], BF16) as WOF,
        nc.psum_tensor(uname("lc_py0"), [128, TOK], F32) as py0,
        nc.psum_tensor(uname("lc_py1"), [128, TOK], F32) as py1,
        nc.psum_tensor(uname("lc_py2"), [128, TOK], F32) as py2,
    ):
        gidx4 = kc["g4idx"]
        pys = (py0, py1, py2)
        for grp in range(4):
            gb4 = c.sG.reserve(16 * 16)
            pe4 = c.sP.reserve(16)
            v4 = c.sV.reserve(16)

            @blk.gpsimd
            def _(g, grp=grp):
                for i in range(8):
                    hv = grp * 8 + i
                    g.indirect_dma_start(
                        out=OGC[:, i * TOK:(i + 1) * TOK], out_offset=None,
                        in_=cc2f, in_offset=IndirectOffsetOnAxis(
                            gidx4[:, hv:hv + 1], 0)).then_inc(c.sG.h, 16)
                for i in range(8):
                    hv = grp * 8 + i
                    g.dma_start(
                        out=WOF[:, i * 2048:(i + 1) * 2048]
                        .rearrange("p (a b) -> p a b", b=128),
                        in_=io["L1_attn_W_out"][hv * 128:(hv + 1) * 128, :]
                        .rearrange("(cc p) o -> p cc o", p=128)
                        ).then_inc(c.sG.h, 16)

            @blk.tensor
            def _(t, grp=grp, gb4=gb4, v4=v4):
                t.wait_ge(c.sG.h, gb4 + 16 * 16)
                for ot in range(NCH):
                    py = pys[ot % 3]
                    if ot >= 3:
                        t.wait_ge(c.sV.h, v4 + ot - 2)
                    for i in range(8):
                        for hh in range(2):
                            inst = t.matmul(
                                py[:, hh * 512:(hh + 1) * 512],
                                WOF[:, i * 2048 + ot * 128:
                                    i * 2048 + (ot + 1) * 128],
                                OGC[:, i * TOK + hh * 512:
                                    i * TOK + (hh + 1) * 512],
                                start=(i == 0), stop=(i == 7))
                    inst.then_inc(c.sP.h, 1)

            @blk.vector
            def _(v, grp=grp, pe4=pe4, v4=v4):
                for ot in range(NCH):
                    v.wait_ge(c.sP.h, pe4 + ot + 1)
                    py = pys[ot % 3]
                    v.tensor_add(res[:, ot * TOK:(ot + 1) * TOK],
                                 res[:, ot * TOK:(ot + 1) * TOK],
                                 py[:]).then_inc(c.sV.h, 1)
            c.barrier()


def emit_epilogue(c):
    nc, blk = c.nc, c.blk
    res, io = c.bufs["RES"], c.ios
    ob = c.sO.reserve(16)

    @blk.sync
    def _(s):
        s.dma_start(out=io["outT"].rearrange("(c p) t -> p c t", p=128),
                    in_=res[:].rearrange("p (c t) -> p c t", t=TOK)
                    ).then_inc(c.sO.h, 16)
        s.wait_ge(c.sO.h, ob + 16)



PHASES_ALL = ["l0ffn_a", "l0attn", "l0ffn_b", "l1ffn_a", "l1attn", "l1ffn_b"]
PHASES_DONE = ["l0ffn_a", "l0attn", "l0ffn_b", "l1ffn_a", "l1attn",
               "l1ffn_b"]


def build_program(phases=None):
    if phases is None:
        phases = PHASES_DONE
    _uid[0] = 0
    nc = bass.Bass("TRN2", target_bir_lowering=False, debug=False,
                   detect_race_conditions=RACE_DETECT[0])
    io = {}
    io["x0T"] = nc.dram_tensor("x0T", [D, TOK], F32, kind="ExternalInput").ap()
    for nm, (shp, dt_) in CONST_SPECS.items():
        io[nm] = nc.dram_tensor(nm, shp, dt_, kind="ExternalInput").ap()
    for nm, (shp, dt_) in WEIGHT_SPECS.items():
        io[nm] = nc.dram_tensor(nm, shp, dt_, kind="ExternalInput").ap()
    io["outT"] = nc.dram_tensor("outT", [D, TOK], F32,
                                kind="ExternalOutput").ap()
    io["qscr"] = nc.dram_tensor("qscr", [16, 128, TOK], BF16,
                                kind="Internal").ap()
    io["kv_in"] = nc.dram_tensor("kv_in", [1024, 1024], BF16,
                                 kind="Internal").ap()
    io["kv_out"] = nc.dram_tensor("kv_out", [4096, 1024], BF16,
                                  kind="Internal").ap()
    io["cc1_in"] = nc.dram_tensor("cc1_in", [6208, 2048], BF16,
                                  kind="Internal").ap()
    io["cc1_out"] = nc.dram_tensor("cc1_out", [8 * 6208, 2048], BF16,
                                   kind="Internal").ap()
    io["ccbg_in"] = nc.dram_tensor("ccbg_in", [64, 1024], F32,
                                   kind="Internal").ap()
    io["ccbg_out"] = nc.dram_tensor("ccbg_out", [512, 1024], F32,
                                    kind="Internal").ap()
    io["cc2_in"] = nc.dram_tensor("cc2_in", [2048, 2048], BF16,
                                  kind="Internal").ap()
    io["cc2_out"] = nc.dram_tensor("cc2_out", [16384, 2048], BF16,
                                   kind="Internal").ap()
    io["ba_scr"] = nc.dram_tensor("ba_scr", [64, 1024], F32,
                                  kind="Internal").ap()

    with ExitStack() as stack:
        res = stack.enter_context(nc.sbuf_tensor("RES", [128, NCH * TOK], F32))
        kc = {}
        for nm in KC_NAMES:
            shp, dt_ = (CONST_SPECS.get(nm) or WEIGHT_SPECS[nm])
            kc[nm] = stack.enter_context(nc.sbuf_tensor("kc_" + nm, shp, dt_))
        gwt = stack.enter_context(
            nc.sbuf_tensor("gate_wT", [128, NCH * 8], BF16))

        blk = stack.enter_context(nc.Block())
        c = Ctx(nc, blk, stack)
        c.ios = io
        c.bufs = {"RES": res, "KC": kc, "gate_wT": gwt}

        emit_prologue(c)

        for ph in phases:
            if ph in ("l0ffn_a", "l0ffn_b"):
                with (
                    nc.sbuf_tensor(uname("ffn_ln"), [128, NCH * 512], BF16) as ln,
                    nc.sbuf_tensor(uname("ffn_w0"), [128, 22528], BF16) as w0,
                    nc.sbuf_tensor(uname("ffn_w1"), [128, 22528], BF16) as w1,
                    nc.sbuf_tensor(uname("ffn_hs"), [128, 11 * 512], BF16) as hs,
                ):
                    emit_l0ffn(c, ln, (w0, w1), hs)
            elif ph == "l0attn":
                emit_l0attn(c)
            elif ph == "l1attn":
                emit_l1attn(c)
            elif ph in ("l1ffn_a", "l1ffn_b"):
                with (
                    nc.sbuf_tensor(uname("f1_ln"), [128, NCH * TOK], BF16) as ln,
                    nc.sbuf_tensor(uname("f1_wg0"), [128, 16 * 512], BF16) as wg0,
                    nc.sbuf_tensor(uname("f1_wg1"), [128, 16 * 512], BF16) as wg1,
                    nc.sbuf_tensor(uname("f1_wu0"), [128, 16 * 512], BF16) as wu0,
                    nc.sbuf_tensor(uname("f1_wu1"), [128, 16 * 512], BF16) as wu1,
                    nc.sbuf_tensor(uname("f1_wd"), [128, 4 * 2048], BF16) as wd,
                    nc.sbuf_tensor(uname("f1_G"), [128, 2 * TOK], BF16) as G,
                    nc.sbuf_tensor(uname("f1_F"), [128, 4 * TOK], BF16) as FU,
                    nc.sbuf_tensor(uname("f1_sdh"), [128, 16], F32) as sdh,
                ):
                    emit_l1ffn(c, ln, (wg0, wg1), (wu0, wu1), wd, G, FU, sdh)
            else:
                raise NotImplementedError(ph)

        emit_epilogue(c)
    return nc



def prep_weight_inputs(inputs):
    w = {}
    w["L0_attn_Wq"] = np.ascontiguousarray(
        inputs["L0_attn_Wq"].reshape(2048, 2048))
    w["L0_attn_Wk"] = np.ascontiguousarray(
        inputs["L0_attn_Wk"].reshape(2048, 512))
    w["L0_attn_Wv"] = np.ascontiguousarray(
        inputs["L0_attn_Wv"].reshape(2048, 512))
    w["L0_attn_Wo"] = np.ascontiguousarray(inputs["L0_attn_Wo"])
    for nm in ("gate_w", "w1", "w3", "w2", "sw1", "sw3", "sw2"):
        w[f"L0_ffn_{nm}"] = np.ascontiguousarray(inputs[f"L0_ffn_{nm}"])
    w["L1_attn_W_qkvz"] = np.ascontiguousarray(inputs["L1_attn_W_qkvz"])
    w["L1_attn_W_ba"] = np.ascontiguousarray(inputs["L1_attn_W_ba"])
    w["L1_attn_conv_w"] = np.ascontiguousarray(inputs["L1_attn_conv_w"])
    w["L1_attn_dt_bias"] = np.ascontiguousarray(
        inputs["L1_attn_dt_bias"].reshape(32, 1))
    w["L1_attn_A_log"] = np.ascontiguousarray(
        inputs["L1_attn_A_log"].reshape(32, 1))
    w["L1_attn_norm_w"] = np.ascontiguousarray(
        inputs["L1_attn_norm_w"].reshape(1, 128))
    w["L1_attn_W_out"] = np.ascontiguousarray(inputs["L1_attn_W_out"])
    for nm in ("w_gate", "w_up", "w_down"):
        w[f"L1_ffn_{nm}"] = np.ascontiguousarray(inputs[f"L1_ffn_{nm}"])
    w["L1_ffn_s_gate"] = np.ascontiguousarray(
        inputs["L1_ffn_s_gate"].reshape(64, 128).T)
    w["L1_ffn_s_up"] = np.ascontiguousarray(
        inputs["L1_ffn_s_up"].reshape(64, 128).T)
    w["L1_ffn_s_down"] = np.ascontiguousarray(
        inputs["L1_ffn_s_down"].reshape(16, 128).T)
    return w


def make_in_maps(inputs, phases=None):
    x = inputs["L0_attn_x_q"]
    w = prep_weight_inputs(inputs)
    in_maps = []
    for core in range(NCORE):
        b, p = core // 4, core % 4
        xs = np.ascontiguousarray(x[b, p * TOK:(p + 1) * TOK, :].T)
        m = {"x0T": xs}
        m.update(make_const_inputs(core))
        m.update(w)
        in_maps.append(m)
    return in_maps


def kernel(**inputs):
    nc = build_program()
    in_maps = make_in_maps(inputs)
    res = run_bass_kernel_spmd(nc, in_maps, core_ids=list(range(NCORE)))
    out = np.empty((2, 4096, 2048), np.float32)
    for core in range(NCORE):
        b, p = core // 4, core % 4
        out[b, p * TOK:(p + 1) * TOK, :] = res.results[core]["outT"].T
    return out
